# version 01
# baseline (speedup 1.0000x reference)
import numpy as np
import concourse.bass as bass
import concourse.mybir as mybir
from concourse.bass_utils import run_bass_kernel_spmd

F32 = mybir.dt.float32
BF16 = mybir.dt.bfloat16
I32 = mybir.dt.int32
AF = mybir.ActivationFunctionType
ALU = mybir.AluOpType
AX = mybir.AxisListType

NCORE = 8
B, L, D = 2, 16384, 1024
T = B * L
TI = T // NCORE
DI, NST, DTR, KC = 2048, 16, 64, 4
DC = DI // NCORE
FH = 3072
EPS = 1e-5
MOE_H, MOE_E, MOE_S, MOE_D = 8, 16, 256, 128


class Prog:

    ENGINES = ("sync", "vector", "scalar", "tensor", "gpsimd")

    def __init__(self, nc, sems):
        self.nc = nc
        self.sems = sems
        self.cnt = {k: 0 for k in sems}
        self.ops = {e: [] for e in self.ENGINES}
        self.seen = {e: {} for e in self.ENGINES}

    def _merge(self, deps):
        fr = {}
        for d in deps:
            if d is None:
                continue
            for s, v in d.items():
                if fr.get(s, -1) < v:
                    fr[s] = v
        return fr

    NOCACHE = False

    def _waits(self, engine, frontier):
        cache = self.seen[engine]
        out = {}
        for s, v in frontier.items():
            if cache.get(s, -1) < v:
                out[s] = v
                cache[s] = v
            elif self.NOCACHE:
                out[s] = v
        return out

    def op(self, engine, fn, deps=(), sem=None, step=1):
        if sem is None:
            sem = "c_" + engine
        fr = self._merge(deps)
        waits = self._waits(engine, fr)
        self.cnt[sem] += step
        val = self.cnt[sem]
        sems = self.sems

        def run(eng, waits=waits, fn=fn, sem=sem, step=step):
            for s, v in waits.items():
                eng.wait_ge(sems[s], v)
            inst = fn(eng)
            inst.then_inc(sems[sem], step)
        self.ops[engine].append(run)
        tok = dict(fr)
        tok[sem] = max(tok.get(sem, 0), val)
        return tok

    def raw(self, engine, fn, deps=()):
        waits = self._waits(engine, self._merge(deps))
        sems = self.sems

        def run(eng, waits=waits, fn=fn):
            for s, v in waits.items():
                eng.wait_ge(sems[s], v)
            fn(eng)
        self.ops[engine].append(run)
        return None

    def dma(self, queue, out, in_, deps=(), **kw):
        base = "dS" if queue == "sync" else "dG"
        if not hasattr(self, "_dmarr"):
            self._dmarr = {}
        idx = self._dmarr.get(queue, 0)
        self._dmarr[queue] = idx + 1
        sem = f"{base}{idx % 4}"
        prev_done = 16 * (idx // 4)
        fr = self._merge(deps)
        if prev_done > 0:
            fr[sem] = max(fr.get(sem, 0), prev_done)
        waits = self._waits(queue, fr)
        self.cnt[sem] += 16
        val = self.cnt[sem]
        sems = self.sems

        def run(eng, waits=waits, sem=sem, out=out, in_=in_, kw=kw):
            for s, v in waits.items():
                eng.wait_ge(sems[s], v)
            eng.dma_start(out=out, in_=in_, **kw).then_inc(sems[sem], 16)
        self.ops[queue].append(run)
        tok = dict(fr)
        tok[sem] = max(tok.get(sem, 0), val)
        return tok

    def wait_only(self, engine, deps):
        waits = self._waits(engine, self._merge(deps))
        sems = self.sems

        def run(eng, waits=waits):
            for s, v in waits.items():
                eng.wait_ge(sems[s], v)
        self.ops[engine].append(run)

    def finalize(self, block):
        for name in self.ENGINES:
            ops = self.ops[name]
            if not ops:
                continue

            def section(eng, ops=ops):
                for f in ops:
                    f(eng)
            getattr(block, name)(section)


def std_sems(nc, stack):
    sems = {}
    names = ["c_vector", "c_scalar", "c_tensor", "c_gpsimd", "c_sync", "cc"]
    names += [f"dS{i}" for i in range(4)] + [f"dG{i}" for i in range(4)]
    for name in names:
        sems[name] = stack.enter_context(nc.semaphore(name))
    return sems


def emit_matmul_accum(p, psum_ap, lhsT_tiles, rhs_tiles, deps):
    n = len(lhsT_tiles)
    tok = None
    for k in range(n):
        lt, rt = lhsT_tiles[k], rhs_tiles[k]
        tok = p.op("tensor",
                   lambda e, lt=lt, rt=rt, k=k: e.matmul(
                       out=psum_ap, lhsT=lt, rhs=rt,
                       start=(k == 0), stop=(k == n - 1)),
                   deps=deps if k == 0 else ())
    return tok


def build_l1():
    nc = bass.Bass("TRN2", target_bir_lowering=False, debug=False)
    x_in = nc.dram_tensor("x_slab", [TI, D], F32, kind="ExternalInput").ap()
    wgT_in = nc.dram_tensor("wgT", [D, FH], I32, kind="ExternalInput").ap()
    wuT_in = nc.dram_tensor("wuT", [D, FH], I32, kind="ExternalInput").ap()
    wdT_in = nc.dram_tensor("wdT", [FH, D], I32, kind="ExternalInput").ap()
    ident_in = nc.dram_tensor("ident", [128, 128], F32, kind="ExternalInput").ap()
    xnT_out = nc.dram_tensor("xnT", [D, TI], BF16, kind="ExternalOutput").ap()
    gT_out = nc.dram_tensor("gT", [D, TI], F32, kind="ExternalOutput").ap()

    import contextlib
    with contextlib.ExitStack() as st:
        sb = lambda name, shape, dt: st.enter_context(nc.sbuf_tensor(name, shape, dt))
        ps = lambda name, shape, dt: st.enter_context(nc.psum_tensor(name, shape, dt))
        sems = std_sems(nc, st)
        block = st.enter_context(nc.Block())
        p = Prog(nc, sems)

        consts = sb("consts", [128, 5], F32)
        x_t = sb("x_t", [128, D], F32)
        stat = sb("stat", [128, 8], F32)
        xsq = sb("xsq", [128, D], F32)
        xn_tok = sb("xn_tok", [128, D], BF16)
        ident = sb("ident_sb", [128, 128], F32)
        ident_bf = sb("identbf", [128, 128], BF16)
        xnT_buf = sb("xnT_buf", [128, 8 * 128], BF16)
        ps_tr = ps("ps_tr", [128, 128], BF16)
        wg_bf = sb("wg_bf", [128, 8 * FH], BF16)
        wu_bf = sb("wu_bf", [128, 8 * FH], BF16)
        wd_bf = sb("wd_bf", [128, 24 * D], BF16)
        wstage = sb("wstage", [128, FH], I32)
        xnT_chunk = sb("xnT_chunk", [128, 8 * 512], BF16)
        h_bf = sb("h_bf", [128, 24 * 512], BF16)
        sg_t = sb("sg_t", [128, 512], F32)
        ps_g = ps("ps_g", [128, 512], F32)
        ps_u = ps("ps_u", [128, 512], F32)
        ps_d = ps("ps_d", [128, 512], F32)
        g_t = sb("g_t", [128, 512], F32)

        p.raw("vector", lambda e: e.memset(consts[:, 0:1], EPS))
        p.raw("vector", lambda e: e.memset(consts[:, 1:2], 1.0 / 127.0))
        p.raw("vector", lambda e: e.memset(consts[:, 2:3], 1.0 / (127.0 * 127.0)))
        tok_consts = p.op("vector", lambda e: e.memset(consts[:, 3:4], 1.0 / D))
        tok_id = p.dma("sync", ident[:], ident_in)
        tok_idbf = p.op("vector", lambda e: e.tensor_copy(out=ident_bf[:], in_=ident[:]),
                        deps=[tok_id])

        wtoks = []
        for (src, dst, nch) in ((wgT_in, wg_bf, 8), (wuT_in, wu_bf, 8), (wdT_in, wd_bf, 24)):
            width = src.shape[1]
            for kc in range(nch):
                t0 = p.dma("gpsimd", wstage[:, 0:width], src[kc * 128:(kc + 1) * 128, :],
                           deps=wtoks[-1:])
                t2 = p.op("vector", lambda e, d=dst, kc=kc, w=width: e.tensor_copy(
                    out=d[:, kc * w:(kc + 1) * w], in_=wstage[:, 0:w]), deps=[t0])
                wtoks.append(t2)
        wdone = wtoks[-1]

        prev_cons = []
        prev_ev = []
        prev_st = []
        for tt in range(TI // 128):
            tok_x = p.dma("sync", x_t[:], x_in[tt * 128:(tt + 1) * 128, :],
                          deps=prev_cons)
            t_sum = p.op("vector", lambda e: e.tensor_reduce(
                out=stat[:, 0:1], in_=x_t[:], axis=AX.X, op=ALU.add), deps=[tok_x])
            t_sq = p.op("scalar", lambda e: e.activation(
                out=xsq[:], in_=x_t[:], func=AF.Square, accum_out=stat[:, 1:2]),
                deps=[tok_x])
            t_mu = p.op("vector", lambda e: e.tensor_scalar(
                out=stat[:, 2:3], in0=stat[:, 0:1], scalar1=consts[:, 3:4], scalar2=None,
                op0=ALU.mult), deps=[t_sum, tok_consts])
            t_msq = p.op("vector", lambda e: e.tensor_mul(
                out=stat[:, 3:4], in0=stat[:, 2:3], in1=stat[:, 2:3]), deps=[t_mu])
            t_var = p.op("vector", lambda e: e.tensor_scalar(
                out=stat[:, 4:5], in0=stat[:, 1:2], scalar1=consts[:, 3:4],
                scalar2=stat[:, 3:4], op0=ALU.mult, op1=ALU.subtract),
                deps=[t_sq, t_msq])
            t_sd = p.op("scalar", lambda e: e.activation(
                out=stat[:, 5:6], in_=stat[:, 4:5], func=AF.Sqrt, bias=consts[:, 0:1]),
                deps=[t_var, tok_consts])
            t_rstd = p.op("vector", lambda e: e.reciprocal(
                out=stat[:, 6:7], in_=stat[:, 5:6]), deps=[t_sd])
            t_xn = p.op("vector", lambda e: e.tensor_scalar(
                out=xn_tok[:], in0=x_t[:], scalar1=stat[:, 2:3], scalar2=stat[:, 6:7],
                op0=ALU.subtract, op1=ALU.mult), deps=[t_rstd, t_mu] + prev_ev)
            ev_toks = []
            for c in range(8):
                t_tr = p.op("tensor", lambda e, c=c: e.transpose(
                    out=ps_tr[:], in_=xn_tok[:, c * 128:(c + 1) * 128],
                    identity=ident_bf[:]), deps=[t_xn, tok_idbf] + ev_toks[-1:])
                t_ev = p.op("scalar", lambda e, c=c: e.copy(
                    out=xnT_buf[:, c * 128:(c + 1) * 128], in_=ps_tr[:]),
                    deps=[t_tr] + prev_st)
                ev_toks.append(t_ev)
            t_st = p.dma("sync", xnT_out.rearrange("(c p) t -> c p t", p=128)[
                :, :, tt * 128:(tt + 1) * 128].transpose([1, 0, 2]),
                xnT_buf[:].rearrange("p (c t) -> p c t", c=8), deps=ev_toks)
            prev_cons = [t_xn, t_st, t_sq]
            prev_ev = [ev_toks[-1]]
            prev_st = [t_st]

        ln_done = prev_cons
        for ts in range(TI // 512):
            tok_xc = p.dma("sync", xnT_chunk[:].rearrange("p (c t) -> p c t", c=8),
                           xnT_out.rearrange("(c p) t -> c p t", p=128)[
                               :, :, ts * 512:(ts + 1) * 512].transpose([1, 0, 2]),
                           deps=ln_done)
            h_toks = []
            for ht in range(24):
                lg = [wg_bf[:, kc * FH + ht * 128: kc * FH + (ht + 1) * 128]
                      for kc in range(8)]
                lu = [wu_bf[:, kc * FH + ht * 128: kc * FH + (ht + 1) * 128]
                      for kc in range(8)]
                rx = [xnT_chunk[:, kc * 512:(kc + 1) * 512] for kc in range(8)]
                tg = emit_matmul_accum(p, ps_g[:], lg, rx, deps=[tok_xc, wdone] + h_toks[-1:])
                t_sg = p.op("scalar", lambda e: e.activation(
                    out=sg_t[:], in_=ps_g[:], func=AF.Silu, scale=consts[:, 1:2]),
                    deps=[tg])
                tu = emit_matmul_accum(p, ps_u[:], lu, rx, deps=[t_sg])
                t_h = p.op("vector", lambda e, ht=ht: e.tensor_tensor(
                    out=h_bf[:, ht * 512:(ht + 1) * 512], in0=sg_t[:], in1=ps_u[:],
                    op=ALU.mult), deps=[tu, t_sg])
                h_toks.append(t_h)
            st_toks = []
            for ot in range(8):
                ld = [wd_bf[:, hc * D + ot * 128: hc * D + (ot + 1) * 128]
                      for hc in range(24)]
                rh = [h_bf[:, hc * 512:(hc + 1) * 512] for hc in range(24)]
                td = emit_matmul_accum(p, ps_d[:], ld, rh, deps=h_toks + st_toks[-1:])
                t_g = p.op("scalar", lambda e: e.activation(
                    out=g_t[:], in_=ps_d[:], func=AF.Silu, scale=consts[:, 2:3]),
                    deps=[td])
                t_st = p.dma("sync", gT_out[ot * 128:(ot + 1) * 128,
                                            ts * 512:(ts + 1) * 512], g_t[:],
                             deps=[t_g])
                st_toks.append(t_st)
            ln_done = st_toks + h_toks

        p.wait_only("gpsimd", ln_done)
        p.finalize(block)
    return nc


def build_l2():
    nc = bass.Bass("TRN2", target_bir_lowering=False, debug=False)
    xnT_in = nc.dram_tensor("xnT_full", [D, T], BF16, kind="ExternalInput").ap()
    winT_in = nc.dram_tensor("winT", [D, 2 * DC], F32, kind="ExternalInput").ap()
    convw_in = nc.dram_tensor("convw", [DC, KC], F32, kind="ExternalInput").ap()
    convb_in = nc.dram_tensor("convb", [DC, 1], F32, kind="ExternalInput").ap()
    xpT_in = nc.dram_tensor("xpT", [DC, 96], F32, kind="ExternalInput").ap()
    pd_out = nc.dram_tensor("pd", [DTR, T], F32, kind="ExternalOutput").ap()
    pB_out = nc.dram_tensor("pB", [NST, T], F32, kind="ExternalOutput").ap()
    pC_out = nc.dram_tensor("pC", [NST, T], F32, kind="ExternalOutput").ap()
    u_out = nc.dram_tensor("u_sl", [DC, T], F32, kind="ExternalOutput").ap()
    resS_out = nc.dram_tensor("resS", [DC, T], F32, kind="ExternalOutput").ap()

    TC = 2048
    NSUB = TC // 512
    import contextlib
    with contextlib.ExitStack() as st:
        sb = lambda name, shape, dt: st.enter_context(nc.sbuf_tensor(name, shape, dt))
        ps = lambda name, shape, dt: st.enter_context(nc.psum_tensor(name, shape, dt))
        sems = std_sems(nc, st)
        block = st.enter_context(nc.Block())
        p = Prog(nc, sems)

        xnc = sb("xnc", [128, 8 * TC], BF16)
        win_bf = sb("win_bf", [128, 8 * 512], BF16)
        xpT_bf = sb("xpT_bf", [128, 2 * 96], BF16)
        convw_sb = sb("convw_sb", [128, 2 * KC], F32)
        convb_sb = sb("convb_sb", [128, 2], F32)
        xpart = [sb(f"xpart{i}", [128, TC + 3], F32) for i in range(2)]
        u_f = [sb(f"u_f{i}", [128, TC], F32) for i in range(2)]
        u_bf = [sb(f"u_bf{i}", [128, TC], BF16) for i in range(2)]
        ev_t = sb("ev_t", [128, 512], F32)
        p_sb = sb("p_sb", [96, 512], F32)
        ps_a = ps("ps_a", [128, 512], F32)
        ps_b = ps("ps_b", [128, 512], F32)
        ps_p = ps("ps_p", [96, 512], F32)

        t_w = p.dma("gpsimd", win_bf[:].rearrange("p (k o) -> p k o", k=8),
                    winT_in.rearrange("(k p) o -> p k o", p=128))
        t_xp = p.dma("gpsimd", xpT_bf[:].rearrange("p (k o) -> p k o", k=2),
                     xpT_in.rearrange("(k p) o -> p k o", p=128))
        t_cw = p.dma("sync", convw_sb[:].rearrange("p (k o) -> p k o", k=2),
                     convw_in.rearrange("(k p) o -> p k o", p=128))
        t_cb = p.dma("sync", convb_sb[:].rearrange("p (k o) -> p k o", k=2),
                     convb_in.rearrange("(k p) o -> p k o", p=128),
                     allow_slow_non_contiguous=True)
        wdeps = [t_w, t_xp, t_cw, t_cb]

        prev = {"xnc": [], "xpart": [], "u": [], "psA": [], "psB": [], "psP": [],
                "ev": [], "psb": []}
        for tc in range(T // TC):
            batch_start = (tc % (L // TC) == 0)
            t_x = p.dma("sync", xnc[:].rearrange("p (k t) -> p k t", k=8),
                        xnT_in.rearrange("(k p) t -> p k t", p=128)[
                            :, :, tc * TC:(tc + 1) * TC],
                        deps=prev["xnc"])
            mm_toks = []
            conv_deps = []
            res_sts = []
            for ot in range(4):
                for sub in range(NSUB):
                    lh = [win_bf[:, kc * 512 + ot * 128:(kc) * 512 + (ot + 1) * 128]
                          for kc in range(8)]
                    rh = [xnc[:, kc * TC + sub * 512: kc * TC + (sub + 1) * 512]
                          for kc in range(8)]
                    pbank = ps_a if (ot * NSUB + sub) % 2 == 0 else ps_b
                    pdep = prev["psA"] if pbank is ps_a else prev["psB"]
                    tm = emit_matmul_accum(p, pbank[:], lh, rh,
                                           deps=[t_x, t_w] + pdep)
                    mm_toks.append(tm)
                    if ot < 2:
                        tev = p.op("scalar", lambda e, ot=ot, sub=sub, pb=pbank: e.copy(
                            out=xpart[ot][:, 3 + sub * 512: 3 + (sub + 1) * 512],
                            in_=pb[:]), deps=[tm] + prev["xpart"])
                        conv_deps.append(tev)
                    else:
                        tev = p.op("scalar", lambda e, pb=pbank: e.activation(
                            out=ev_t[:], in_=pb[:], func=AF.Silu),
                            deps=[tm] + prev["ev"])
                        tst = p.dma("sync", resS_out[(ot - 2) * 128:(ot - 1) * 128,
                                                     tc * TC + sub * 512:
                                                     tc * TC + (sub + 1) * 512],
                                    ev_t[:], deps=[tev])
                        res_sts.append(tst)
                        prev["ev"] = [tst]
                    if pbank is ps_a:
                        prev["psA"] = [tev]
                    else:
                        prev["psB"] = [tev]
            if batch_start:
                hdeps = prev["u"] + prev.get("halo", [])
                th = p.op("vector", lambda e: e.memset(xpart[0][:, 0:3], 0.0),
                          deps=hdeps)
                th2 = p.op("vector", lambda e: e.memset(xpart[1][:, 0:3], 0.0),
                          deps=hdeps)
                halo_toks = [th, th2]
            else:
                halo_toks = prev["halo"]
            u_toks = []
            ub_toks = []
            for ct in range(2):
                t_acc = p.op("vector", lambda e, ct=ct: e.tensor_scalar(
                    out=u_f[ct][:], in0=xpart[ct][:, 0:TC],
                    scalar1=convw_sb[:, ct * KC:ct * KC + 1], scalar2=None,
                    op0=ALU.mult),
                    deps=conv_deps + halo_toks + [t_cw] + prev["u"])
                for k in range(1, KC):
                    t_acc = p.op("vector", lambda e, ct=ct, k=k: e.scalar_tensor_tensor(
                        out=u_f[ct][:], in0=xpart[ct][:, k:k + TC],
                        scalar=convw_sb[:, ct * KC + k:ct * KC + k + 1],
                        in1=u_f[ct][:], op0=ALU.mult, op1=ALU.add), deps=[t_acc])
                t_u = p.op("scalar", lambda e, ct=ct: e.activation(
                    out=u_f[ct][:], in_=u_f[ct][:], func=AF.Silu,
                    bias=convb_sb[:, ct:ct + 1]), deps=[t_acc, t_cb])
                t_ub = p.op("vector", lambda e, ct=ct: e.tensor_copy(
                    out=u_bf[ct][:], in_=u_f[ct][:]), deps=[t_u])
                t_us = p.dma("sync", u_out[ct * 128:(ct + 1) * 128,
                                           tc * TC:(tc + 1) * TC], u_f[ct][:],
                             deps=[t_u])
                u_toks += [t_u, t_us]
                ub_toks.append(t_ub)
            nhalo = []
            for ct in range(2):
                nh = p.op("vector", lambda e, ct=ct: e.tensor_copy(
                    out=xpart[ct][:, 0:3], in_=xpart[ct][:, TC:TC + 3]),
                    deps=u_toks + ub_toks)
                nhalo.append(nh)
            prev["halo"] = nhalo
            p_sts = []
            for sub in range(NSUB):
                lh = [xpT_bf[:, k * 96:(k + 1) * 96] for k in range(2)]
                rh = [u_bf[k][:, sub * 512:(sub + 1) * 512] for k in range(2)]
                tm = emit_matmul_accum(p, ps_p[:], lh, rh,
                                       deps=[t_xp] + ub_toks + prev["psP"])
                tpe = p.op("scalar", lambda e: e.copy(out=p_sb[:], in_=ps_p[:]),
                           deps=[tm] + prev["psb"])
                prev["psP"] = [tpe]
                c0 = tc * TC + sub * 512
                s1 = p.dma("sync", pd_out[:, c0:c0 + 512], p_sb[0:64, :], deps=[tpe])
                s2 = p.dma("sync", pB_out[:, c0:c0 + 512], p_sb[64:80, :], deps=[tpe])
                s3 = p.dma("sync", pC_out[:, c0:c0 + 512], p_sb[80:96, :], deps=[tpe])
                prev["psb"] = [s1, s2, s3]
                p_sts += [s1, s2, s3]
            prev["xnc"] = mm_toks[-1:]
            prev["xpart"] = u_toks + nhalo
            prev["u"] = u_toks + p_sts + ub_toks
        p.wait_only("gpsimd", prev["u"] + res_sts + prev["psb"] + prev["ev"])
        p.finalize(block)
    return nc


def host_prep_l2(inputs, xnT_full):
    ipw = inputs["L0_attn_in_proj_w"]
    xpw = inputs["L0_attn_x_proj_w"]
    maps = []
    for c in range(NCORE):
        sl = slice(c * DC, (c + 1) * DC)
        winT = np.ascontiguousarray(
            np.concatenate([ipw[sl], ipw[DI:][sl]], axis=0).T)
        maps.append({
            "xnT_full": xnT_full,
            "winT": winT,
            "convw": np.ascontiguousarray(inputs["L0_attn_conv_w"][sl, 0, :]),
            "convb": np.ascontiguousarray(inputs["L0_attn_conv_b"][sl, None]),
            "xpT": np.ascontiguousarray(xpw[:, sl].T),
        })
    return maps


def run_l2(inputs, xnT_full, **kw):
    nc = _get("l2", build_l2)
    maps = host_prep_l2(inputs, xnT_full)
    return run_bass_kernel_spmd(nc, maps, core_ids=list(range(NCORE)), **kw)


def build_l3():
    nc = bass.Bass("TRN2", target_bir_lowering=False, debug=False)
    pd_all = nc.dram_tensor("pd_all", [NCORE, DTR, T], F32, kind="ExternalInput").ap()
    pB_all = nc.dram_tensor("pB_all", [NCORE, NST, T], F32, kind="ExternalInput").ap()
    pC_all = nc.dram_tensor("pC_all", [NCORE, NST, T], F32, kind="ExternalInput").ap()
    dtT_in = nc.dram_tensor("dtT", [DTR, DC], F32, kind="ExternalInput").ap()
    dtb_in = nc.dram_tensor("dtb", [DC, 1], F32, kind="ExternalInput").ap()
    Alog_in = nc.dram_tensor("Alog", [DC, NST], F32, kind="ExternalInput").ap()
    Dp_in = nc.dram_tensor("Dp", [DC, 1], F32, kind="ExternalInput").ap()
    u_in = nc.dram_tensor("u_sl", [DC, T], F32, kind="ExternalInput").ap()
    resS_in = nc.dram_tensor("resS", [DC, T], F32, kind="ExternalInput").ap()
    repl_in = nc.dram_tensor("replmat", [NST, NST * 128], F32, kind="ExternalInput").ap()
    yg_out = nc.dram_tensor("ygT", [DC, T], BF16, kind="ExternalOutput").ap()

    TC = 2048
    NSUB = TC // 512
    import contextlib
    with contextlib.ExitStack() as st:
        sb = lambda name, shape, dt: st.enter_context(nc.sbuf_tensor(name, shape, dt))
        ps = lambda name, shape, dt: st.enter_context(nc.psum_tensor(name, shape, dt))
        sems = std_sems(nc, st)
        block = st.enter_context(nc.Block())
        p = Prog(nc, sems)

        consts = sb("consts", [128, 2], F32)
        dtT_bf = sb("dtT_bf", [DTR, DC], BF16)
        dtb_sb = sb("dtb_sb", [128, 2], F32)
        Dp_sb = sb("Dp_sb", [128, 2], F32)
        A_sb = sb("A_sb", [128, 2 * NST], F32)
        repl_sb = sb("repl_sb", [NST, NST * 128], F32)
        state = sb("state", [128, 2 * NST], F32)
        stage = sb("stage", [DTR, TC], F32)
        pd_sum = sb("pd_sum", [DTR, TC], F32)
        pB_sum = sb("pB_sum", [NST, TC], F32)
        pC_sum = sb("pC_sum", [NST, TC], F32)
        dr_bf = sb("dr_bf", [DTR, TC], BF16)
        delta = [sb(f"delta{i}", [128, TC], F32) for i in range(2)]
        du = [sb(f"du{i}", [128, TC], F32) for i in range(2)]
        dA = [sb(f"dA{i}", [128, TC], F32) for i in range(2)]
        zB = [sb(f"zB{i}", [128, TC], F32) for i in range(2)]
        xbuf = [sb(f"xbuf{i}", [128, TC], F32) for i in range(2)]
        yacc = [sb(f"yacc{i}", [128, TC], F32) for i in range(2)]
        u_ch = [sb(f"u_ch{i}", [128, TC], F32) for i in range(2)]
        res_ch = [sb(f"res_ch{i}", [128, TC], F32) for i in range(2)]
        yg_bf = [sb(f"yg_bf{i}", [128, TC], BF16) for i in range(2)]
        tmp_s = sb("tmp_s", [128, 512], F32)
        sp_a = sb("sp_a", [128, 512], F32)
        sp_l = sb("sp_l", [128, 512], F32)
        ps_dt = ps("ps_dt", [128, 512], F32)
        ps_B = [ps(f"ps_B{i}", [128, 512], F32) for i in range(2)]
        ps_C = [ps(f"ps_C{i}", [128, 512], F32) for i in range(2)]

        tok_c = p.op("vector", lambda e: e.memset(consts[:, 0:1], -1.0))
        t_dtT = p.dma("gpsimd", dtT_bf[:], dtT_in)
        t_dtb = p.dma("sync", dtb_sb[:].rearrange("p (k o) -> p k o", k=2),
                      dtb_in.rearrange("(k p) o -> p k o", p=128),
                      allow_slow_non_contiguous=True)
        t_Dp = p.dma("sync", Dp_sb[:].rearrange("p (k o) -> p k o", k=2),
                     Dp_in.rearrange("(k p) o -> p k o", p=128),
                     allow_slow_non_contiguous=True)
        t_Al = p.dma("sync", A_sb[:].rearrange("p (k o) -> p k o", k=2),
                     Alog_in.rearrange("(k p) o -> p k o", p=128))
        t_repl = p.dma("sync", repl_sb[:], repl_in)
        t_A1 = p.op("scalar", lambda e: e.activation(
            out=A_sb[:], in_=A_sb[:], func=AF.Exp), deps=[t_Al])
        t_A = p.op("vector", lambda e: e.tensor_scalar(
            out=A_sb[:], in0=A_sb[:], scalar1=consts[:, 0:1], scalar2=None,
            op0=ALU.mult), deps=[t_A1, tok_c])
        t_st0 = p.op("vector", lambda e: e.memset(state[:], 0.0))

        prev = {"stage": [], "psum": [], "dr": [], "delta": [], "du": [],
                "dA": [[], []], "zB": [[], []], "xb": [[], []], "ya": [],
                "uch": [], "res": [], "yg": [], "st": [t_st0],
                "psB": [[], []], "psC": [[], []], "psdt": [], "tmp": [],
                "spa": [], "spl": []}
        for tc in range(T // TC):
            c0 = tc * TC
            sums = []
            for (src, dst, rows) in ((pd_all, pd_sum, DTR), (pB_all, pB_sum, NST),
                                     (pC_all, pC_sum, NST)):
                t_l = p.dma("sync", dst[:, :], src[0, :, c0:c0 + TC],
                            deps=prev["psum"])
                for j in range(1, NCORE):
                    t_s = p.dma("sync", stage[0:rows, :], src[j, :, c0:c0 + TC],
                                deps=prev["stage"] + [t_l])
                    t_l = p.op("vector", lambda e, dst=dst, rows=rows: e.tensor_add(
                        out=dst[0:rows, :], in0=dst[0:rows, :], in1=stage[0:rows, :]),
                        deps=[t_s, t_l])
                    prev["stage"] = [t_l]
                sums.append(t_l)
            t_pd, t_pB, t_pC = sums
            t_dr = p.op("vector", lambda e: e.tensor_copy(
                out=dr_bf[:], in_=pd_sum[:]), deps=[t_pd] + prev["dr"])
            t_u, t_res = [], []
            for ct in range(2):
                t_u.append(p.dma("sync", u_ch[ct][:],
                                 u_in[ct * 128:(ct + 1) * 128, c0:c0 + TC],
                                 deps=prev["uch"]))
                t_res.append(p.dma("sync", res_ch[ct][:],
                                   resS_in[ct * 128:(ct + 1) * 128, c0:c0 + TC],
                                   deps=prev["res"]))
            d_toks = [[], []]
            for ct in range(2):
                for sub in range(NSUB):
                    t_mm = p.op("tensor", lambda e, ct=ct, sub=sub: e.matmul(
                        out=ps_dt[:], lhsT=dtT_bf[:, ct * 128:(ct + 1) * 128],
                        rhs=dr_bf[:, sub * 512:(sub + 1) * 512],
                        start=True, stop=True),
                        deps=[t_dr, t_dtT] + prev["psdt"])
                    t_a = p.op("scalar", lambda e, ct=ct: e.activation(
                        out=sp_a[:], in_=ps_dt[:], func=AF.Abs,
                        bias=dtb_sb[:, ct:ct + 1]), deps=[t_mm, t_dtb] + prev["spa"])
                    t_e = p.op("scalar", lambda e: e.activation(
                        out=sp_a[:], in_=sp_a[:], func=AF.Exp,
                        scale=consts[:, 0:1]), deps=[t_a, tok_c])
                    t_ln = p.op("scalar", lambda e: e.activation(
                        out=sp_l[:], in_=sp_a[:], func=AF.Ln, bias=1.0),
                        deps=[t_e] + prev["spl"])
                    t_r = p.op("scalar", lambda e, ct=ct: e.activation(
                        out=sp_a[:], in_=ps_dt[:], func=AF.Relu,
                        bias=dtb_sb[:, ct:ct + 1]), deps=[t_ln])
                    t_d = p.op("vector", lambda e, ct=ct, sub=sub: e.tensor_add(
                        out=delta[ct][:, sub * 512:(sub + 1) * 512],
                        in0=sp_a[:], in1=sp_l[:]), deps=[t_r] + prev["delta"])
                    prev["psdt"] = [t_d]
                    prev["spa"] = [t_d]
                    prev["spl"] = [t_d]
                    d_toks[ct].append(t_d)
            du_toks = []
            for ct in range(2):
                du_toks.append(p.op("vector", lambda e, ct=ct: e.tensor_mul(
                    out=du[ct][:], in0=delta[ct][:], in1=u_ch[ct][:]),
                    deps=d_toks[ct] + [t_u[ct]] + prev["du"]))
            st_dep = prev["st"]
            if c0 == L:
                t_rs = p.op("vector", lambda e: e.memset(state[:], 0.0),
                            deps=prev["st"])
                st_dep = [t_rs]
            y_toks = [[], []]
            first_n = [True, True]
            for n in range(NST):
                for ct in range(2):
                    t_da = p.op("scalar", lambda e, ct=ct, n=n: e.activation(
                        out=dA[ct][:], in_=delta[ct][:], func=AF.Exp,
                        scale=A_sb[:, ct * NST + n:ct * NST + n + 1]),
                        deps=d_toks[ct] + [t_A] + prev["dA"][ct])
                    zb_subs = []
                    for sub in range(NSUB):
                        pb = ps_B[sub % 2]
                        t_mb = p.op("tensor", lambda e, n=n, sub=sub, pb=pb: e.matmul(
                            out=pb[:], lhsT=repl_sb[:, n * 128:(n + 1) * 128],
                            rhs=pB_sum[:, sub * 512:(sub + 1) * 512],
                            start=True, stop=True),
                            deps=[t_pB, t_repl] + prev["psB"][sub % 2])
                        t_zb = p.op("vector", lambda e, ct=ct, sub=sub, pb=pb: e.tensor_tensor(
                            out=zB[ct][:, sub * 512:(sub + 1) * 512],
                            in0=du[ct][:, sub * 512:(sub + 1) * 512],
                            in1=pb[:], op=ALU.mult),
                            deps=[t_mb, du_toks[ct]] + prev["zB"][ct])
                        prev["psB"][sub % 2] = [t_zb]
                        zb_subs.append(t_zb)
                    scol = state[:, ct * NST + n:ct * NST + n + 1]
                    t_scan = p.op("vector", lambda e, ct=ct, sc=scol: e.tensor_tensor_scan(
                        out=xbuf[ct][:], data0=dA[ct][:], data1=zB[ct][:],
                        initial=sc, op0=ALU.mult, op1=ALU.add),
                        deps=zb_subs + [t_da] + st_dep + prev["xb"][ct])
                    t_sv = p.op("scalar", lambda e, ct=ct, sc=scol: e.copy(
                        out=sc, in_=xbuf[ct][:, TC - 1:TC]), deps=[t_scan])
                    st_dep = st_dep + [t_sv]
                    prev["st"] = st_dep
                    for sub in range(NSUB):
                        pc = ps_C[sub % 2]
                        t_mc = p.op("tensor", lambda e, n=n, sub=sub, pc=pc: e.matmul(
                            out=pc[:], lhsT=repl_sb[:, n * 128:(n + 1) * 128],
                            rhs=pC_sum[:, sub * 512:(sub + 1) * 512],
                            start=True, stop=True),
                            deps=[t_pC, t_repl] + prev["psC"][sub % 2])
                        if first_n[ct]:
                            t_y = p.op("vector", lambda e, ct=ct, sub=sub, pc=pc: e.tensor_tensor(
                                out=yacc[ct][:, sub * 512:(sub + 1) * 512],
                                in0=xbuf[ct][:, sub * 512:(sub + 1) * 512],
                                in1=pc[:], op=ALU.mult),
                                deps=[t_mc, t_scan] + prev["ya"])
                        else:
                            t_m = p.op("vector", lambda e, ct=ct, sub=sub, pc=pc: e.tensor_tensor(
                                out=tmp_s[:],
                                in0=xbuf[ct][:, sub * 512:(sub + 1) * 512],
                                in1=pc[:], op=ALU.mult),
                                deps=[t_mc, t_scan] + prev["tmp"])
                            t_y = p.op("vector", lambda e, ct=ct, sub=sub: e.tensor_add(
                                out=yacc[ct][:, sub * 512:(sub + 1) * 512],
                                in0=yacc[ct][:, sub * 512:(sub + 1) * 512],
                                in1=tmp_s[:]), deps=[t_m])
                            prev["tmp"] = [t_y]
                        prev["psC"][sub % 2] = [t_y]
                        y_toks[ct].append(t_y)
                    first_n[ct] = False
                    prev["xb"][ct] = [y_toks[ct][-1], t_sv]
                    prev["dA"][ct] = [t_scan]
                    prev["zB"][ct] = [t_scan]
            yg_toks = []
            for ct in range(2):
                t_yd = p.op("vector", lambda e, ct=ct: e.scalar_tensor_tensor(
                    out=yacc[ct][:], in0=u_ch[ct][:], scalar=Dp_sb[:, ct:ct + 1],
                    in1=yacc[ct][:], op0=ALU.mult, op1=ALU.add),
                    deps=y_toks[ct] + [t_Dp])
                t_yg = p.op("vector", lambda e, ct=ct: e.tensor_mul(
                    out=yg_bf[ct][:], in0=yacc[ct][:], in1=res_ch[ct][:]),
                    deps=[t_yd, t_res[ct]] + prev["yg"])
                t_sg = p.dma("sync", yg_out[ct * 128:(ct + 1) * 128, c0:c0 + TC],
                             yg_bf[ct][:], deps=[t_yg])
                yg_toks += [t_yg, t_sg]
            prev["psum"] = yg_toks
            prev["dr"] = d_toks[0] + d_toks[1]
            prev["delta"] = [t for ct in range(2) for t in
                             (prev["dA"][ct] + du_toks)]
            prev["du"] = [t for ct in range(2) for t in prev["zB"][ct]] + du_toks
            prev["uch"] = yg_toks + du_toks
            prev["res"] = yg_toks
            prev["ya"] = yg_toks
            prev["yg"] = yg_toks
        p.wait_only("gpsimd", yg_toks + prev["st"])
        p.finalize(block)
    return nc


def host_prep_l3(inputs, pd_all, pB_all, pC_all, u_all, resS_all):
    dtw = inputs["L0_attn_dt_proj_w"]
    replmat = np.zeros((NST, NST * 128), np.float32)
    for n in range(NST):
        replmat[n, n * 128:(n + 1) * 128] = 1.0
    maps = []
    for c in range(NCORE):
        sl = slice(c * DC, (c + 1) * DC)
        maps.append({
            "pd_all": pd_all, "pB_all": pB_all, "pC_all": pC_all,
            "dtT": np.ascontiguousarray(dtw[sl, :].T),
            "dtb": np.ascontiguousarray(inputs["L0_attn_dt_proj_b"][sl, None]),
            "Alog": np.ascontiguousarray(inputs["L0_attn_A_log"][sl, :]),
            "Dp": np.ascontiguousarray(inputs["L0_attn_D_param"][sl, None]),
            "u_sl": u_all[sl], "resS": resS_all[sl],
            "replmat": replmat,
        })
    return maps


def run_l3(inputs, pd_all, pB_all, pC_all, u_all, resS_all, **kw):
    nc = _get("l3", build_l3)
    maps = host_prep_l3(inputs, pd_all, pB_all, pC_all, u_all, resS_all)
    return run_bass_kernel_spmd(nc, maps, core_ids=list(range(NCORE)), **kw)


def build_l4():
    nc = bass.Bass("TRN2", target_bir_lowering=False, debug=False)
    yg_in = nc.dram_tensor("ygT_slab", [DI, TI], BF16, kind="ExternalInput").ap()
    opT_in = nc.dram_tensor("opT", [DI, D], F32, kind="ExternalInput").ap()
    x_in = nc.dram_tensor("x_slab", [TI, D], F32, kind="ExternalInput").ap()
    gT_in = nc.dram_tensor("gT", [D, TI], F32, kind="ExternalInput").ap()
    Wk_in = nc.dram_tensor("Wk", [D, D], F32, kind="ExternalInput").ap()
    bk_in = nc.dram_tensor("bk", [D, 1], F32, kind="ExternalInput").ap()
    Wv_in = nc.dram_tensor("Wv", [D, D], F32, kind="ExternalInput").ap()
    bv_in = nc.dram_tensor("bv", [D, 1], F32, kind="ExternalInput").ap()
    RH_in = nc.dram_tensor("RHblk", [D, NST], F32, kind="ExternalInput").ap()
    ident_in = nc.dram_tensor("ident", [128, 128], F32, kind="ExternalInput").ap()
    x1_out = nc.dram_tensor("x1T", [D, TI], F32, kind="ExternalOutput").ap()
    xn1_out = nc.dram_tensor("xn1T", [D, TI], BF16, kind="ExternalOutput").ap()
    V1_out = nc.dram_tensor("V1T", [D, TI], F32, kind="ExternalOutput").ap()
    valid_out = nc.dram_tensor("valid", [NST, TI], F32, kind="ExternalOutput").ap()

    TC = 512
    import contextlib
    with contextlib.ExitStack() as st:
        sb = lambda name, shape, dt: st.enter_context(nc.sbuf_tensor(name, shape, dt))
        ps = lambda name, shape, dt: st.enter_context(nc.psum_tensor(name, shape, dt))
        sems = std_sems(nc, st)
        block = st.enter_context(nc.Block())
        p = Prog(nc, sems)

        consts = sb("consts", [128, 4], F32)
        ident = sb("ident_sb", [128, 128], F32)
        ones1 = sb("ones1", [1, 128], F32)
        ones_col = sb("ones_col", [128, 1], F32)
        opT_bf = sb("opT_bf", [128, 16 * D], BF16)
        Wk_bf = sb("Wk_bf", [128, 8 * D], BF16)
        Wv_bf = sb("Wv_bf", [128, 8 * D], BF16)
        RH_bf = sb("RH_bf", [128, 8 * NST], BF16)
        bk_sb = sb("bk_sb", [128, 8], F32)
        bv_sb = sb("bv_sb", [128, 8], F32)
        ygc = sb("ygc", [128, 16 * TC], BF16)
        s_sb = sb("s_sb", [128, 8 * TC], F32)
        x_tok = sb("x_tok", [128, D], F32)
        xT_sb = sb("xT_sb", [128, 8 * TC], F32)
        g_sb = sb("g_sb", [128, 8 * TC], F32)
        x1_sb = sb("x1_sb", [128, 8 * TC], F32)
        tmp_t = sb("tmp_t", [128, TC], F32)
        sq_t = [sb(f"sq_t{i}", [128, TC], F32) for i in range(2)]
        strow = sb("strow", [1, 4 * TC], F32)
        xn1_sb = sb("xn1_sb", [128, 8 * TC], BF16)
        K1_sb = sb("K1_sb", [128, 8 * TC], BF16)
        V1_sb = sb("V1_sb", [128, 8 * TC], F32)
        val_sb = sb("val_sb", [NST, TC], F32)
        ps_a = ps("ps_a", [128, TC], F32)
        ps_b = ps("ps_b", [128, TC], F32)
        ps_st1 = ps("ps_st1", [1, TC], F32)
        ps_st2 = ps("ps_st2", [1, TC], F32)
        ps_mu = ps("ps_mu", [128, TC], F32)
        ps_rs = ps("ps_rs", [128, TC], F32)
        ps_dot = ps("ps_dot", [NST, TC], F32)

        p.raw("vector", lambda e: e.memset(consts[:, 0:1], 1.0 / D))
        tok_c = p.op("vector", lambda e: e.memset(consts[:, 1:2], EPS))
        tok_o1 = p.op("vector", lambda e: e.memset(ones1[:], 1.0))
        tok_oc = p.op("vector", lambda e: e.memset(ones_col[:], 1.0))
        t_id = p.dma("sync", ident[:], ident_in)
        t_op = p.dma("gpsimd", opT_bf[:].rearrange("p (k o) -> p k o", k=16),
                     opT_in.rearrange("(k p) o -> p k o", p=128))
        t_wk = p.dma("gpsimd", Wk_bf[:].rearrange("p (k o) -> p k o", k=8),
                     Wk_in.rearrange("(k p) o -> p k o", p=128))
        t_wv = p.dma("gpsimd", Wv_bf[:].rearrange("p (k o) -> p k o", k=8),
                     Wv_in.rearrange("(k p) o -> p k o", p=128))
        t_rh = p.dma("gpsimd", RH_bf[:].rearrange("p (k o) -> p k o", k=8),
                     RH_in.rearrange("(k p) o -> p k o", p=128))
        t_bk = p.dma("sync", bk_sb[:].rearrange("p (k o) -> p k o", k=8),
                     bk_in.rearrange("(k p) o -> p k o", p=128),
                     allow_slow_non_contiguous=True)
        t_bv = p.dma("sync", bv_sb[:].rearrange("p (k o) -> p k o", k=8),
                     bv_in.rearrange("(k p) o -> p k o", p=128),
                     allow_slow_non_contiguous=True)

        prev = {"ygc": [], "sA": [], "sB": [], "s": [], "xtok": [], "xT": [],
                "g": [], "x1": [], "sq": [[], []], "st": [], "mu": [],
                "xn1": [], "K1": [], "V1": [], "val": [], "tmp": [], "dot": []}
        for tc in range(TI // TC):
            c0 = tc * TC
            t_yg = p.dma("sync", ygc[:].rearrange("p (k t) -> p k t", k=16),
                         yg_in.rearrange("(k p) t -> p k t", p=128)[:, :, c0:c0 + TC],
                         deps=prev["ygc"])
            t_g = p.dma("sync", g_sb[:].rearrange("p (k t) -> p k t", k=8),
                        gT_in.rearrange("(k p) t -> p k t", p=128)[:, :, c0:c0 + TC],
                        deps=prev["g"])
            s_toks = []
            for ot in range(8):
                pb = ps_a if ot % 2 == 0 else ps_b
                pdep = prev["sA"] if ot % 2 == 0 else prev["sB"]
                lh = [opT_bf[:, kc * D + ot * 128: kc * D + (ot + 1) * 128]
                      for kc in range(16)]
                rh = [ygc[:, kc * TC:(kc + 1) * TC] for kc in range(16)]
                tm = emit_matmul_accum(p, pb[:], lh, rh, deps=[t_yg, t_op] + pdep)
                tev = p.op("scalar", lambda e, ot=ot, pb=pb: e.copy(
                    out=s_sb[:, ot * TC:(ot + 1) * TC], in_=pb[:]),
                    deps=[tm] + prev["s"])
                if ot % 2 == 0:
                    prev["sA"] = [tev]
                else:
                    prev["sB"] = [tev]
                s_toks.append(tev)
            xT_toks = []
            for tt in range(4):
                t_x = p.dma("sync", x_tok[:],
                            x_in[c0 + tt * 128:c0 + (tt + 1) * 128, :],
                            deps=prev["xtok"])
                for c in range(8):
                    ttr = p.op("tensor", lambda e, c=c: e.transpose(
                        out=ps_mu[:, 0:128], in_=x_tok[:, c * 128:(c + 1) * 128],
                        identity=ident[:]),
                        deps=[t_x, t_id] + prev["mu"] + prev.get("muwar", []))
                    tev = p.op("scalar", lambda e, c=c, tt=tt: e.copy(
                        out=xT_sb[:, c * TC + tt * 128: c * TC + (tt + 1) * 128],
                        in_=ps_mu[:, 0:128]), deps=[ttr] + prev["xT"])
                    prev["mu"] = [tev]
                    xT_toks.append(tev)
                prev["xtok"] = [xT_toks[-1]]
            x1_toks = []
            for c in range(8):
                t1 = p.op("vector", lambda e, c=c: e.tensor_mul(
                    out=tmp_t[:], in0=s_sb[:, c * TC:(c + 1) * TC],
                    in1=g_sb[:, c * TC:(c + 1) * TC]),
                    deps=[s_toks[c], t_g] + prev["tmp"])
                t2 = p.op("vector", lambda e, c=c: e.tensor_add(
                    out=x1_sb[:, c * TC:(c + 1) * TC],
                    in0=xT_sb[:, c * TC:(c + 1) * TC], in1=tmp_t[:]),
                    deps=[t1, xT_toks[tt * 8 + c] if False else xT_toks[-1]] + prev["x1"])
                prev["tmp"] = [t2]
                x1_toks.append(t2)
            t_x1st = p.dma("sync", x1_out[:, c0:c0 + TC].rearrange(
                "(k p) t -> p k t", p=128),
                x1_sb[:].rearrange("p (k t) -> p k t", k=8), deps=x1_toks)
            sq_toks = []
            for c in range(8):
                sq = sq_t[c % 2]
                t_sq = p.op("scalar", lambda e, c=c, sq=sq: e.activation(
                    out=sq[:], in_=x1_sb[:, c * TC:(c + 1) * TC], func=AF.Square),
                    deps=[x1_toks[c]] + prev["sq"][c % 2])
                t_m1 = p.op("tensor", lambda e, c=c: e.matmul(
                    out=ps_st1[:], lhsT=ones_col[:],
                    rhs=x1_sb[:, c * TC:(c + 1) * TC],
                    start=(c == 0), stop=(c == 7)),
                    deps=[x1_toks[c], tok_oc] + (prev["st"] if c == 0 else []))
                t_m2 = p.op("tensor", lambda e, c=c, sq=sq: e.matmul(
                    out=ps_st2[:], lhsT=ones_col[:], rhs=sq[:],
                    start=(c == 0), stop=(c == 7)), deps=[t_sq])
                prev["sq"][c % 2] = [t_m2]
                sq_toks.append(t_m2)
            t_mu = p.op("vector", lambda e: e.tensor_scalar(
                out=strow[:, 2 * TC:3 * TC], in0=ps_st1[:], scalar1=consts[0:1, 0:1],
                scalar2=None, op0=ALU.mult), deps=sq_toks + [tok_c] + prev["st"])
            t_ms = p.op("vector", lambda e: e.tensor_mul(
                out=strow[:, 0:TC], in0=strow[:, 2 * TC:3 * TC],
                in1=strow[:, 2 * TC:3 * TC]), deps=[t_mu])
            t_var = p.op("vector", lambda e: e.tensor_scalar(
                out=strow[:, TC:2 * TC], in0=ps_st2[:], scalar1=consts[0:1, 0:1],
                scalar2=None, op0=ALU.mult), deps=sq_toks + [t_ms])
            t_var2 = p.op("vector", lambda e: e.tensor_sub(
                out=strow[:, TC:2 * TC], in0=strow[:, TC:2 * TC],
                in1=strow[:, 0:TC]), deps=[t_var])
            t_sd = p.op("scalar", lambda e: e.activation(
                out=strow[:, 0:TC], in_=strow[:, TC:2 * TC], func=AF.Sqrt,
                bias=consts[0:1, 1:2]), deps=[t_var2])
            t_rstd = p.op("vector", lambda e: e.reciprocal(
                out=strow[:, 3 * TC:4 * TC], in_=strow[:, 0:TC]), deps=[t_sd])
            prev["st"] = [t_rstd]
            t_rmu = p.op("tensor", lambda e: e.matmul(
                out=ps_mu[:], lhsT=ones1[:], rhs=strow[:, 2 * TC:3 * TC],
                start=True, stop=True), deps=[t_mu, prev["mu"][0]])
            t_rrs = p.op("tensor", lambda e: e.matmul(
                out=ps_rs[:], lhsT=ones1[:], rhs=strow[:, 3 * TC:4 * TC],
                start=True, stop=True), deps=[t_rstd] + prev["xn1"])
            prev["mu"] = []
            xn1_toks = []
            for c in range(8):
                ts = p.op("vector", lambda e, c=c: e.tensor_sub(
                    out=tmp_t[:], in0=x1_sb[:, c * TC:(c + 1) * TC], in1=ps_mu[:]),
                    deps=[t_rmu] + prev["tmp"] + (x1_toks if c == 0 else []))
                tn = p.op("vector", lambda e, c=c: e.tensor_mul(
                    out=xn1_sb[:, c * TC:(c + 1) * TC], in0=tmp_t[:], in1=ps_rs[:]),
                    deps=[ts, t_rrs] + prev["xn1"] + prev.get("xnwar", []))
                prev["tmp"] = [tn]
                xn1_toks.append(tn)
            prev["x1"] = xn1_toks + [t_x1st]
            t_xnst = p.dma("sync", xn1_out[:, c0:c0 + TC].rearrange(
                "(k p) t -> p k t", p=128),
                xn1_sb[:].rearrange("p (k t) -> p k t", k=8), deps=xn1_toks)
            prev["xn1"] = [t_xnst]
            K_toks, V_toks = [], []
            for ot in range(8):
                pb = ps_a if ot % 2 == 0 else ps_b
                pdep = prev["sA"] if ot % 2 == 0 else prev["sB"]
                lh = [Wk_bf[:, kc * D + ot * 128: kc * D + (ot + 1) * 128]
                      for kc in range(8)]
                rh = [xn1_sb[:, kc * TC:(kc + 1) * TC] for kc in range(8)]
                tm = emit_matmul_accum(p, pb[:], lh, rh,
                                       deps=xn1_toks + [t_wk] + pdep)
                tev = p.op("scalar", lambda e, ot=ot, pb=pb: e.activation(
                    out=K1_sb[:, ot * TC:(ot + 1) * TC], in_=pb[:], func=AF.Relu,
                    bias=bk_sb[:, ot:ot + 1]), deps=[tm, t_bk] + prev["K1"])
                if ot % 2 == 0:
                    prev["sA"] = [tev]
                else:
                    prev["sB"] = [tev]
                K_toks.append(tev)
            for ot in range(8):
                pb = ps_a if ot % 2 == 0 else ps_b
                pdep = prev["sA"] if ot % 2 == 0 else prev["sB"]
                lh = [Wv_bf[:, kc * D + ot * 128: kc * D + (ot + 1) * 128]
                      for kc in range(8)]
                rh = [xn1_sb[:, kc * TC:(kc + 1) * TC] for kc in range(8)]
                tm = emit_matmul_accum(p, pb[:], lh, rh,
                                       deps=xn1_toks + [t_wv] + pdep)
                tev = p.op("scalar", lambda e, ot=ot, pb=pb: e.activation(
                    out=V1_sb[:, ot * TC:(ot + 1) * TC], in_=pb[:], func=AF.Relu,
                    bias=bv_sb[:, ot:ot + 1]), deps=[tm, t_bv] + prev["V1"])
                if ot % 2 == 0:
                    prev["sA"] = [tev]
                else:
                    prev["sB"] = [tev]
                V_toks.append(tev)
            t_vst = p.dma("sync", V1_out[:, c0:c0 + TC].rearrange(
                "(k p) t -> p k t", p=128),
                V1_sb[:].rearrange("p (k t) -> p k t", k=8), deps=V_toks)
            prev["V1"] = [t_vst]
            dm = None
            for c in range(8):
                dm = p.op("tensor", lambda e, c=c: e.matmul(
                    out=ps_dot[:], lhsT=RH_bf[:, c * NST:(c + 1) * NST],
                    rhs=K1_sb[:, c * TC:(c + 1) * TC],
                    start=(c == 0), stop=(c == 7)),
                    deps=K_toks + [t_rh] + (prev["dot"] if c == 0 else []))
            t_val = p.op("vector", lambda e: e.tensor_scalar(
                out=val_sb[:], in0=ps_dot[:], scalar1=0.5, scalar2=None,
                op0=ALU.is_gt), deps=[dm] + prev["val"])
            prev["dot"] = [t_val]
            prev["muwar"] = xn1_toks
            prev["xnwar"] = [t_vst, dm, t_xnst]
            t_vlst = p.dma("sync", valid_out[:, c0:c0 + TC], val_sb[:],
                           deps=[t_val])
            prev["val"] = [t_vlst]
            prev["ygc"] = s_toks
            prev["g"] = x1_toks
            prev["s"] = x1_toks
            prev["xT"] = x1_toks
            prev["K1"] = [dm]
        p.wait_only("gpsimd", [t_vlst, t_vst, t_xnst, t_x1st] + prev["sA"] + prev["sB"])
        p.finalize(block)
    return nc


def host_prep_l4(inputs, ygT_full, gT_full, x_full):
    RH = inputs["L1_attn_ReadingHead"]
    RHblk = np.zeros((D, NST), np.float32)
    for h in range(NST):
        c = h // 2
        off = (h % 2) * 64
        RHblk[c * 128 + off: c * 128 + off + 64, h] = RH[h]
    opT = np.ascontiguousarray(inputs["L0_attn_out_proj_w"].T)
    ident = np.eye(128, dtype=np.float32)
    maps = []
    for c in range(NCORE):
        sl = slice(c * TI, (c + 1) * TI)
        maps.append({
            "ygT_slab": np.ascontiguousarray(ygT_full[:, sl]),
            "opT": opT,
            "x_slab": np.ascontiguousarray(x_full[sl]),
            "gT": np.ascontiguousarray(gT_full[:, sl]),
            "Wk": inputs["L1_attn_Wk"], "bk": inputs["L1_attn_bk"][:, None],
            "Wv": inputs["L1_attn_Wv"], "bv": inputs["L1_attn_bv"][:, None],
            "RHblk": RHblk, "ident": ident,
        })
    return maps


def run_l4(inputs, ygT_full, gT_full, x_full, **kw):
    nc = _get("l4", build_l4)
    maps = host_prep_l4(inputs, ygT_full, gT_full, x_full)
    return run_bass_kernel_spmd(nc, maps, core_ids=list(range(NCORE)), **kw)


def build_l5():
    nc = bass.Bass("TRN2", target_bir_lowering=False, debug=False)
    V1p_in = nc.dram_tensor("V1p", [256, L], F32, kind="ExternalInput").ap()
    val_in = [nc.dram_tensor(f"val{i}", [2, L], F32, kind="ExternalInput").ap()
              for i in range(2)]
    wfb_in = [nc.dram_tensor(f"wfb{i}", [128, 8], F32, kind="ExternalInput").ap()
              for i in range(2)]
    repl2_in = nc.dram_tensor("repl2", [2, 128], F32, kind="ExternalInput").ap()
    attn_out = nc.dram_tensor("attn", [256, L], F32, kind="ExternalOutput").ap()

    TCH = min(4096, L // 4)
    NCH = L // TCH
    import contextlib
    with contextlib.ExitStack() as st:
        sb = lambda name, shape, dt: st.enter_context(nc.sbuf_tensor(name, shape, dt))
        ps = lambda name, shape, dt: st.enter_context(nc.psum_tensor(name, shape, dt))
        sems = std_sems(nc, st)
        block = st.enter_context(nc.Block())
        p = Prog(nc, sems)

        repl2 = sb("repl2_sb", [2, 128], F32)
        val_sb = sb("val_sb2", [2, 4096], F32)
        wfb = sb("wfb_sb", [128, 8], F32)
        wbsum = sb("wbsum", [128, 1], F32)
        Vc = sb("Vc", [128, TCH], F32)
        v_s = sb("v_s", [128, TCH], F32)
        vb_s = sb("vb_s", [128, TCH], F32)
        ev = sb("ev", [128, TCH], F32)
        slot = [sb(f"slot{r}", [128, TCH], F32) for r in range(4)]
        acc = sb("acc", [128, TCH], F32)
        ftail = [sb(f"ftail{i}", [128, 4], F32) for i in range(2)]
        snap = sb("snap", [128, 8], F32)
        ps_v = ps("ps_v", [128, 512], F32)

        t_r2 = p.dma("sync", repl2[:], repl2_in)

        def replicate_v(val_rows, c0, deps, fix0, fixL):
            toks = []
            pv = None
            for sub in range(TCH // 512):
                pv_dep = [] if pv is None else [pv]
                tm = p.op("tensor", lambda e, sub=sub: e.matmul(
                    out=ps_v[:], lhsT=repl2[:],
                    rhs=val_rows[:, c0 + sub * 512:c0 + (sub + 1) * 512],
                    start=True, stop=True), deps=deps + [t_r2] + pv_dep)
                t1 = p.op("scalar", lambda e, sub=sub: e.copy(
                    out=v_s[:, sub * 512:(sub + 1) * 512], in_=ps_v[:]), deps=[tm])
                t2 = p.op("scalar", lambda e, sub=sub: e.activation(
                    out=vb_s[:, sub * 512:(sub + 1) * 512], in_=ps_v[:],
                    func=AF.Identity, bias=1.0, scale=-1.0), deps=[tm])
                pv = t2
                toks += [t1, t2]
            fixes = []
            if fixL:
                tsn = p.op("scalar", lambda e: e.copy(
                    out=snap[:, 2:3], in_=v_s[:, TCH - 1:TCH]), deps=toks)
                f1 = p.op("vector", lambda e: e.memset(v_s[:, TCH - 1:TCH], 0.0),
                          deps=[tsn])
                f2 = p.op("vector", lambda e: e.memset(vb_s[:, TCH - 1:TCH], 1.0),
                          deps=toks)
                fixes += [f1, f2, tsn]
            if fix0:
                f1 = p.op("vector", lambda e: e.memset(v_s[:, 0:1], 0.0), deps=toks)
                f2 = p.op("vector", lambda e: e.memset(vb_s[:, 0:1], 1.0), deps=toks)
                fixes += [f1, f2]
            return toks + fixes

        last_toks = []
        for tile in range(2):
            row0 = tile * 128
            t_w = p.dma("sync", wfb[:], wfb_in[tile], deps=last_toks)
            t_ws = p.op("vector", lambda e: e.tensor_reduce(
                out=wbsum[:], in_=wfb[:, 4:8], axis=AX.X, op=ALU.add),
                deps=[t_w])
            t_bv0 = p.dma("sync", snap[:, 0:1], V1p_in[row0:row0 + 128, 0:1],
                          deps=last_toks, allow_slow_non_contiguous=True)
            ft_cur = 0
            t_ftin = p.op("vector", lambda e: e.memset(ftail[0][:], 0.0),
                          deps=last_toks)
            prev_ft = [t_ftin]
            prev_V = last_toks
            prev_sl = [[], [], [], []]
            prev_acc = []
            prev_ev = []
            st_toks = []
            for ch in range(NCH):
                c0 = ch * TCH
                t_V = p.dma("sync", Vc[:], V1p_in[row0:row0 + 128, c0:c0 + TCH],
                            deps=prev_V)
                t_val = p.dma("sync", val_sb[:, 0:TCH],
                              val_in[tile][:, c0:c0 + TCH],
                              deps=prev_ev + prev_sl[0])
                vt = replicate_v(val_sb, 0, deps=[t_val] + prev_ev + prev_sl[0],
                                 fix0=(ch == 0), fixL=False)
                sl_toks = []
                for r in range(4):
                    if r == 0:
                        t_ev = p.op("vector", lambda e: e.tensor_mul(
                            out=ev[:], in0=v_s[:], in1=Vc[:]),
                            deps=vt + [t_V] + prev_ev)
                    else:
                        t_e1 = p.op("vector", lambda e, r=r: e.tensor_mul(
                            out=ev[:, 1:TCH], in0=v_s[:, 1:TCH],
                            in1=slot[r - 1][:, 0:TCH - 1]),
                            deps=vt + [sl_toks[r - 1]] + prev_ev)
                        t_ev = p.op("vector", lambda e, r=r, ft=ft_cur: e.tensor_mul(
                            out=ev[:, 0:1], in0=v_s[:, 0:1],
                            in1=ftail[ft][:, r - 1:r]), deps=[t_e1] + prev_ft)
                    init = (snap[:, 0:1] if ch == 0 else ftail[ft_cur][:, r:r + 1])
                    ideps = [t_bv0] if ch == 0 else prev_ft
                    t_sc = p.op("vector", lambda e, r=r, init=init: e.tensor_tensor_scan(
                        out=slot[r][:], data0=vb_s[:], data1=ev[:], initial=init,
                        op0=ALU.mult, op1=ALU.add),
                        deps=[t_ev] + ideps + prev_sl[r])
                    sl_toks.append(t_sc)
                    if r == 0:
                        t_ac = p.op("vector", lambda e: e.tensor_scalar(
                            out=acc[:], in0=slot[0][:], scalar1=wfb[:, 0:1],
                            scalar2=None, op0=ALU.mult),
                            deps=[t_sc, t_ws] + prev_acc)
                    else:
                        t_ac = p.op("vector", lambda e, r=r: e.scalar_tensor_tensor(
                            out=acc[:], in0=slot[r][:], scalar=wfb[:, r:r + 1],
                            in1=acc[:], op0=ALU.mult, op1=ALU.add),
                            deps=[t_sc, t_ac])
                ft_nxt = 1 - ft_cur
                nt = []
                for r in range(4):
                    nt.append(p.op("scalar", lambda e, r=r, fn=ft_nxt: e.copy(
                        out=ftail[fn][:, r:r + 1], in_=slot[r][:, TCH - 1:TCH]),
                        deps=[sl_toks[r]] + prev_ft))
                t_st = p.dma("sync", attn_out[row0:row0 + 128, c0:c0 + TCH],
                             acc[:], deps=[t_ac])
                st_toks.append(t_st)
                prev_ft = nt
                ft_cur = ft_nxt
                prev_V = sl_toks + [t_ev if False else sl_toks[0]]
                prev_V = sl_toks
                prev_sl = [[t_ac] for _ in range(4)]
                prev_acc = [t_st]
                prev_ev = sl_toks
            t_ftin = p.op("vector", lambda e: e.memset(ftail[ft_cur][:], 0.0),
                          deps=prev_ft + [t_ac])
            prev_ft = [t_ftin]
            prev_bh = []
            for ch in range(NCH - 1, -1, -1):
                c0 = ch * TCH
                t_V = p.dma("sync", Vc[:], V1p_in[row0:row0 + 128, c0:c0 + TCH],
                            deps=prev_V)
                t_val = p.dma("sync", val_sb[:, 0:TCH],
                              val_in[tile][:, c0:c0 + TCH],
                              deps=prev_ev + prev_sl[0])
                vt = replicate_v(val_sb, 0, deps=[t_val] + prev_ev + prev_sl[0],
                                 fix0=(ch == 0), fixL=(ch == NCH - 1))
                t_la = p.dma("sync", acc[:], attn_out[row0:row0 + 128, c0:c0 + TCH],
                             deps=prev_acc + st_toks)
                acc_extra = []
                if ch == 0:
                    t_fa = p.op("scalar", lambda e: e.copy(
                        out=snap[:, 3:4], in_=acc[:, 0:1]), deps=[t_la])
                    acc_extra = [t_fa]
                if ch == NCH - 1:
                    t_b1 = p.op("vector", lambda e: e.tensor_mul(
                        out=snap[:, 4:5], in0=snap[:, 2:3], in1=Vc[:, TCH - 1:TCH]),
                        deps=vt + [t_V])
                    t_b2 = p.op("vector", lambda e: e.tensor_scalar(
                        out=snap[:, 5:6], in0=snap[:, 2:3], scalar1=-1.0,
                        scalar2=None, op0=ALU.mult), deps=vt)
                    t_b3 = p.op("vector", lambda e: e.tensor_scalar(
                        out=snap[:, 5:6], in0=snap[:, 5:6], scalar1=1.0,
                        scalar2=None, op0=ALU.add), deps=[t_b2])
                    t_b4 = p.op("vector", lambda e: e.tensor_mul(
                        out=snap[:, 5:6], in0=snap[:, 5:6], in1=snap[:, 0:1]),
                        deps=[t_b3, t_bv0])
                    t_binit = p.op("vector", lambda e: e.tensor_add(
                        out=snap[:, 4:5], in0=snap[:, 4:5], in1=snap[:, 5:6]),
                        deps=[t_b1, t_b4])
                sl_toks = []
                for r in range(4):
                    if r == 0:
                        t_e1 = p.op("vector", lambda e: e.tensor_mul(
                            out=ev[:, 0:TCH - 1], in0=v_s[:, 0:TCH - 1],
                            in1=Vc[:, 1:TCH]), deps=vt + [t_V] + prev_ev)
                        if ch == NCH - 1:
                            t_ev = p.op("vector", lambda e: e.memset(
                                ev[:, TCH - 1:TCH], 0.0), deps=[t_e1])
                        else:
                            t_ev = p.op("vector", lambda e: e.tensor_mul(
                                out=ev[:, TCH - 1:TCH], in0=v_s[:, TCH - 1:TCH],
                                in1=snap[:, 1:2]), deps=[t_e1] + prev_bh)
                        init = snap[:, 4:5] if ch == NCH - 1 else ftail[ft_cur][:, 0:1]
                        ideps = ([t_binit] if ch == NCH - 1 else prev_ft)
                        t_sc = p.op("vector", lambda e, init=init: e.tensor_tensor_scan(
                            out=slot[0][:], data0=vb_s[:, ::-1], data1=ev[:, ::-1],
                            initial=init, op0=ALU.mult, op1=ALU.add),
                            deps=[t_ev] + ideps + prev_sl[0])
                    else:
                        t_e1 = p.op("vector", lambda e, r=r: e.tensor_mul(
                            out=ev[:, 1:TCH], in0=v_s[:, TCH - 2::-1],
                            in1=slot[r - 1][:, 0:TCH - 1]),
                            deps=vt + [sl_toks[r - 1]] + prev_ev)
                        t_ev = p.op("vector", lambda e, r=r, ft=ft_cur: e.tensor_mul(
                            out=ev[:, 0:1], in0=v_s[:, TCH - 1:TCH],
                            in1=ftail[ft][:, r - 1:r]), deps=[t_e1] + prev_ft)
                        init = (snap[:, 0:1] if ch == NCH - 1
                                else ftail[ft_cur][:, r:r + 1])
                        ideps = [t_bv0] if ch == NCH - 1 else prev_ft
                        t_sc = p.op("vector", lambda e, r=r, init=init: e.tensor_tensor_scan(
                            out=slot[r][:], data0=vb_s[:, ::-1], data1=ev[:],
                            initial=init, op0=ALU.mult, op1=ALU.add),
                            deps=[t_ev] + ideps + prev_sl[r])
                    sl_toks.append(t_sc)
                    t_ac = p.op("vector", lambda e, r=r: e.scalar_tensor_tensor(
                        out=acc[:], in0=slot[r][:, ::-1], scalar=wfb[:, 4 + r:5 + r],
                        in1=acc[:], op0=ALU.mult, op1=ALU.add),
                        deps=[t_sc, t_la] + acc_extra
                        + ([] if r == 0 else [t_ac]) + prev_acc)
                if ch == 0:
                    t_ac = p.op("vector", lambda e: e.scalar_tensor_tensor(
                        out=acc[:, 0:1], in0=snap[:, 0:1], scalar=wbsum[:, 0:1],
                        in1=snap[:, 3:4], op0=ALU.mult, op1=ALU.add),
                        deps=[t_ac, t_fa, t_ws, t_bv0])
                ft_nxt = 1 - ft_cur
                nt = []
                for r in range(4):
                    nt.append(p.op("scalar", lambda e, r=r, fn=ft_nxt: e.copy(
                        out=ftail[fn][:, r:r + 1], in_=slot[r][:, TCH - 1:TCH]),
                        deps=[sl_toks[r]] + prev_ft))
                t_bh = p.op("scalar", lambda e: e.copy(
                    out=snap[:, 1:2], in_=Vc[:, 0:1]), deps=[t_V] + prev_bh)
                prev_bh = [t_bh]
                t_st = p.dma("sync", attn_out[row0:row0 + 128, c0:c0 + TCH],
                             acc[:], deps=[t_ac])
                prev_ft = nt
                ft_cur = ft_nxt
                prev_V = sl_toks + [t_bh]
                prev_sl = [[t_ac] for _ in range(4)]
                prev_acc = [t_st]
                prev_ev = sl_toks
            last_toks = [t_st] + nt + prev_bh
        p.wait_only("gpsimd", last_toks)
        p.finalize(block)
    return nc


def host_prep_l5(inputs, V1_full, valid_full):
    bw = inputs["L1_attn_bidirection_weight"]
    repl2 = np.zeros((2, 128), np.float32)
    repl2[0, 0:64] = 1.0
    repl2[1, 64:128] = 1.0
    V1r = V1_full.reshape(16, 64, B, L)
    valr = valid_full.reshape(16, B, L)
    maps = []
    for c in range(NCORE):
        pairs = [(pid // 16, pid % 16) for pid in range(c * 4, c * 4 + 4)]
        V1p = np.ascontiguousarray(
            np.concatenate([V1r[h, :, b, :] for (b, h) in pairs], axis=0))
        m = {"V1p": V1p, "repl2": repl2}
        for i in range(2):
            pr = pairs[2 * i:2 * i + 2]
            m[f"val{i}"] = np.ascontiguousarray(
                np.stack([valr[h, b, :] for (b, h) in pr], axis=0))
            wfb = np.zeros((128, 8), np.float32)
            for j, (b, h) in enumerate(pr):
                wfb[j * 64:(j + 1) * 64, :] = bw[h][None, :]
            m[f"wfb{i}"] = wfb
        maps.append(m)
    return maps


def run_l5(inputs, V1_full, valid_full, **kw):
    nc = _get("l5", build_l5)
    maps = host_prep_l5(inputs, V1_full, valid_full)
    return run_bass_kernel_spmd(nc, maps, core_ids=list(range(NCORE)), **kw)


def build_l6():
    nc = bass.Bass("TRN2", target_bir_lowering=False, debug=False)
    xn1_in = nc.dram_tensor("xn1T", [D, TI], BF16, kind="ExternalInput").ap()
    x1_in = nc.dram_tensor("x1T", [D, TI], F32, kind="ExternalInput").ap()
    s1_in = nc.dram_tensor("s1T", [D, TI], F32, kind="ExternalInput").ap()
    wrT_in = nc.dram_tensor("wrT", [D, NST], F32, kind="ExternalInput").ap()
    wmhT_in = nc.dram_tensor("wmhT", [D, D], F32, kind="ExternalInput").ap()
    wmgT_in = nc.dram_tensor("wmgT", [D, D], F32, kind="ExternalInput").ap()
    kT_in = nc.dram_tensor("kT", [MOE_H, MOE_E, MOE_D, MOE_S], F32,
                           kind="ExternalInput").ap()
    vf_in = nc.dram_tensor("vf", [MOE_H, MOE_E, MOE_S, MOE_D], F32,
                           kind="ExternalInput").ap()
    repl_in = nc.dram_tensor("replmat", [NST, NST * 128], F32,
                             kind="ExternalInput").ap()
    ident_in = nc.dram_tensor("ident", [128, 128], F32, kind="ExternalInput").ap()
    out_o = nc.dram_tensor("out_slab", [TI, D], F32, kind="ExternalOutput").ap()

    TCK = 2048
    NSUB = TCK // 512
    import contextlib
    with contextlib.ExitStack() as st:
        sb = lambda name, shape, dt: st.enter_context(nc.sbuf_tensor(name, shape, dt))
        ps = lambda name, shape, dt: st.enter_context(nc.psum_tensor(name, shape, dt))
        sems = std_sems(nc, st)
        block = st.enter_context(nc.Block())
        p = Prog(nc, sems)

        ident = sb("ident_sb", [128, 128], F32)
        ones16 = sb("ones16", [128, NST], F32)
        wrT_bf = sb("wrT_bf", [128, 8 * NST], BF16)
        wmh_bf = sb("wmh_bf", [128, 8 * D], BF16)
        wmg_bf = sb("wmg_bf", [128, 8 * D], BF16)
        repl_sb = sb("repl_sb", [NST, NST * 128], F32)
        xn1_ch = sb("xn1_ch", [128, 8 * TCK], BF16)
        w_sb = sb("w_sb", [NST, TCK], F32)
        lt = sb("lt", [128, NST], F32)
        mx = sb("mx", [128, 8], F32)
        msk = sb("msk", [128, 6 * NST], F32)
        sgc = sb("sgc", [128, 4], F32)
        wtok = sb("wtok", [128, NST], F32)
        k_bf = [sb(f"k_bf{i}", [128, MOE_E * MOE_S], BF16) for i in range(2)]
        v_bf = [sb(f"v_bf{i}", [128, MOE_E * 2 * MOE_D], BF16) for i in range(2)]
        q_bf = sb("q_bf", [128, 512], BF16)
        wrepl = sb("wrepl", [128, NST * 512], F32)
        ymoe = sb("ymoe", [128, 8 * 512], BF16)
        gel = sb("gel", [128, 512], F32)
        wm_bf = sb("wm_bf", [128, 512], BF16)
        x1_sb = sb("x1_sb", [128, 8 * 512], F32)
        s1_sb = sb("s1_sb", [128, 8 * 512], F32)
        g1_sb = sb("g1_sb", [128, 512], F32)
        o_sb = sb("o_sb", [128, 512], F32)
        otile = [sb(f"otile{i}", [128, 128], F32) for i in range(2)]
        ps_lg = ps("ps_lg", [128, NST], F32)
        ps_tr = ps("ps_tr", [NST, 128], F32)
        ps_q = ps("ps_q", [128, 512], F32)
        ps_sc = ps("ps_sc", [128, 512], F32)
        ps_y = ps("ps_y", [128, 512], F32)
        ps_w = ps("ps_w", [128, 512], F32)
        ps_tr2 = ps("ps_tr2", [128, 128], F32)

        t_id = p.dma("sync", ident[:], ident_in)
        tok_o = p.op("vector", lambda e: e.memset(ones16[:], 1.0))
        t_wr = p.dma("gpsimd", wrT_bf[:].rearrange("p (k o) -> p k o", k=8),
                     wrT_in.rearrange("(k p) o -> p k o", p=128))
        t_mh = p.dma("gpsimd", wmh_bf[:].rearrange("p (k o) -> p k o", k=8),
                     wmhT_in.rearrange("(k p) o -> p k o", p=128))
        t_mg = p.dma("gpsimd", wmg_bf[:].rearrange("p (k o) -> p k o", k=8),
                     wmgT_in.rearrange("(k p) o -> p k o", p=128))
        t_rp = p.dma("sync", repl_sb[:], repl_in)

        prev = {"xn1": [], "lt": [], "wsb": [], "tr": [], "kv": [[], []],
                "q": [], "wrepl": [], "ymoe": [], "sc": [], "gel": [], "wm": [],
                "y": [], "x1": [], "s1": [], "g1": [], "o": [], "ot": [],
                "lg": [], "st": [None, None], "msk": [], "mx": [], "wtok": [],
                "sg": []}
        kv_i = 0
        for ck in range(TI // TCK):
            c0 = ck * TCK
            t_xc = p.dma("sync", xn1_ch[:].rearrange("p (k t) -> p k t", k=8),
                         xn1_in.rearrange("(k p) t -> p k t", p=128)[
                             :, :, c0:c0 + TCK], deps=prev["xn1"])
            wsb_toks = []
            for tt in range(TCK // 128):
                dm = None
                for kc in range(8):
                    dm = p.op("tensor", lambda e, kc=kc, tt=tt: e.matmul(
                        out=ps_lg[:], lhsT=xn1_ch[:, kc * TCK + tt * 128:
                                                  kc * TCK + (tt + 1) * 128],
                        rhs=wrT_bf[:, kc * NST:(kc + 1) * NST],
                        start=(kc == 0), stop=(kc == 7)),
                        deps=[t_xc, t_wr] + (prev["lg"] if kc == 0 else []))
                t_lt = p.op("scalar", lambda e: e.copy(out=lt[:], in_=ps_lg[:]),
                            deps=[dm] + prev["lt"])
                prev["lg"] = [t_lt]
                t_mx = p.op("vector", lambda e: e.max(out=mx[:], in_=lt[:]),
                            deps=[t_lt] + prev["mx"])
                t_mA = p.op("vector", lambda e: e.tensor_scalar(
                    out=msk[:, 0:NST], in0=lt[:], scalar1=mx[:, 0:1], scalar2=None,
                    op0=ALU.is_equal), deps=[t_mx] + prev["msk"])
                t_cA = p.op("vector", lambda e: e.tensor_tensor_scan(
                    out=msk[:, NST:2 * NST], data0=ones16[:], data1=msk[:, 0:NST],
                    initial=0.0, op0=ALU.mult, op1=ALU.add), deps=[t_mA, tok_o])
                t_e1 = p.op("vector", lambda e: e.tensor_scalar(
                    out=msk[:, NST:2 * NST], in0=msk[:, NST:2 * NST], scalar1=1.0,
                    scalar2=None, op0=ALU.is_equal), deps=[t_cA])
                t_m1 = p.op("vector", lambda e: e.tensor_mul(
                    out=msk[:, 0:NST], in0=msk[:, 0:NST], in1=msk[:, NST:2 * NST]),
                    deps=[t_e1])
                t_mB = p.op("vector", lambda e: e.tensor_scalar(
                    out=msk[:, 2 * NST:3 * NST], in0=lt[:], scalar1=mx[:, 1:2],
                    scalar2=None, op0=ALU.is_equal), deps=[t_mx, t_lt])
                t_cB = p.op("vector", lambda e: e.tensor_tensor_scan(
                    out=msk[:, 3 * NST:4 * NST], data0=ones16[:],
                    data1=msk[:, 2 * NST:3 * NST],
                    initial=0.0, op0=ALU.mult, op1=ALU.add), deps=[t_mB])
                t_eq = p.op("vector", lambda e: e.tensor_tensor(
                    out=sgc[:, 2:3], in0=mx[:, 0:1], in1=mx[:, 1:2],
                    op=ALU.is_equal), deps=[t_mx] + prev["sg"])
                t_tg = p.op("vector", lambda e: e.tensor_scalar(
                    out=sgc[:, 2:3], in0=sgc[:, 2:3], scalar1=1.0, scalar2=None,
                    op0=ALU.add), deps=[t_eq])
                t_e2 = p.op("vector", lambda e: e.tensor_scalar(
                    out=msk[:, 3 * NST:4 * NST], in0=msk[:, 3 * NST:4 * NST],
                    scalar1=sgc[:, 2:3], scalar2=None, op0=ALU.is_equal),
                    deps=[t_cB, t_tg])
                t_m2 = p.op("vector", lambda e: e.tensor_mul(
                    out=msk[:, 2 * NST:3 * NST], in0=msk[:, 2 * NST:3 * NST],
                    in1=msk[:, 3 * NST:4 * NST]), deps=[t_e2])
                t_sg = p.op("scalar", lambda e: e.activation(
                    out=sgc[:, 0:2], in_=mx[:, 0:2], func=AF.Sigmoid),
                    deps=[t_mx] + prev["sg"])
                t_w1 = p.op("vector", lambda e: e.tensor_scalar(
                    out=wtok[:], in0=msk[:, 0:NST], scalar1=sgc[:, 0:1],
                    scalar2=None, op0=ALU.mult), deps=[t_m1, t_sg] + prev["wtok"])
                t_wt = p.op("vector", lambda e: e.scalar_tensor_tensor(
                    out=wtok[:], in0=msk[:, 2 * NST:3 * NST], scalar=sgc[:, 1:2],
                    in1=wtok[:], op0=ALU.mult, op1=ALU.add), deps=[t_m2, t_w1])
                t_tr = p.op("tensor", lambda e: e.transpose(
                    out=ps_tr[:], in_=wtok[:], identity=ident[:]),
                    deps=[t_wt, t_id] + prev["tr"])
                t_we = p.op("scalar", lambda e, tt=tt: e.copy(
                    out=w_sb[:, tt * 128:(tt + 1) * 128], in_=ps_tr[:]),
                    deps=[t_tr] + prev["wsb"])
                prev["tr"] = [t_we]
                prev["msk"] = [t_wt]
                prev["mx"] = [t_w1]
                prev["sg"] = [t_wt]
                prev["wtok"] = [t_we]
                prev["lt"] = [t_mB]
                wsb_toks.append(t_we)
            st_toks = []
            for sub in range(NSUB):
                s0 = sub * 512
                t_x1 = p.dma("sync", x1_sb[:].rearrange("p (k t) -> p k t", k=8),
                             x1_in.rearrange("(k p) t -> p k t", p=128)[
                                 :, :, c0 + s0:c0 + s0 + 512], deps=prev["x1"])
                t_s1 = p.dma("sync", s1_sb[:].rearrange("p (k t) -> p k t", k=8),
                             s1_in.rearrange("(k p) t -> p k t", p=128)[
                                 :, :, c0 + s0:c0 + s0 + 512], deps=prev["s1"])
                wr_toks = []
                for en in range(NST):
                    t_mw = p.op("tensor", lambda e, en=en, s0=s0: e.matmul(
                        out=ps_w[:], lhsT=repl_sb[:, en * 128:(en + 1) * 128],
                        rhs=w_sb[:, s0:s0 + 512], start=True, stop=True),
                        deps=wsb_toks + [t_rp] + prev["wrepl"])
                    t_ew = p.op("scalar", lambda e, en=en: e.copy(
                        out=wrepl[:, en * 512:(en + 1) * 512], in_=ps_w[:]),
                        deps=[t_mw] + (prev["wm"] if en == 0 else []))
                    prev["wrepl"] = [t_ew]
                    wr_toks.append(t_ew)
                y_toks = []
                for h in range(MOE_H):
                    kv = kv_i % 2
                    kv_i += 1
                    t_k = p.dma("gpsimd",
                                k_bf[kv][:].rearrange("p (e s) -> p e s", e=MOE_E),
                                kT_in[h].transpose([1, 0, 2]),
                                deps=prev["kv"][kv])
                    t_v = p.dma("gpsimd",
                                v_bf[kv][:].rearrange("p (e c d) -> p e c d",
                                                      e=MOE_E, c=2),
                                vf_in[h].rearrange("e (c sp) d -> sp e c d", sp=128),
                                deps=prev["kv"][kv])
                    qm = None
                    for kc in range(8):
                        qm = p.op("tensor", lambda e, kc=kc, h=h, s0=s0: e.matmul(
                            out=ps_q[:], lhsT=wmh_bf[:, kc * D + h * 128:
                                                     kc * D + (h + 1) * 128],
                            rhs=xn1_ch[:, kc * TCK + s0:kc * TCK + s0 + 512],
                            start=(kc == 0), stop=(kc == 7)),
                            deps=[t_xc, t_mh] + (prev["q"] if kc == 0 else []))
                    t_q = p.op("scalar", lambda e: e.copy(out=q_bf[:], in_=ps_q[:]),
                               deps=[qm] + prev["sc"])
                    prev["q"] = [t_q]
                    ny = 0
                    for en in range(NST):
                        for sc in range(2):
                            t_s = p.op("tensor", lambda e, en=en, sc=sc, kv=kv: e.matmul(
                                out=ps_sc[:],
                                lhsT=k_bf[kv][:, en * MOE_S + sc * 128:
                                              en * MOE_S + (sc + 1) * 128],
                                rhs=q_bf[:], start=True, stop=True),
                                deps=[t_q, t_k] + prev["gel"])
                            t_g = p.op("scalar", lambda e: e.activation(
                                out=gel[:], in_=ps_sc[:], func=AF.Gelu),
                                deps=[t_s] + prev["wm"])
                            prev["gel"] = [t_g]
                            t_wm = p.op("vector", lambda e, en=en: e.tensor_mul(
                                out=wm_bf[:], in0=gel[:],
                                in1=wrepl[:, en * 512:(en + 1) * 512]),
                                deps=[t_g, wr_toks[en]] + prev["y"])
                            prev["wm"] = [t_wm]
                            t_y = p.op("tensor", lambda e, en=en, sc=sc, kv=kv,
                                       first=(ny == 0), last=(ny == 31): e.matmul(
                                out=ps_y[:],
                                lhsT=v_bf[kv][:, (en * 2 + sc) * MOE_D:
                                              (en * 2 + sc + 1) * MOE_D],
                                rhs=wm_bf[:], start=first, stop=last),
                                deps=[t_wm, t_v] + (prev["ymoe"] if ny == 0 else []))
                            prev["y"] = [t_y]
                            ny += 1
                    t_ye = p.op("scalar", lambda e, h=h: e.copy(
                        out=ymoe[:, h * 512:(h + 1) * 512], in_=ps_y[:]),
                        deps=[t_y] + prev["ymoe"])
                    prev["ymoe"] = [t_ye]
                    prev["kv"][kv] = [t_y, t_s]
                    prev["sc"] = [t_y]
                    y_toks.append(t_ye)
                tr_toks = []
                for oc in range(8):
                    gm = None
                    for hc in range(8):
                        gm = p.op("tensor", lambda e, hc=hc, oc=oc: e.matmul(
                            out=ps_q[:], lhsT=wmg_bf[:, hc * D + oc * 128:
                                                     hc * D + (oc + 1) * 128],
                            rhs=ymoe[:, hc * 512:(hc + 1) * 512],
                            start=(hc == 0), stop=(hc == 7)),
                            deps=y_toks + [t_mg] + prev["q"])
                    t_g1 = p.op("scalar", lambda e: e.activation(
                        out=g1_sb[:], in_=ps_q[:], func=AF.Silu),
                        deps=[gm] + prev["g1"])
                    prev["q"] = [t_g1]
                    t_o1 = p.op("vector", lambda e, oc=oc: e.tensor_mul(
                        out=o_sb[:], in0=s1_sb[:, oc * 512:(oc + 1) * 512],
                        in1=g1_sb[:]), deps=[t_g1, t_s1] + prev["o"])
                    t_o2 = p.op("vector", lambda e, oc=oc: e.tensor_add(
                        out=o_sb[:], in0=o_sb[:],
                        in1=x1_sb[:, oc * 512:(oc + 1) * 512]), deps=[t_o1, t_x1])
                    prev["g1"] = [t_o1]
                    for tt in range(4):
                        ob = otile[tt % 2]
                        t_t = p.op("tensor", lambda e, tt=tt: e.transpose(
                            out=ps_tr2[:], in_=o_sb[:, tt * 128:(tt + 1) * 128],
                            identity=ident[:]),
                            deps=[t_o2, t_id] + prev["ot"])
                        t_te = p.op("scalar", lambda e, ob=ob: e.copy(
                            out=ob[:], in_=ps_tr2[:]),
                            deps=[t_t] + prev["st"][tt % 2:tt % 2 + 1])
                        prev["ot"] = [t_te]
                        r0 = c0 + s0 + tt * 128
                        t_sto = p.dma("sync",
                                      out_o[r0:r0 + 128, oc * 128:(oc + 1) * 128],
                                      ob[:], deps=[t_te])
                        prev["st"][tt % 2] = t_sto
                        tr_toks.append(t_sto)
                    prev["o"] = [tr_toks[-1]]
                sts = tr_toks[-8:]
                st_toks += sts
                prev["x1"] = [t_o2]
                prev["s1"] = [t_o1]
                prev["ymoe"] = prev["q"]
                prev["wrepl"] = prev["wm"]
            prev["xn1"] = [qm] + wsb_toks
            prev["wsb"] = prev["wrepl"]
        p.wait_only("gpsimd", st_toks)
        p.finalize(block)
    return nc


def host_prep_l6(inputs, xn1_full, x1_full, s1_full):
    replmat = np.zeros((NST, NST * 128), np.float32)
    for n in range(NST):
        replmat[n, n * 128:(n + 1) * 128] = 1.0
    kT = np.ascontiguousarray(np.transpose(inputs["L1_ffn_k_ffwd"], (0, 1, 3, 2)))
    ident = np.eye(128, dtype=np.float32)
    maps = []
    for c in range(NCORE):
        sl = slice(c * TI, (c + 1) * TI)
        maps.append({
            "xn1T": np.ascontiguousarray(xn1_full[:, sl]),
            "x1T": np.ascontiguousarray(x1_full[:, sl]),
            "s1T": np.ascontiguousarray(s1_full[:, sl]),
            "wrT": np.ascontiguousarray(inputs["L1_ffn_w_router"].T),
            "wmhT": np.ascontiguousarray(inputs["L1_ffn_w_mh"].T),
            "wmgT": np.ascontiguousarray(inputs["L1_ffn_w_mg"].T),
            "kT": kT, "vf": inputs["L1_ffn_v_ffwd"],
            "replmat": replmat, "ident": ident,
        })
    return maps


def run_l6(inputs, xn1_full, x1_full, s1_full, **kw):
    nc = _get("l6", build_l6)
    maps = host_prep_l6(inputs, xn1_full, x1_full, s1_full)
    return run_bass_kernel_spmd(nc, maps, core_ids=list(range(NCORE)), **kw)


_built = {}


def _get(name, builder):
    if name not in _built:
        _built[name] = builder()
    return _built[name]


def host_prep_l1(inputs):
    x = np.ascontiguousarray(inputs["L0_attn_x"].reshape(T, D))
    wgT = np.ascontiguousarray(inputs["L0_ffn_w_gate"].T)
    wuT = np.ascontiguousarray(inputs["L0_ffn_w_up"].T)
    wdT = np.ascontiguousarray(inputs["L0_ffn_w_down"].T)
    ident = np.eye(128, dtype=np.float32)
    maps = []
    for c in range(NCORE):
        maps.append({
            "x_slab": np.ascontiguousarray(x[c * TI:(c + 1) * TI]),
            "wgT": wgT, "wuT": wuT, "wdT": wdT, "ident": ident,
        })
    return maps


def run_l1(inputs, **kw):
    nc = _get("l1", build_l1)
    maps = host_prep_l1(inputs)
    return run_bass_kernel_spmd(nc, maps, core_ids=list(range(NCORE)), **kw)


def kernel(**inputs):
    inputs = {k: np.asarray(v) for k, v in inputs.items()}
    r1 = run_l1(inputs)
    xnT_full = np.ascontiguousarray(
        np.concatenate([np.asarray(r["xnT"]) for r in r1.results], axis=1))
    gT_full = np.concatenate([r["gT"] for r in r1.results], axis=1)
    r2 = run_l2(inputs, xnT_full)
    pd_all = np.stack([r["pd"] for r in r2.results], axis=0)
    pB_all = np.stack([r["pB"] for r in r2.results], axis=0)
    pC_all = np.stack([r["pC"] for r in r2.results], axis=0)
    u_all = np.concatenate([r["u_sl"] for r in r2.results], axis=0)
    resS_all = np.concatenate([r["resS"] for r in r2.results], axis=0)
    r3 = run_l3(inputs, pd_all, pB_all, pC_all, u_all, resS_all)
    ygT_full = np.ascontiguousarray(
        np.concatenate([np.asarray(r["ygT"]) for r in r3.results], axis=0))
    x_full = np.ascontiguousarray(inputs["L0_attn_x"].reshape(T, D))
    r4 = run_l4(inputs, ygT_full, gT_full, x_full)
    x1_full = np.concatenate([r["x1T"] for r in r4.results], axis=1)
    xn1_full = np.ascontiguousarray(
        np.concatenate([np.asarray(r["xn1T"]) for r in r4.results], axis=1))
    V1_full = np.ascontiguousarray(
        np.concatenate([r["V1T"] for r in r4.results], axis=1))
    valid_full = np.ascontiguousarray(
        np.concatenate([r["valid"] for r in r4.results], axis=1))
    r5 = run_l5(inputs, V1_full, valid_full)
    s1_full = np.zeros((B, L, 16, 64), np.float32)
    for c in range(NCORE):
        a = r5.results[c]["attn"]
        for j in range(4):
            pid = c * 4 + j
            s1_full[pid // 16, :, pid % 16, :] = a[j * 64:(j + 1) * 64, :].T
    s1T = np.ascontiguousarray(
        s1_full.reshape(T, D).T)
    r6 = run_l6(inputs, xn1_full, np.ascontiguousarray(x1_full), s1T)
    out = np.concatenate([r["out_slab"] for r in r6.results], axis=0)
    return np.ascontiguousarray(out.reshape(B, L, D))


# version 02
# speedup vs baseline: 1.4074x; 1.4074x over previous
import numpy as np
import concourse.bass as bass
import concourse.mybir as mybir
from concourse.bass_utils import run_bass_kernel_spmd

F32 = mybir.dt.float32
BF16 = mybir.dt.bfloat16
I32 = mybir.dt.int32
AF = mybir.ActivationFunctionType
ALU = mybir.AluOpType
AX = mybir.AxisListType

NCORE = 8
B, L, D = 2, 16384, 1024
T = B * L
TI = T // NCORE
DI, NST, DTR, KC = 2048, 16, 64, 4
DC = DI // NCORE
FH = 3072
EPS = 1e-5
MOE_H, MOE_E, MOE_S, MOE_D = 8, 16, 256, 128


class Prog:

    ENGINES = ("sync", "vector", "scalar", "tensor", "gpsimd")

    def __init__(self, nc, sems):
        self.nc = nc
        self.sems = sems
        self.cnt = {k: 0 for k in sems}
        self.ops = {e: [] for e in self.ENGINES}
        self.seen = {e: {} for e in self.ENGINES}

    def _merge(self, deps):
        fr = {}
        for d in deps:
            if d is None:
                continue
            for s, v in d.items():
                if fr.get(s, -1) < v:
                    fr[s] = v
        return fr

    NOCACHE = False

    def _waits(self, engine, frontier):
        cache = self.seen[engine]
        out = {}
        for s, v in frontier.items():
            if cache.get(s, -1) < v:
                out[s] = v
                cache[s] = v
            elif self.NOCACHE:
                out[s] = v
        return out

    def op(self, engine, fn, deps=(), sem=None, step=1):
        if sem is None:
            sem = "c_" + engine
        fr = self._merge(deps)
        waits = self._waits(engine, fr)
        self.cnt[sem] += step
        val = self.cnt[sem]
        sems = self.sems

        def run(eng, waits=waits, fn=fn, sem=sem, step=step):
            for s, v in waits.items():
                eng.wait_ge(sems[s], v)
            inst = fn(eng)
            inst.then_inc(sems[sem], step)
        self.ops[engine].append(run)
        tok = dict(fr)
        tok[sem] = max(tok.get(sem, 0), val)
        return tok

    def raw(self, engine, fn, deps=()):
        waits = self._waits(engine, self._merge(deps))
        sems = self.sems

        def run(eng, waits=waits, fn=fn):
            for s, v in waits.items():
                eng.wait_ge(sems[s], v)
            fn(eng)
        self.ops[engine].append(run)
        return None

    def dma(self, queue, out, in_, deps=(), **kw):
        base = "dS" if queue == "sync" else "dG"
        if not hasattr(self, "_dmarr"):
            self._dmarr = {}
        idx = self._dmarr.get(queue, 0)
        self._dmarr[queue] = idx + 1
        sem = f"{base}{idx % 4}"
        prev_done = 16 * (idx // 4)
        fr = self._merge(deps)
        if prev_done > 0:
            fr[sem] = max(fr.get(sem, 0), prev_done)
        waits = self._waits(queue, fr)
        self.cnt[sem] += 16
        val = self.cnt[sem]
        sems = self.sems

        def run(eng, waits=waits, sem=sem, out=out, in_=in_, kw=kw):
            for s, v in waits.items():
                eng.wait_ge(sems[s], v)
            eng.dma_start(out=out, in_=in_, **kw).then_inc(sems[sem], 16)
        self.ops[queue].append(run)
        tok = dict(fr)
        tok[sem] = max(tok.get(sem, 0), val)
        return tok

    def wait_only(self, engine, deps):
        waits = self._waits(engine, self._merge(deps))
        sems = self.sems

        def run(eng, waits=waits):
            for s, v in waits.items():
                eng.wait_ge(sems[s], v)
        self.ops[engine].append(run)

    def finalize(self, block):
        for name in self.ENGINES:
            ops = self.ops[name]
            if not ops:
                continue

            def section(eng, ops=ops):
                for f in ops:
                    f(eng)
            getattr(block, name)(section)


def std_sems(nc, stack):
    sems = {}
    names = ["c_vector", "c_scalar", "c_tensor", "c_gpsimd", "c_sync", "cc"]
    names += [f"dS{i}" for i in range(4)] + [f"dG{i}" for i in range(4)]
    for name in names:
        sems[name] = stack.enter_context(nc.semaphore(name))
    return sems


def emit_matmul_accum(p, psum_ap, lhsT_tiles, rhs_tiles, deps):
    n = len(lhsT_tiles)
    tok = None
    for k in range(n):
        lt, rt = lhsT_tiles[k], rhs_tiles[k]
        tok = p.op("tensor",
                   lambda e, lt=lt, rt=rt, k=k: e.matmul(
                       out=psum_ap, lhsT=lt, rhs=rt,
                       start=(k == 0), stop=(k == n - 1)),
                   deps=deps if k == 0 else ())
    return tok


def build_l1():
    nc = bass.Bass("TRN2", target_bir_lowering=False, debug=False)
    x_in = nc.dram_tensor("x_slab", [TI, D], F32, kind="ExternalInput").ap()
    wgT_in = nc.dram_tensor("wgT", [D, FH], I32, kind="ExternalInput").ap()
    wuT_in = nc.dram_tensor("wuT", [D, FH], I32, kind="ExternalInput").ap()
    wdT_in = nc.dram_tensor("wdT", [FH, D], I32, kind="ExternalInput").ap()
    ident_in = nc.dram_tensor("ident", [128, 128], F32, kind="ExternalInput").ap()
    xnT_out = nc.dram_tensor("xnT", [D, TI], BF16, kind="ExternalOutput").ap()
    gT_out = nc.dram_tensor("gT", [D, TI], F32, kind="ExternalOutput").ap()

    import contextlib
    with contextlib.ExitStack() as st:
        sb = lambda name, shape, dt: st.enter_context(nc.sbuf_tensor(name, shape, dt))
        ps = lambda name, shape, dt: st.enter_context(nc.psum_tensor(name, shape, dt))
        sems = std_sems(nc, st)
        block = st.enter_context(nc.Block())
        p = Prog(nc, sems)

        consts = sb("consts", [128, 5], F32)
        x_t = sb("x_t", [128, D], F32)
        stat = sb("stat", [128, 8], F32)
        xsq = sb("xsq", [128, D], F32)
        xn_tok = sb("xn_tok", [128, D], BF16)
        ident = sb("ident_sb", [128, 128], F32)
        ident_bf = sb("identbf", [128, 128], BF16)
        xnT_buf = sb("xnT_buf", [128, 8 * 128], BF16)
        ps_tr = ps("ps_tr", [128, 128], BF16)
        wg_bf = sb("wg_bf", [128, 8 * FH], BF16)
        wu_bf = sb("wu_bf", [128, 8 * FH], BF16)
        wd_bf = sb("wd_bf", [128, 24 * D], BF16)
        wstage = sb("wstage", [128, FH], I32)
        xnT_chunk = sb("xnT_chunk", [128, 8 * 512], BF16)
        h_bf = sb("h_bf", [128, 24 * 512], BF16)
        sg_t = sb("sg_t", [128, 512], F32)
        ps_g = ps("ps_g", [128, 512], F32)
        ps_u = ps("ps_u", [128, 512], F32)
        ps_d = ps("ps_d", [128, 512], F32)
        g_t = sb("g_t", [128, 512], F32)

        p.raw("vector", lambda e: e.memset(consts[:, 0:1], EPS))
        p.raw("vector", lambda e: e.memset(consts[:, 1:2], 1.0 / 127.0))
        p.raw("vector", lambda e: e.memset(consts[:, 2:3], 1.0 / (127.0 * 127.0)))
        tok_consts = p.op("vector", lambda e: e.memset(consts[:, 3:4], 1.0 / D))
        tok_id = p.dma("sync", ident[:], ident_in)
        tok_idbf = p.op("vector", lambda e: e.tensor_copy(out=ident_bf[:], in_=ident[:]),
                        deps=[tok_id])

        wtoks = []
        for (src, dst, nch) in ((wgT_in, wg_bf, 8), (wuT_in, wu_bf, 8), (wdT_in, wd_bf, 24)):
            width = src.shape[1]
            for kc in range(nch):
                t0 = p.dma("gpsimd", wstage[:, 0:width], src[kc * 128:(kc + 1) * 128, :],
                           deps=wtoks[-1:])
                t2 = p.op("vector", lambda e, d=dst, kc=kc, w=width: e.tensor_copy(
                    out=d[:, kc * w:(kc + 1) * w], in_=wstage[:, 0:w]), deps=[t0])
                wtoks.append(t2)
        wdone = wtoks[-1]

        prev_cons = []
        prev_ev = []
        prev_st = []
        for tt in range(TI // 128):
            tok_x = p.dma("sync", x_t[:], x_in[tt * 128:(tt + 1) * 128, :],
                          deps=prev_cons)
            t_sum = p.op("vector", lambda e: e.tensor_reduce(
                out=stat[:, 0:1], in_=x_t[:], axis=AX.X, op=ALU.add), deps=[tok_x])
            t_sq = p.op("scalar", lambda e: e.activation(
                out=xsq[:], in_=x_t[:], func=AF.Square, accum_out=stat[:, 1:2]),
                deps=[tok_x])
            t_mu = p.op("vector", lambda e: e.tensor_scalar(
                out=stat[:, 2:3], in0=stat[:, 0:1], scalar1=consts[:, 3:4], scalar2=None,
                op0=ALU.mult), deps=[t_sum, tok_consts])
            t_msq = p.op("vector", lambda e: e.tensor_mul(
                out=stat[:, 3:4], in0=stat[:, 2:3], in1=stat[:, 2:3]), deps=[t_mu])
            t_var = p.op("vector", lambda e: e.tensor_scalar(
                out=stat[:, 4:5], in0=stat[:, 1:2], scalar1=consts[:, 3:4],
                scalar2=stat[:, 3:4], op0=ALU.mult, op1=ALU.subtract),
                deps=[t_sq, t_msq])
            t_sd = p.op("scalar", lambda e: e.activation(
                out=stat[:, 5:6], in_=stat[:, 4:5], func=AF.Sqrt, bias=consts[:, 0:1]),
                deps=[t_var, tok_consts])
            t_rstd = p.op("vector", lambda e: e.reciprocal(
                out=stat[:, 6:7], in_=stat[:, 5:6]), deps=[t_sd])
            t_xn = p.op("vector", lambda e: e.tensor_scalar(
                out=xn_tok[:], in0=x_t[:], scalar1=stat[:, 2:3], scalar2=stat[:, 6:7],
                op0=ALU.subtract, op1=ALU.mult), deps=[t_rstd, t_mu] + prev_ev)
            ev_toks = []
            for c in range(8):
                t_tr = p.op("tensor", lambda e, c=c: e.transpose(
                    out=ps_tr[:], in_=xn_tok[:, c * 128:(c + 1) * 128],
                    identity=ident_bf[:]), deps=[t_xn, tok_idbf] + ev_toks[-1:])
                t_ev = p.op("scalar", lambda e, c=c: e.copy(
                    out=xnT_buf[:, c * 128:(c + 1) * 128], in_=ps_tr[:]),
                    deps=[t_tr] + prev_st)
                ev_toks.append(t_ev)
            t_st = p.dma("sync", xnT_out.rearrange("(c p) t -> c p t", p=128)[
                :, :, tt * 128:(tt + 1) * 128].transpose([1, 0, 2]),
                xnT_buf[:].rearrange("p (c t) -> p c t", c=8), deps=ev_toks)
            prev_cons = [t_xn, t_st, t_sq]
            prev_ev = [ev_toks[-1]]
            prev_st = [t_st]

        ln_done = prev_cons
        for ts in range(TI // 512):
            tok_xc = p.dma("sync", xnT_chunk[:].rearrange("p (c t) -> p c t", c=8),
                           xnT_out.rearrange("(c p) t -> c p t", p=128)[
                               :, :, ts * 512:(ts + 1) * 512].transpose([1, 0, 2]),
                           deps=ln_done)
            h_toks = []
            for ht in range(24):
                lg = [wg_bf[:, kc * FH + ht * 128: kc * FH + (ht + 1) * 128]
                      for kc in range(8)]
                lu = [wu_bf[:, kc * FH + ht * 128: kc * FH + (ht + 1) * 128]
                      for kc in range(8)]
                rx = [xnT_chunk[:, kc * 512:(kc + 1) * 512] for kc in range(8)]
                tg = emit_matmul_accum(p, ps_g[:], lg, rx, deps=[tok_xc, wdone] + h_toks[-1:])
                t_sg = p.op("scalar", lambda e: e.activation(
                    out=sg_t[:], in_=ps_g[:], func=AF.Silu, scale=consts[:, 1:2]),
                    deps=[tg])
                tu = emit_matmul_accum(p, ps_u[:], lu, rx, deps=[t_sg])
                t_h = p.op("vector", lambda e, ht=ht: e.tensor_tensor(
                    out=h_bf[:, ht * 512:(ht + 1) * 512], in0=sg_t[:], in1=ps_u[:],
                    op=ALU.mult), deps=[tu, t_sg])
                h_toks.append(t_h)
            st_toks = []
            for ot in range(8):
                ld = [wd_bf[:, hc * D + ot * 128: hc * D + (ot + 1) * 128]
                      for hc in range(24)]
                rh = [h_bf[:, hc * 512:(hc + 1) * 512] for hc in range(24)]
                td = emit_matmul_accum(p, ps_d[:], ld, rh, deps=h_toks + st_toks[-1:])
                t_g = p.op("scalar", lambda e: e.activation(
                    out=g_t[:], in_=ps_d[:], func=AF.Silu, scale=consts[:, 2:3]),
                    deps=[td])
                t_st = p.dma("sync", gT_out[ot * 128:(ot + 1) * 128,
                                            ts * 512:(ts + 1) * 512], g_t[:],
                             deps=[t_g])
                st_toks.append(t_st)
            ln_done = st_toks + h_toks

        p.wait_only("gpsimd", ln_done)
        p.finalize(block)
    return nc


def build_l2():
    nc = bass.Bass("TRN2", target_bir_lowering=False, debug=False)
    xnT_in = nc.dram_tensor("xnT_full", [D, T], BF16, kind="ExternalInput").ap()
    winT_in = nc.dram_tensor("winT", [D, 2 * DC], F32, kind="ExternalInput").ap()
    convw_in = nc.dram_tensor("convw", [DC, KC], F32, kind="ExternalInput").ap()
    convb_in = nc.dram_tensor("convb", [DC, 1], F32, kind="ExternalInput").ap()
    xpT_in = nc.dram_tensor("xpT", [DC, 96], F32, kind="ExternalInput").ap()
    pd_out = nc.dram_tensor("pd", [DTR, T], F32, kind="ExternalOutput").ap()
    pB_out = nc.dram_tensor("pB", [NST, T], F32, kind="ExternalOutput").ap()
    pC_out = nc.dram_tensor("pC", [NST, T], F32, kind="ExternalOutput").ap()
    u_out = nc.dram_tensor("u_sl", [DC, T], F32, kind="ExternalOutput").ap()
    resS_out = nc.dram_tensor("resS", [DC, T], F32, kind="ExternalOutput").ap()

    TC = 2048
    NSUB = TC // 512
    import contextlib
    with contextlib.ExitStack() as st:
        sb = lambda name, shape, dt: st.enter_context(nc.sbuf_tensor(name, shape, dt))
        ps = lambda name, shape, dt: st.enter_context(nc.psum_tensor(name, shape, dt))
        sems = std_sems(nc, st)
        block = st.enter_context(nc.Block())
        p = Prog(nc, sems)

        xnc = sb("xnc", [128, 8 * TC], BF16)
        win_bf = sb("win_bf", [128, 8 * 512], BF16)
        xpT_bf = sb("xpT_bf", [128, 2 * 96], BF16)
        convw_sb = sb("convw_sb", [128, 2 * KC], F32)
        convb_sb = sb("convb_sb", [128, 2], F32)
        xpart = [sb(f"xpart{i}", [128, TC + 3], F32) for i in range(2)]
        u_f = [sb(f"u_f{i}", [128, TC], F32) for i in range(2)]
        u_bf = [sb(f"u_bf{i}", [128, TC], BF16) for i in range(2)]
        ev_t = sb("ev_t", [128, 512], F32)
        p_sb = sb("p_sb", [96, 512], F32)
        ps_a = ps("ps_a", [128, 512], F32)
        ps_b = ps("ps_b", [128, 512], F32)
        ps_p = ps("ps_p", [96, 512], F32)

        t_w = p.dma("gpsimd", win_bf[:].rearrange("p (k o) -> p k o", k=8),
                    winT_in.rearrange("(k p) o -> p k o", p=128))
        t_xp = p.dma("gpsimd", xpT_bf[:].rearrange("p (k o) -> p k o", k=2),
                     xpT_in.rearrange("(k p) o -> p k o", p=128))
        t_cw = p.dma("sync", convw_sb[:].rearrange("p (k o) -> p k o", k=2),
                     convw_in.rearrange("(k p) o -> p k o", p=128))
        t_cb = p.dma("sync", convb_sb[:].rearrange("p (k o) -> p k o", k=2),
                     convb_in.rearrange("(k p) o -> p k o", p=128),
                     allow_slow_non_contiguous=True)
        wdeps = [t_w, t_xp, t_cw, t_cb]

        prev = {"xnc": [], "xpart": [], "u": [], "psA": [], "psB": [], "psP": [],
                "ev": [], "psb": []}
        for tc in range(T // TC):
            batch_start = (tc % (L // TC) == 0)
            t_x = p.dma("sync", xnc[:].rearrange("p (k t) -> p k t", k=8),
                        xnT_in.rearrange("(k p) t -> p k t", p=128)[
                            :, :, tc * TC:(tc + 1) * TC],
                        deps=prev["xnc"])
            mm_toks = []
            conv_deps = []
            res_sts = []
            for ot in range(4):
                for sub in range(NSUB):
                    lh = [win_bf[:, kc * 512 + ot * 128:(kc) * 512 + (ot + 1) * 128]
                          for kc in range(8)]
                    rh = [xnc[:, kc * TC + sub * 512: kc * TC + (sub + 1) * 512]
                          for kc in range(8)]
                    pbank = ps_a if (ot * NSUB + sub) % 2 == 0 else ps_b
                    pdep = prev["psA"] if pbank is ps_a else prev["psB"]
                    tm = emit_matmul_accum(p, pbank[:], lh, rh,
                                           deps=[t_x, t_w] + pdep)
                    mm_toks.append(tm)
                    if ot < 2:
                        tev = p.op("scalar", lambda e, ot=ot, sub=sub, pb=pbank: e.copy(
                            out=xpart[ot][:, 3 + sub * 512: 3 + (sub + 1) * 512],
                            in_=pb[:]), deps=[tm] + prev["xpart"])
                        conv_deps.append(tev)
                    else:
                        tev = p.op("scalar", lambda e, pb=pbank: e.activation(
                            out=ev_t[:], in_=pb[:], func=AF.Silu),
                            deps=[tm] + prev["ev"])
                        tst = p.dma("sync", resS_out[(ot - 2) * 128:(ot - 1) * 128,
                                                     tc * TC + sub * 512:
                                                     tc * TC + (sub + 1) * 512],
                                    ev_t[:], deps=[tev])
                        res_sts.append(tst)
                        prev["ev"] = [tst]
                    if pbank is ps_a:
                        prev["psA"] = [tev]
                    else:
                        prev["psB"] = [tev]
            if batch_start:
                hdeps = prev["u"] + prev.get("halo", [])
                th = p.op("vector", lambda e: e.memset(xpart[0][:, 0:3], 0.0),
                          deps=hdeps)
                th2 = p.op("vector", lambda e: e.memset(xpart[1][:, 0:3], 0.0),
                          deps=hdeps)
                halo_toks = [th, th2]
            else:
                halo_toks = prev["halo"]
            u_toks = []
            ub_toks = []
            for ct in range(2):
                t_acc = p.op("vector", lambda e, ct=ct: e.tensor_scalar(
                    out=u_f[ct][:], in0=xpart[ct][:, 0:TC],
                    scalar1=convw_sb[:, ct * KC:ct * KC + 1], scalar2=None,
                    op0=ALU.mult),
                    deps=conv_deps + halo_toks + [t_cw] + prev["u"])
                for k in range(1, KC):
                    t_acc = p.op("vector", lambda e, ct=ct, k=k: e.scalar_tensor_tensor(
                        out=u_f[ct][:], in0=xpart[ct][:, k:k + TC],
                        scalar=convw_sb[:, ct * KC + k:ct * KC + k + 1],
                        in1=u_f[ct][:], op0=ALU.mult, op1=ALU.add), deps=[t_acc])
                t_u = p.op("scalar", lambda e, ct=ct: e.activation(
                    out=u_f[ct][:], in_=u_f[ct][:], func=AF.Silu,
                    bias=convb_sb[:, ct:ct + 1]), deps=[t_acc, t_cb])
                t_ub = p.op("vector", lambda e, ct=ct: e.tensor_copy(
                    out=u_bf[ct][:], in_=u_f[ct][:]), deps=[t_u])
                t_us = p.dma("sync", u_out[ct * 128:(ct + 1) * 128,
                                           tc * TC:(tc + 1) * TC], u_f[ct][:],
                             deps=[t_u])
                u_toks += [t_u, t_us]
                ub_toks.append(t_ub)
            nhalo = []
            for ct in range(2):
                nh = p.op("vector", lambda e, ct=ct: e.tensor_copy(
                    out=xpart[ct][:, 0:3], in_=xpart[ct][:, TC:TC + 3]),
                    deps=u_toks + ub_toks)
                nhalo.append(nh)
            prev["halo"] = nhalo
            p_sts = []
            for sub in range(NSUB):
                lh = [xpT_bf[:, k * 96:(k + 1) * 96] for k in range(2)]
                rh = [u_bf[k][:, sub * 512:(sub + 1) * 512] for k in range(2)]
                tm = emit_matmul_accum(p, ps_p[:], lh, rh,
                                       deps=[t_xp] + ub_toks + prev["psP"])
                tpe = p.op("scalar", lambda e: e.copy(out=p_sb[:], in_=ps_p[:]),
                           deps=[tm] + prev["psb"])
                prev["psP"] = [tpe]
                c0 = tc * TC + sub * 512
                s1 = p.dma("sync", pd_out[:, c0:c0 + 512], p_sb[0:64, :], deps=[tpe])
                s2 = p.dma("sync", pB_out[:, c0:c0 + 512], p_sb[64:80, :], deps=[tpe])
                s3 = p.dma("sync", pC_out[:, c0:c0 + 512], p_sb[80:96, :], deps=[tpe])
                prev["psb"] = [s1, s2, s3]
                p_sts += [s1, s2, s3]
            prev["xnc"] = mm_toks[-1:]
            prev["xpart"] = u_toks + nhalo
            prev["u"] = u_toks + p_sts + ub_toks
        p.wait_only("gpsimd", prev["u"] + res_sts + prev["psb"] + prev["ev"])
        p.finalize(block)
    return nc


def host_prep_l2(inputs, xnT_full):
    ipw = inputs["L0_attn_in_proj_w"]
    xpw = inputs["L0_attn_x_proj_w"]
    maps = []
    for c in range(NCORE):
        sl = slice(c * DC, (c + 1) * DC)
        winT = np.ascontiguousarray(
            np.concatenate([ipw[sl], ipw[DI:][sl]], axis=0).T)
        maps.append({
            "xnT_full": xnT_full,
            "winT": winT,
            "convw": np.ascontiguousarray(inputs["L0_attn_conv_w"][sl, 0, :]),
            "convb": np.ascontiguousarray(inputs["L0_attn_conv_b"][sl, None]),
            "xpT": np.ascontiguousarray(xpw[:, sl].T),
        })
    return maps


def run_l2(inputs, xnT_full, **kw):
    nc = _get("l2", build_l2)
    maps = host_prep_l2(inputs, xnT_full)
    return run_bass_kernel_spmd(nc, maps, core_ids=list(range(NCORE)), **kw)


def build_l3():
    nc = bass.Bass("TRN2", target_bir_lowering=False, debug=False)
    pd_all = nc.dram_tensor("pd_all", [NCORE, DTR, T], F32, kind="ExternalInput").ap()
    pB_all = nc.dram_tensor("pB_all", [NCORE, NST, T], F32, kind="ExternalInput").ap()
    pC_all = nc.dram_tensor("pC_all", [NCORE, NST, T], F32, kind="ExternalInput").ap()
    dtT_in = nc.dram_tensor("dtT", [DTR, DC], F32, kind="ExternalInput").ap()
    dtb_in = nc.dram_tensor("dtb", [DC, 1], F32, kind="ExternalInput").ap()
    Alog_in = nc.dram_tensor("Alog", [DC, NST], F32, kind="ExternalInput").ap()
    Dp_in = nc.dram_tensor("Dp", [DC, 1], F32, kind="ExternalInput").ap()
    u_in = nc.dram_tensor("u_sl", [DC, T], F32, kind="ExternalInput").ap()
    resS_in = nc.dram_tensor("resS", [DC, T], F32, kind="ExternalInput").ap()
    yg_out = nc.dram_tensor("ygT", [DC, T], BF16, kind="ExternalOutput").ap()
    pBC_dram = nc.dram_tensor("pBC_scr", [2 * NST, T], F32, kind="Internal").ap()

    TC = 2048
    NSUB = TC // 512
    import contextlib
    with contextlib.ExitStack() as st:
        sb = lambda name, shape, dt: st.enter_context(nc.sbuf_tensor(name, shape, dt))
        ps = lambda name, shape, dt: st.enter_context(nc.psum_tensor(name, shape, dt))
        sems = std_sems(nc, st)
        block = st.enter_context(nc.Block())
        p = Prog(nc, sems)

        consts = sb("consts", [128, 2], F32)
        dtT_bf = sb("dtT_bf", [DTR, DC], BF16)
        dtb_sb = sb("dtb_sb", [128, 2], F32)
        Dp_sb = sb("Dp_sb", [128, 2], F32)
        A_sb = sb("A_sb", [128, 2 * NST], F32)
        state = sb("state", [128, 2 * NST], F32)
        pd_sum = sb("pd_sum", [DTR, TC], F32)
        pB_sum = sb("pB_sum", [NST, TC], F32)
        pC_sum = sb("pC_sum", [NST, TC], F32)
        dr_bf = sb("dr_bf", [DTR, TC], BF16)
        delta = [sb(f"delta{i}", [128, TC], F32) for i in range(2)]
        du = [sb(f"du{i}", [128, TC], F32) for i in range(2)]
        dA = [sb(f"dA{i}", [128, TC], F32) for i in range(2)]
        B_bc = [sb(f"B_bc{j}", [128, TC], F32) for j in range(2)]
        C_bc = [sb(f"C_bc{j}", [128, TC], F32) for j in range(2)]
        zB = [sb(f"zB{i}", [128, TC], F32) for i in range(2)]
        xbuf = [sb(f"xbuf{i}", [128, TC], F32) for i in range(2)]
        yacc = [sb(f"yacc{i}", [128, TC], F32) for i in range(2)]
        u_ch = [sb(f"u_ch{i}", [128, TC], F32) for i in range(2)]
        res_ch = [sb(f"res_ch{i}", [128, TC], F32) for i in range(2)]
        yg_bf = sb("yg_bf", [128, TC], BF16)
        sp_a = sb("sp_a", [128, 512], F32)
        sp_l = sb("sp_l", [128, 512], F32)
        ps_dt = ps("ps_dt", [128, 512], F32)

        tok_c = p.op("vector", lambda e: e.memset(consts[:, 0:1], -1.0))
        t_dtT = p.dma("gpsimd", dtT_bf[:], dtT_in)
        t_dtb = p.dma("sync", dtb_sb[:].rearrange("p (k o) -> p k o", k=2),
                      dtb_in.rearrange("(k p) o -> p k o", p=128),
                      allow_slow_non_contiguous=True)
        t_Dp = p.dma("sync", Dp_sb[:].rearrange("p (k o) -> p k o", k=2),
                     Dp_in.rearrange("(k p) o -> p k o", p=128),
                     allow_slow_non_contiguous=True)
        t_Al = p.dma("sync", A_sb[:].rearrange("p (k o) -> p k o", k=2),
                     Alog_in.rearrange("(k p) o -> p k o", p=128))
        t_A1 = p.op("scalar", lambda e: e.activation(
            out=A_sb[:], in_=A_sb[:], func=AF.Exp), deps=[t_Al])
        t_A = p.op("vector", lambda e: e.tensor_scalar(
            out=A_sb[:], in0=A_sb[:], scalar1=consts[:, 0:1], scalar2=None,
            op0=ALU.mult), deps=[t_A1, tok_c])
        t_st0 = p.op("vector", lambda e: e.memset(state[:], 0.0))

        prev = {"stage": [], "psum": [], "dr": [], "delta": [], "du": [],
                "dA": [[], []], "zB": [[], []], "xb": [[], []], "ya": [],
                "uch": [], "res": [], "yg": [], "st": [t_st0],
                "bcB": [[], []], "bcC": [[], []], "psdt": [],
                "spa": [], "spl": []}
        for tc in range(T // TC):
            c0 = tc * TC
            sums = []
            for (src, dst, rows) in ((pd_all, pd_sum, DTR), (pB_all, pB_sum, NST),
                                     (pC_all, pC_sum, NST)):
                t_l = p.dma("gpsimd", dst[:, :], src[0, :, c0:c0 + TC],
                            deps=prev["psum"])
                for j in range(1, NCORE):
                    t_l = p.dma("gpsimd", dst[:, :], src[j, :, c0:c0 + TC],
                                accum_op=ALU.add, deps=[t_l])
                sums.append(t_l)
            t_pd, t_pB, t_pC = sums
            t_wB = p.dma("sync", pBC_dram[0:NST, c0:c0 + TC], pB_sum[:, :],
                         deps=[t_pB])
            t_wC = p.dma("sync", pBC_dram[NST:2 * NST, c0:c0 + TC], pC_sum[:, :],
                         deps=[t_pC])
            t_dr = p.op("vector", lambda e: e.tensor_copy(
                out=dr_bf[:], in_=pd_sum[:]), deps=[t_pd] + prev["dr"])
            t_u, t_res = [], []
            for ct in range(2):
                t_u.append(p.dma("sync", u_ch[ct][:],
                                 u_in[ct * 128:(ct + 1) * 128, c0:c0 + TC],
                                 deps=prev["uch"]))
                t_res.append(p.dma("sync", res_ch[ct][:],
                                   resS_in[ct * 128:(ct + 1) * 128, c0:c0 + TC],
                                   deps=prev["res"]))
            d_toks = [[], []]
            for ct in range(2):
                for sub in range(NSUB):
                    t_mm = p.op("tensor", lambda e, ct=ct, sub=sub: e.matmul(
                        out=ps_dt[:], lhsT=dtT_bf[:, ct * 128:(ct + 1) * 128],
                        rhs=dr_bf[:, sub * 512:(sub + 1) * 512],
                        start=True, stop=True),
                        deps=[t_dr, t_dtT] + prev["psdt"])
                    t_a = p.op("scalar", lambda e, ct=ct: e.activation(
                        out=sp_a[:], in_=ps_dt[:], func=AF.Abs,
                        bias=dtb_sb[:, ct:ct + 1]), deps=[t_mm, t_dtb] + prev["spa"])
                    t_e = p.op("scalar", lambda e: e.activation(
                        out=sp_a[:], in_=sp_a[:], func=AF.Exp,
                        scale=consts[:, 0:1]), deps=[t_a, tok_c])
                    t_ln = p.op("scalar", lambda e: e.activation(
                        out=sp_l[:], in_=sp_a[:], func=AF.Ln, bias=1.0),
                        deps=[t_e] + prev["spl"])
                    t_r = p.op("scalar", lambda e, ct=ct: e.activation(
                        out=sp_a[:], in_=ps_dt[:], func=AF.Relu,
                        bias=dtb_sb[:, ct:ct + 1]), deps=[t_ln])
                    t_d = p.op("vector", lambda e, ct=ct, sub=sub: e.tensor_add(
                        out=delta[ct][:, sub * 512:(sub + 1) * 512],
                        in0=sp_a[:], in1=sp_l[:]), deps=[t_r] + prev["delta"])
                    prev["psdt"] = [t_d]
                    prev["spa"] = [t_d]
                    prev["spl"] = [t_d]
                    d_toks[ct].append(t_d)
            du_toks = []
            for ct in range(2):
                du_toks.append(p.op("vector", lambda e, ct=ct: e.tensor_mul(
                    out=du[ct][:], in0=delta[ct][:], in1=u_ch[ct][:]),
                    deps=d_toks[ct] + [t_u[ct]] + prev["du"]))
            st_dep = prev["st"]
            if c0 == L:
                t_rs = p.op("vector", lambda e: e.memset(state[:], 0.0),
                            deps=prev["st"])
                st_dep = [t_rs]
            y_toks = [[], []]
            first_n = [True, True]
            for n in range(NST):
                bb, cb = B_bc[n % 2], C_bc[n % 2]
                t_bB = p.dma("gpsimd", bb[:],
                             pBC_dram[n:n + 1, c0:c0 + TC].broadcast_to([128, TC]),
                             deps=[t_wB] + prev["bcB"][n % 2])
                t_bC = p.dma("sync", cb[:],
                             pBC_dram[NST + n:NST + n + 1,
                                      c0:c0 + TC].broadcast_to([128, TC]),
                             deps=[t_wC] + prev["bcC"][n % 2])
                for ct in range(2):
                    da = dA[ct]
                    t_da = p.op("scalar", lambda e, ct=ct, n=n, da=da: e.activation(
                        out=da[:], in_=delta[ct][:], func=AF.Exp,
                        scale=A_sb[:, ct * NST + n:ct * NST + n + 1]),
                        deps=d_toks[ct] + [t_A] + prev["dA"][ct])
                    t_zb = p.op("vector", lambda e, ct=ct, bb=bb: e.tensor_mul(
                        out=zB[ct][:], in0=du[ct][:], in1=bb[:]),
                        deps=[t_bB, du_toks[ct]] + prev["zB"][ct])
                    scol = state[:, ct * NST + n:ct * NST + n + 1]
                    t_scan = p.op("vector", lambda e, ct=ct, sc=scol, da=da: e.tensor_tensor_scan(
                        out=xbuf[ct][:], data0=da[:], data1=zB[ct][:],
                        initial=sc, op0=ALU.mult, op1=ALU.add),
                        deps=[t_zb, t_da] + st_dep + prev["xb"][ct])
                    t_sv = p.op("scalar", lambda e, ct=ct, sc=scol: e.copy(
                        out=sc, in_=xbuf[ct][:, TC - 1:TC]), deps=[t_scan])
                    st_dep = st_dep + [t_sv]
                    prev["st"] = st_dep
                    if first_n[ct]:
                        t_y = p.op("vector", lambda e, ct=ct, cb=cb: e.tensor_mul(
                            out=yacc[ct][:], in0=xbuf[ct][:], in1=cb[:]),
                            deps=[t_bC, t_scan] + prev["ya"])
                    else:
                        t_m = p.op("vector", lambda e, ct=ct, cb=cb: e.tensor_mul(
                            out=zB[ct][:], in0=xbuf[ct][:], in1=cb[:]),
                            deps=[t_bC, t_scan])
                        t_y = p.op("vector", lambda e, ct=ct: e.tensor_add(
                            out=yacc[ct][:], in0=yacc[ct][:], in1=zB[ct][:]),
                            deps=[t_m])
                    y_toks[ct].append(t_y)
                    first_n[ct] = False
                    prev["xb"][ct] = [t_y, t_sv]
                    prev["dA"][ct] = [t_scan]
                    prev["zB"][ct] = [t_y]
                prev["bcB"][n % 2] = [y_toks[0][-1], y_toks[1][-1]]
                prev["bcC"][n % 2] = [y_toks[0][-1], y_toks[1][-1]]
            yg_toks = []
            for ct in range(2):
                t_yd = p.op("vector", lambda e, ct=ct: e.scalar_tensor_tensor(
                    out=yacc[ct][:], in0=u_ch[ct][:], scalar=Dp_sb[:, ct:ct + 1],
                    in1=yacc[ct][:], op0=ALU.mult, op1=ALU.add),
                    deps=y_toks[ct] + [t_Dp])
                t_yg = p.op("vector", lambda e, ct=ct: e.tensor_mul(
                    out=yg_bf[:], in0=yacc[ct][:], in1=res_ch[ct][:]),
                    deps=[t_yd, t_res[ct]] + prev["yg"] + yg_toks[-1:])
                t_sg = p.dma("sync", yg_out[ct * 128:(ct + 1) * 128, c0:c0 + TC],
                             yg_bf[:], deps=[t_yg])
                yg_toks += [t_yg, t_sg]
            prev["psum"] = [t_dr, t_wB, t_wC]
            prev["dr"] = d_toks[0] + d_toks[1]
            prev["delta"] = [t for ct in range(2) for t in prev["dA"][ct]] + du_toks
            prev["du"] = [t for ct in range(2) for t in prev["zB"][ct]] + du_toks
            prev["uch"] = yg_toks + du_toks
            prev["res"] = yg_toks
            prev["ya"] = yg_toks
            prev["yg"] = yg_toks
        p.wait_only("gpsimd", yg_toks + prev["st"])
        p.finalize(block)
    return nc


def host_prep_l3(inputs, pd_all, pB_all, pC_all, u_all, resS_all):
    dtw = inputs["L0_attn_dt_proj_w"]
    maps = []
    for c in range(NCORE):
        sl = slice(c * DC, (c + 1) * DC)
        maps.append({
            "pd_all": pd_all, "pB_all": pB_all, "pC_all": pC_all,
            "dtT": np.ascontiguousarray(dtw[sl, :].T),
            "dtb": np.ascontiguousarray(inputs["L0_attn_dt_proj_b"][sl, None]),
            "Alog": np.ascontiguousarray(inputs["L0_attn_A_log"][sl, :]),
            "Dp": np.ascontiguousarray(inputs["L0_attn_D_param"][sl, None]),
            "u_sl": u_all[sl], "resS": resS_all[sl],
        })
    return maps


def run_l3(inputs, pd_all, pB_all, pC_all, u_all, resS_all, **kw):
    nc = _get("l3", build_l3)
    maps = host_prep_l3(inputs, pd_all, pB_all, pC_all, u_all, resS_all)
    return run_bass_kernel_spmd(nc, maps, core_ids=list(range(NCORE)), **kw)


def build_l4():
    nc = bass.Bass("TRN2", target_bir_lowering=False, debug=False)
    yg_in = nc.dram_tensor("ygT_slab", [DI, TI], BF16, kind="ExternalInput").ap()
    opT_in = nc.dram_tensor("opT", [DI, D], F32, kind="ExternalInput").ap()
    x_in = nc.dram_tensor("x_slab", [TI, D], F32, kind="ExternalInput").ap()
    gT_in = nc.dram_tensor("gT", [D, TI], F32, kind="ExternalInput").ap()
    Wk_in = nc.dram_tensor("Wk", [D, D], F32, kind="ExternalInput").ap()
    bk_in = nc.dram_tensor("bk", [D, 1], F32, kind="ExternalInput").ap()
    Wv_in = nc.dram_tensor("Wv", [D, D], F32, kind="ExternalInput").ap()
    bv_in = nc.dram_tensor("bv", [D, 1], F32, kind="ExternalInput").ap()
    RH_in = nc.dram_tensor("RHblk", [D, NST], F32, kind="ExternalInput").ap()
    ident_in = nc.dram_tensor("ident", [128, 128], F32, kind="ExternalInput").ap()
    x1_out = nc.dram_tensor("x1T", [D, TI], F32, kind="ExternalOutput").ap()
    xn1_out = nc.dram_tensor("xn1T", [D, TI], BF16, kind="ExternalOutput").ap()
    V1_out = nc.dram_tensor("V1T", [D, TI], F32, kind="ExternalOutput").ap()
    valid_out = nc.dram_tensor("valid", [NST, TI], F32, kind="ExternalOutput").ap()

    TC = 512
    import contextlib
    with contextlib.ExitStack() as st:
        sb = lambda name, shape, dt: st.enter_context(nc.sbuf_tensor(name, shape, dt))
        ps = lambda name, shape, dt: st.enter_context(nc.psum_tensor(name, shape, dt))
        sems = std_sems(nc, st)
        block = st.enter_context(nc.Block())
        p = Prog(nc, sems)

        consts = sb("consts", [128, 4], F32)
        ident = sb("ident_sb", [128, 128], F32)
        ones1 = sb("ones1", [1, 128], F32)
        ones_col = sb("ones_col", [128, 1], F32)
        opT_bf = sb("opT_bf", [128, 16 * D], BF16)
        Wk_bf = sb("Wk_bf", [128, 8 * D], BF16)
        Wv_bf = sb("Wv_bf", [128, 8 * D], BF16)
        RH_bf = sb("RH_bf", [128, 8 * NST], BF16)
        bk_sb = sb("bk_sb", [128, 8], F32)
        bv_sb = sb("bv_sb", [128, 8], F32)
        ygc = sb("ygc", [128, 16 * TC], BF16)
        s_sb = sb("s_sb", [128, 8 * TC], F32)
        x_tok = sb("x_tok", [128, D], F32)
        xT_sb = sb("xT_sb", [128, 8 * TC], F32)
        g_sb = sb("g_sb", [128, 8 * TC], F32)
        x1_sb = sb("x1_sb", [128, 8 * TC], F32)
        tmp_t = sb("tmp_t", [128, TC], F32)
        sq_t = [sb(f"sq_t{i}", [128, TC], F32) for i in range(2)]
        strow = sb("strow", [1, 4 * TC], F32)
        xn1_sb = sb("xn1_sb", [128, 8 * TC], BF16)
        K1_sb = sb("K1_sb", [128, 8 * TC], BF16)
        V1_sb = sb("V1_sb", [128, 8 * TC], F32)
        val_sb = sb("val_sb", [NST, TC], F32)
        ps_a = ps("ps_a", [128, TC], F32)
        ps_b = ps("ps_b", [128, TC], F32)
        ps_st1 = ps("ps_st1", [1, TC], F32)
        ps_st2 = ps("ps_st2", [1, TC], F32)
        ps_mu = ps("ps_mu", [128, TC], F32)
        ps_rs = ps("ps_rs", [128, TC], F32)
        ps_dot = ps("ps_dot", [NST, TC], F32)

        p.raw("vector", lambda e: e.memset(consts[:, 0:1], 1.0 / D))
        tok_c = p.op("vector", lambda e: e.memset(consts[:, 1:2], EPS))
        tok_o1 = p.op("vector", lambda e: e.memset(ones1[:], 1.0))
        tok_oc = p.op("vector", lambda e: e.memset(ones_col[:], 1.0))
        t_id = p.dma("sync", ident[:], ident_in)
        t_op = p.dma("gpsimd", opT_bf[:].rearrange("p (k o) -> p k o", k=16),
                     opT_in.rearrange("(k p) o -> p k o", p=128))
        t_wk = p.dma("gpsimd", Wk_bf[:].rearrange("p (k o) -> p k o", k=8),
                     Wk_in.rearrange("(k p) o -> p k o", p=128))
        t_wv = p.dma("gpsimd", Wv_bf[:].rearrange("p (k o) -> p k o", k=8),
                     Wv_in.rearrange("(k p) o -> p k o", p=128))
        t_rh = p.dma("gpsimd", RH_bf[:].rearrange("p (k o) -> p k o", k=8),
                     RH_in.rearrange("(k p) o -> p k o", p=128))
        t_bk = p.dma("sync", bk_sb[:].rearrange("p (k o) -> p k o", k=8),
                     bk_in.rearrange("(k p) o -> p k o", p=128),
                     allow_slow_non_contiguous=True)
        t_bv = p.dma("sync", bv_sb[:].rearrange("p (k o) -> p k o", k=8),
                     bv_in.rearrange("(k p) o -> p k o", p=128),
                     allow_slow_non_contiguous=True)

        prev = {"ygc": [], "sA": [], "sB": [], "s": [], "xtok": [], "xT": [],
                "g": [], "x1": [], "sq": [[], []], "st": [], "mu": [],
                "xn1": [], "K1": [], "V1": [], "val": [], "tmp": [], "dot": []}
        for tc in range(TI // TC):
            c0 = tc * TC
            t_yg = p.dma("sync", ygc[:].rearrange("p (k t) -> p k t", k=16),
                         yg_in.rearrange("(k p) t -> p k t", p=128)[:, :, c0:c0 + TC],
                         deps=prev["ygc"])
            t_g = p.dma("sync", g_sb[:].rearrange("p (k t) -> p k t", k=8),
                        gT_in.rearrange("(k p) t -> p k t", p=128)[:, :, c0:c0 + TC],
                        deps=prev["g"])
            s_toks = []
            for ot in range(8):
                pb = ps_a if ot % 2 == 0 else ps_b
                pdep = prev["sA"] if ot % 2 == 0 else prev["sB"]
                lh = [opT_bf[:, kc * D + ot * 128: kc * D + (ot + 1) * 128]
                      for kc in range(16)]
                rh = [ygc[:, kc * TC:(kc + 1) * TC] for kc in range(16)]
                tm = emit_matmul_accum(p, pb[:], lh, rh, deps=[t_yg, t_op] + pdep)
                tev = p.op("scalar", lambda e, ot=ot, pb=pb: e.copy(
                    out=s_sb[:, ot * TC:(ot + 1) * TC], in_=pb[:]),
                    deps=[tm] + prev["s"])
                if ot % 2 == 0:
                    prev["sA"] = [tev]
                else:
                    prev["sB"] = [tev]
                s_toks.append(tev)
            xT_toks = []
            for tt in range(4):
                t_x = p.dma("sync", x_tok[:],
                            x_in[c0 + tt * 128:c0 + (tt + 1) * 128, :],
                            deps=prev["xtok"])
                for c in range(8):
                    ttr = p.op("tensor", lambda e, c=c: e.transpose(
                        out=ps_mu[:, 0:128], in_=x_tok[:, c * 128:(c + 1) * 128],
                        identity=ident[:]),
                        deps=[t_x, t_id] + prev["mu"] + prev.get("muwar", []))
                    tev = p.op("scalar", lambda e, c=c, tt=tt: e.copy(
                        out=xT_sb[:, c * TC + tt * 128: c * TC + (tt + 1) * 128],
                        in_=ps_mu[:, 0:128]), deps=[ttr] + prev["xT"])
                    prev["mu"] = [tev]
                    xT_toks.append(tev)
                prev["xtok"] = [xT_toks[-1]]
            x1_toks = []
            for c in range(8):
                t1 = p.op("vector", lambda e, c=c: e.tensor_mul(
                    out=tmp_t[:], in0=s_sb[:, c * TC:(c + 1) * TC],
                    in1=g_sb[:, c * TC:(c + 1) * TC]),
                    deps=[s_toks[c], t_g] + prev["tmp"])
                t2 = p.op("vector", lambda e, c=c: e.tensor_add(
                    out=x1_sb[:, c * TC:(c + 1) * TC],
                    in0=xT_sb[:, c * TC:(c + 1) * TC], in1=tmp_t[:]),
                    deps=[t1, xT_toks[tt * 8 + c] if False else xT_toks[-1]] + prev["x1"])
                prev["tmp"] = [t2]
                x1_toks.append(t2)
            t_x1st = p.dma("sync", x1_out[:, c0:c0 + TC].rearrange(
                "(k p) t -> p k t", p=128),
                x1_sb[:].rearrange("p (k t) -> p k t", k=8), deps=x1_toks)
            sq_toks = []
            for c in range(8):
                sq = sq_t[c % 2]
                t_sq = p.op("scalar", lambda e, c=c, sq=sq: e.activation(
                    out=sq[:], in_=x1_sb[:, c * TC:(c + 1) * TC], func=AF.Square),
                    deps=[x1_toks[c]] + prev["sq"][c % 2])
                t_m1 = p.op("tensor", lambda e, c=c: e.matmul(
                    out=ps_st1[:], lhsT=ones_col[:],
                    rhs=x1_sb[:, c * TC:(c + 1) * TC],
                    start=(c == 0), stop=(c == 7)),
                    deps=[x1_toks[c], tok_oc] + (prev["st"] if c == 0 else []))
                t_m2 = p.op("tensor", lambda e, c=c, sq=sq: e.matmul(
                    out=ps_st2[:], lhsT=ones_col[:], rhs=sq[:],
                    start=(c == 0), stop=(c == 7)), deps=[t_sq])
                prev["sq"][c % 2] = [t_m2]
                sq_toks.append(t_m2)
            t_mu = p.op("vector", lambda e: e.tensor_scalar(
                out=strow[:, 2 * TC:3 * TC], in0=ps_st1[:], scalar1=consts[0:1, 0:1],
                scalar2=None, op0=ALU.mult), deps=sq_toks + [tok_c] + prev["st"])
            t_ms = p.op("vector", lambda e: e.tensor_mul(
                out=strow[:, 0:TC], in0=strow[:, 2 * TC:3 * TC],
                in1=strow[:, 2 * TC:3 * TC]), deps=[t_mu])
            t_var = p.op("vector", lambda e: e.tensor_scalar(
                out=strow[:, TC:2 * TC], in0=ps_st2[:], scalar1=consts[0:1, 0:1],
                scalar2=None, op0=ALU.mult), deps=sq_toks + [t_ms])
            t_var2 = p.op("vector", lambda e: e.tensor_sub(
                out=strow[:, TC:2 * TC], in0=strow[:, TC:2 * TC],
                in1=strow[:, 0:TC]), deps=[t_var])
            t_sd = p.op("scalar", lambda e: e.activation(
                out=strow[:, 0:TC], in_=strow[:, TC:2 * TC], func=AF.Sqrt,
                bias=consts[0:1, 1:2]), deps=[t_var2])
            t_rstd = p.op("vector", lambda e: e.reciprocal(
                out=strow[:, 3 * TC:4 * TC], in_=strow[:, 0:TC]), deps=[t_sd])
            prev["st"] = [t_rstd]
            t_rmu = p.op("tensor", lambda e: e.matmul(
                out=ps_mu[:], lhsT=ones1[:], rhs=strow[:, 2 * TC:3 * TC],
                start=True, stop=True), deps=[t_mu, prev["mu"][0]])
            t_rrs = p.op("tensor", lambda e: e.matmul(
                out=ps_rs[:], lhsT=ones1[:], rhs=strow[:, 3 * TC:4 * TC],
                start=True, stop=True), deps=[t_rstd] + prev["xn1"])
            prev["mu"] = []
            xn1_toks = []
            for c in range(8):
                ts = p.op("vector", lambda e, c=c: e.tensor_sub(
                    out=tmp_t[:], in0=x1_sb[:, c * TC:(c + 1) * TC], in1=ps_mu[:]),
                    deps=[t_rmu] + prev["tmp"] + (x1_toks if c == 0 else []))
                tn = p.op("vector", lambda e, c=c: e.tensor_mul(
                    out=xn1_sb[:, c * TC:(c + 1) * TC], in0=tmp_t[:], in1=ps_rs[:]),
                    deps=[ts, t_rrs] + prev["xn1"] + prev.get("xnwar", []))
                prev["tmp"] = [tn]
                xn1_toks.append(tn)
            prev["x1"] = xn1_toks + [t_x1st]
            t_xnst = p.dma("sync", xn1_out[:, c0:c0 + TC].rearrange(
                "(k p) t -> p k t", p=128),
                xn1_sb[:].rearrange("p (k t) -> p k t", k=8), deps=xn1_toks)
            prev["xn1"] = [t_xnst]
            K_toks, V_toks = [], []
            for ot in range(8):
                pb = ps_a if ot % 2 == 0 else ps_b
                pdep = prev["sA"] if ot % 2 == 0 else prev["sB"]
                lh = [Wk_bf[:, kc * D + ot * 128: kc * D + (ot + 1) * 128]
                      for kc in range(8)]
                rh = [xn1_sb[:, kc * TC:(kc + 1) * TC] for kc in range(8)]
                tm = emit_matmul_accum(p, pb[:], lh, rh,
                                       deps=xn1_toks + [t_wk] + pdep)
                tev = p.op("scalar", lambda e, ot=ot, pb=pb: e.activation(
                    out=K1_sb[:, ot * TC:(ot + 1) * TC], in_=pb[:], func=AF.Relu,
                    bias=bk_sb[:, ot:ot + 1]), deps=[tm, t_bk] + prev["K1"])
                if ot % 2 == 0:
                    prev["sA"] = [tev]
                else:
                    prev["sB"] = [tev]
                K_toks.append(tev)
            for ot in range(8):
                pb = ps_a if ot % 2 == 0 else ps_b
                pdep = prev["sA"] if ot % 2 == 0 else prev["sB"]
                lh = [Wv_bf[:, kc * D + ot * 128: kc * D + (ot + 1) * 128]
                      for kc in range(8)]
                rh = [xn1_sb[:, kc * TC:(kc + 1) * TC] for kc in range(8)]
                tm = emit_matmul_accum(p, pb[:], lh, rh,
                                       deps=xn1_toks + [t_wv] + pdep)
                tev = p.op("scalar", lambda e, ot=ot, pb=pb: e.activation(
                    out=V1_sb[:, ot * TC:(ot + 1) * TC], in_=pb[:], func=AF.Relu,
                    bias=bv_sb[:, ot:ot + 1]), deps=[tm, t_bv] + prev["V1"])
                if ot % 2 == 0:
                    prev["sA"] = [tev]
                else:
                    prev["sB"] = [tev]
                V_toks.append(tev)
            t_vst = p.dma("sync", V1_out[:, c0:c0 + TC].rearrange(
                "(k p) t -> p k t", p=128),
                V1_sb[:].rearrange("p (k t) -> p k t", k=8), deps=V_toks)
            prev["V1"] = [t_vst]
            dm = None
            for c in range(8):
                dm = p.op("tensor", lambda e, c=c: e.matmul(
                    out=ps_dot[:], lhsT=RH_bf[:, c * NST:(c + 1) * NST],
                    rhs=K1_sb[:, c * TC:(c + 1) * TC],
                    start=(c == 0), stop=(c == 7)),
                    deps=K_toks + [t_rh] + (prev["dot"] if c == 0 else []))
            t_val = p.op("vector", lambda e: e.tensor_scalar(
                out=val_sb[:], in0=ps_dot[:], scalar1=0.5, scalar2=None,
                op0=ALU.is_gt), deps=[dm] + prev["val"])
            prev["dot"] = [t_val]
            prev["muwar"] = xn1_toks
            prev["xnwar"] = [t_vst, dm, t_xnst]
            t_vlst = p.dma("sync", valid_out[:, c0:c0 + TC], val_sb[:],
                           deps=[t_val])
            prev["val"] = [t_vlst]
            prev["ygc"] = s_toks
            prev["g"] = x1_toks
            prev["s"] = x1_toks
            prev["xT"] = x1_toks
            prev["K1"] = [dm]
        p.wait_only("gpsimd", [t_vlst, t_vst, t_xnst, t_x1st] + prev["sA"] + prev["sB"])
        p.finalize(block)
    return nc


def host_prep_l4(inputs, ygT_full, gT_full, x_full):
    RH = inputs["L1_attn_ReadingHead"]
    RHblk = np.zeros((D, NST), np.float32)
    for h in range(NST):
        c = h // 2
        off = (h % 2) * 64
        RHblk[c * 128 + off: c * 128 + off + 64, h] = RH[h]
    opT = np.ascontiguousarray(inputs["L0_attn_out_proj_w"].T)
    ident = np.eye(128, dtype=np.float32)
    maps = []
    for c in range(NCORE):
        sl = slice(c * TI, (c + 1) * TI)
        maps.append({
            "ygT_slab": np.ascontiguousarray(ygT_full[:, sl]),
            "opT": opT,
            "x_slab": np.ascontiguousarray(x_full[sl]),
            "gT": np.ascontiguousarray(gT_full[:, sl]),
            "Wk": inputs["L1_attn_Wk"], "bk": inputs["L1_attn_bk"][:, None],
            "Wv": inputs["L1_attn_Wv"], "bv": inputs["L1_attn_bv"][:, None],
            "RHblk": RHblk, "ident": ident,
        })
    return maps


def run_l4(inputs, ygT_full, gT_full, x_full, **kw):
    nc = _get("l4", build_l4)
    maps = host_prep_l4(inputs, ygT_full, gT_full, x_full)
    return run_bass_kernel_spmd(nc, maps, core_ids=list(range(NCORE)), **kw)


def build_l5():
    nc = bass.Bass("TRN2", target_bir_lowering=False, debug=False)
    V1p_in = nc.dram_tensor("V1p", [256, L], F32, kind="ExternalInput").ap()
    val_in = [nc.dram_tensor(f"val{i}", [2, L], F32, kind="ExternalInput").ap()
              for i in range(2)]
    wfb_in = [nc.dram_tensor(f"wfb{i}", [128, 8], F32, kind="ExternalInput").ap()
              for i in range(2)]
    repl2_in = nc.dram_tensor("repl2", [2, 128], F32, kind="ExternalInput").ap()
    attn_out = nc.dram_tensor("attn", [256, L], F32, kind="ExternalOutput").ap()

    TCH = min(4096, L // 4)
    NCH = L // TCH
    import contextlib
    with contextlib.ExitStack() as st:
        sb = lambda name, shape, dt: st.enter_context(nc.sbuf_tensor(name, shape, dt))
        ps = lambda name, shape, dt: st.enter_context(nc.psum_tensor(name, shape, dt))
        sems = std_sems(nc, st)
        block = st.enter_context(nc.Block())
        p = Prog(nc, sems)

        repl2 = sb("repl2_sb", [2, 128], F32)
        val_sb = sb("val_sb2", [2, 4096], F32)
        wfb = sb("wfb_sb", [128, 8], F32)
        wbsum = sb("wbsum", [128, 1], F32)
        Vc = sb("Vc", [128, TCH], F32)
        v_s = sb("v_s", [128, TCH], F32)
        vb_s = sb("vb_s", [128, TCH], F32)
        ev = sb("ev", [128, TCH], F32)
        slot = [sb(f"slot{r}", [128, TCH], F32) for r in range(4)]
        acc = sb("acc", [128, TCH], F32)
        ftail = [sb(f"ftail{i}", [128, 4], F32) for i in range(2)]
        snap = sb("snap", [128, 8], F32)
        ps_v = ps("ps_v", [128, 512], F32)

        t_r2 = p.dma("sync", repl2[:], repl2_in)

        def replicate_v(val_rows, c0, deps, fix0, fixL):
            toks = []
            pv = None
            for sub in range(TCH // 512):
                pv_dep = [] if pv is None else [pv]
                tm = p.op("tensor", lambda e, sub=sub: e.matmul(
                    out=ps_v[:], lhsT=repl2[:],
                    rhs=val_rows[:, c0 + sub * 512:c0 + (sub + 1) * 512],
                    start=True, stop=True), deps=deps + [t_r2] + pv_dep)
                t1 = p.op("scalar", lambda e, sub=sub: e.copy(
                    out=v_s[:, sub * 512:(sub + 1) * 512], in_=ps_v[:]), deps=[tm])
                t2 = p.op("scalar", lambda e, sub=sub: e.activation(
                    out=vb_s[:, sub * 512:(sub + 1) * 512], in_=ps_v[:],
                    func=AF.Identity, bias=1.0, scale=-1.0), deps=[tm])
                pv = t2
                toks += [t1, t2]
            fixes = []
            if fixL:
                tsn = p.op("scalar", lambda e: e.copy(
                    out=snap[:, 2:3], in_=v_s[:, TCH - 1:TCH]), deps=toks)
                f1 = p.op("vector", lambda e: e.memset(v_s[:, TCH - 1:TCH], 0.0),
                          deps=[tsn])
                f2 = p.op("vector", lambda e: e.memset(vb_s[:, TCH - 1:TCH], 1.0),
                          deps=toks)
                fixes += [f1, f2, tsn]
            if fix0:
                f1 = p.op("vector", lambda e: e.memset(v_s[:, 0:1], 0.0), deps=toks)
                f2 = p.op("vector", lambda e: e.memset(vb_s[:, 0:1], 1.0), deps=toks)
                fixes += [f1, f2]
            return toks + fixes

        last_toks = []
        for tile in range(2):
            row0 = tile * 128
            t_w = p.dma("sync", wfb[:], wfb_in[tile], deps=last_toks)
            t_ws = p.op("vector", lambda e: e.tensor_reduce(
                out=wbsum[:], in_=wfb[:, 4:8], axis=AX.X, op=ALU.add),
                deps=[t_w])
            t_bv0 = p.dma("sync", snap[:, 0:1], V1p_in[row0:row0 + 128, 0:1],
                          deps=last_toks, allow_slow_non_contiguous=True)
            ft_cur = 0
            t_ftin = p.op("vector", lambda e: e.memset(ftail[0][:], 0.0),
                          deps=last_toks)
            prev_ft = [t_ftin]
            prev_V = last_toks
            prev_sl = [[], [], [], []]
            prev_acc = []
            prev_ev = []
            st_toks = []
            for ch in range(NCH):
                c0 = ch * TCH
                t_V = p.dma("sync", Vc[:], V1p_in[row0:row0 + 128, c0:c0 + TCH],
                            deps=prev_V)
                t_val = p.dma("sync", val_sb[:, 0:TCH],
                              val_in[tile][:, c0:c0 + TCH],
                              deps=prev_ev + prev_sl[0])
                vt = replicate_v(val_sb, 0, deps=[t_val] + prev_ev + prev_sl[0],
                                 fix0=(ch == 0), fixL=False)
                sl_toks = []
                for r in range(4):
                    if r == 0:
                        t_ev = p.op("vector", lambda e: e.tensor_mul(
                            out=ev[:], in0=v_s[:], in1=Vc[:]),
                            deps=vt + [t_V] + prev_ev)
                    else:
                        t_e1 = p.op("vector", lambda e, r=r: e.tensor_mul(
                            out=ev[:, 1:TCH], in0=v_s[:, 1:TCH],
                            in1=slot[r - 1][:, 0:TCH - 1]),
                            deps=vt + [sl_toks[r - 1]] + prev_ev)
                        t_ev = p.op("vector", lambda e, r=r, ft=ft_cur: e.tensor_mul(
                            out=ev[:, 0:1], in0=v_s[:, 0:1],
                            in1=ftail[ft][:, r - 1:r]), deps=[t_e1] + prev_ft)
                    init = (snap[:, 0:1] if ch == 0 else ftail[ft_cur][:, r:r + 1])
                    ideps = [t_bv0] if ch == 0 else prev_ft
                    t_sc = p.op("vector", lambda e, r=r, init=init: e.tensor_tensor_scan(
                        out=slot[r][:], data0=vb_s[:], data1=ev[:], initial=init,
                        op0=ALU.mult, op1=ALU.add),
                        deps=[t_ev] + ideps + prev_sl[r])
                    sl_toks.append(t_sc)
                    if r == 0:
                        t_ac = p.op("vector", lambda e: e.tensor_scalar(
                            out=acc[:], in0=slot[0][:], scalar1=wfb[:, 0:1],
                            scalar2=None, op0=ALU.mult),
                            deps=[t_sc, t_ws] + prev_acc)
                    else:
                        t_ac = p.op("vector", lambda e, r=r: e.scalar_tensor_tensor(
                            out=acc[:], in0=slot[r][:], scalar=wfb[:, r:r + 1],
                            in1=acc[:], op0=ALU.mult, op1=ALU.add),
                            deps=[t_sc, t_ac])
                ft_nxt = 1 - ft_cur
                nt = []
                for r in range(4):
                    nt.append(p.op("scalar", lambda e, r=r, fn=ft_nxt: e.copy(
                        out=ftail[fn][:, r:r + 1], in_=slot[r][:, TCH - 1:TCH]),
                        deps=[sl_toks[r]] + prev_ft))
                t_st = p.dma("sync", attn_out[row0:row0 + 128, c0:c0 + TCH],
                             acc[:], deps=[t_ac])
                st_toks.append(t_st)
                prev_ft = nt
                ft_cur = ft_nxt
                prev_V = sl_toks + [t_ev if False else sl_toks[0]]
                prev_V = sl_toks
                prev_sl = [[t_ac] for _ in range(4)]
                prev_acc = [t_st]
                prev_ev = sl_toks
            t_ftin = p.op("vector", lambda e: e.memset(ftail[ft_cur][:], 0.0),
                          deps=prev_ft + [t_ac])
            prev_ft = [t_ftin]
            prev_bh = []
            for ch in range(NCH - 1, -1, -1):
                c0 = ch * TCH
                t_V = p.dma("sync", Vc[:], V1p_in[row0:row0 + 128, c0:c0 + TCH],
                            deps=prev_V)
                t_val = p.dma("sync", val_sb[:, 0:TCH],
                              val_in[tile][:, c0:c0 + TCH],
                              deps=prev_ev + prev_sl[0])
                vt = replicate_v(val_sb, 0, deps=[t_val] + prev_ev + prev_sl[0],
                                 fix0=(ch == 0), fixL=(ch == NCH - 1))
                t_la = p.dma("sync", acc[:], attn_out[row0:row0 + 128, c0:c0 + TCH],
                             deps=prev_acc + st_toks)
                acc_extra = []
                if ch == 0:
                    t_fa = p.op("scalar", lambda e: e.copy(
                        out=snap[:, 3:4], in_=acc[:, 0:1]), deps=[t_la])
                    acc_extra = [t_fa]
                if ch == NCH - 1:
                    t_b1 = p.op("vector", lambda e: e.tensor_mul(
                        out=snap[:, 4:5], in0=snap[:, 2:3], in1=Vc[:, TCH - 1:TCH]),
                        deps=vt + [t_V])
                    t_b2 = p.op("vector", lambda e: e.tensor_scalar(
                        out=snap[:, 5:6], in0=snap[:, 2:3], scalar1=-1.0,
                        scalar2=None, op0=ALU.mult), deps=vt)
                    t_b3 = p.op("vector", lambda e: e.tensor_scalar(
                        out=snap[:, 5:6], in0=snap[:, 5:6], scalar1=1.0,
                        scalar2=None, op0=ALU.add), deps=[t_b2])
                    t_b4 = p.op("vector", lambda e: e.tensor_mul(
                        out=snap[:, 5:6], in0=snap[:, 5:6], in1=snap[:, 0:1]),
                        deps=[t_b3, t_bv0])
                    t_binit = p.op("vector", lambda e: e.tensor_add(
                        out=snap[:, 4:5], in0=snap[:, 4:5], in1=snap[:, 5:6]),
                        deps=[t_b1, t_b4])
                sl_toks = []
                for r in range(4):
                    if r == 0:
                        t_e1 = p.op("vector", lambda e: e.tensor_mul(
                            out=ev[:, 0:TCH - 1], in0=v_s[:, 0:TCH - 1],
                            in1=Vc[:, 1:TCH]), deps=vt + [t_V] + prev_ev)
                        if ch == NCH - 1:
                            t_ev = p.op("vector", lambda e: e.memset(
                                ev[:, TCH - 1:TCH], 0.0), deps=[t_e1])
                        else:
                            t_ev = p.op("vector", lambda e: e.tensor_mul(
                                out=ev[:, TCH - 1:TCH], in0=v_s[:, TCH - 1:TCH],
                                in1=snap[:, 1:2]), deps=[t_e1] + prev_bh)
                        init = snap[:, 4:5] if ch == NCH - 1 else ftail[ft_cur][:, 0:1]
                        ideps = ([t_binit] if ch == NCH - 1 else prev_ft)
                        t_sc = p.op("vector", lambda e, init=init: e.tensor_tensor_scan(
                            out=slot[0][:], data0=vb_s[:, ::-1], data1=ev[:, ::-1],
                            initial=init, op0=ALU.mult, op1=ALU.add),
                            deps=[t_ev] + ideps + prev_sl[0])
                    else:
                        t_e1 = p.op("vector", lambda e, r=r: e.tensor_mul(
                            out=ev[:, 1:TCH], in0=v_s[:, TCH - 2::-1],
                            in1=slot[r - 1][:, 0:TCH - 1]),
                            deps=vt + [sl_toks[r - 1]] + prev_ev)
                        t_ev = p.op("vector", lambda e, r=r, ft=ft_cur: e.tensor_mul(
                            out=ev[:, 0:1], in0=v_s[:, TCH - 1:TCH],
                            in1=ftail[ft][:, r - 1:r]), deps=[t_e1] + prev_ft)
                        init = (snap[:, 0:1] if ch == NCH - 1
                                else ftail[ft_cur][:, r:r + 1])
                        ideps = [t_bv0] if ch == NCH - 1 else prev_ft
                        t_sc = p.op("vector", lambda e, r=r, init=init: e.tensor_tensor_scan(
                            out=slot[r][:], data0=vb_s[:, ::-1], data1=ev[:],
                            initial=init, op0=ALU.mult, op1=ALU.add),
                            deps=[t_ev] + ideps + prev_sl[r])
                    sl_toks.append(t_sc)
                    t_ac = p.op("vector", lambda e, r=r: e.scalar_tensor_tensor(
                        out=acc[:], in0=slot[r][:, ::-1], scalar=wfb[:, 4 + r:5 + r],
                        in1=acc[:], op0=ALU.mult, op1=ALU.add),
                        deps=[t_sc, t_la] + acc_extra
                        + ([] if r == 0 else [t_ac]) + prev_acc)
                if ch == 0:
                    t_ac = p.op("vector", lambda e: e.scalar_tensor_tensor(
                        out=acc[:, 0:1], in0=snap[:, 0:1], scalar=wbsum[:, 0:1],
                        in1=snap[:, 3:4], op0=ALU.mult, op1=ALU.add),
                        deps=[t_ac, t_fa, t_ws, t_bv0])
                ft_nxt = 1 - ft_cur
                nt = []
                for r in range(4):
                    nt.append(p.op("scalar", lambda e, r=r, fn=ft_nxt: e.copy(
                        out=ftail[fn][:, r:r + 1], in_=slot[r][:, TCH - 1:TCH]),
                        deps=[sl_toks[r]] + prev_ft))
                t_bh = p.op("scalar", lambda e: e.copy(
                    out=snap[:, 1:2], in_=Vc[:, 0:1]), deps=[t_V] + prev_bh)
                prev_bh = [t_bh]
                t_st = p.dma("sync", attn_out[row0:row0 + 128, c0:c0 + TCH],
                             acc[:], deps=[t_ac])
                prev_ft = nt
                ft_cur = ft_nxt
                prev_V = sl_toks + [t_bh]
                prev_sl = [[t_ac] for _ in range(4)]
                prev_acc = [t_st]
                prev_ev = sl_toks
            last_toks = [t_st] + nt + prev_bh
        p.wait_only("gpsimd", last_toks)
        p.finalize(block)
    return nc


def host_prep_l5(inputs, V1_full, valid_full):
    bw = inputs["L1_attn_bidirection_weight"]
    repl2 = np.zeros((2, 128), np.float32)
    repl2[0, 0:64] = 1.0
    repl2[1, 64:128] = 1.0
    V1r = V1_full.reshape(16, 64, B, L)
    valr = valid_full.reshape(16, B, L)
    maps = []
    for c in range(NCORE):
        pairs = [(pid // 16, pid % 16) for pid in range(c * 4, c * 4 + 4)]
        V1p = np.ascontiguousarray(
            np.concatenate([V1r[h, :, b, :] for (b, h) in pairs], axis=0))
        m = {"V1p": V1p, "repl2": repl2}
        for i in range(2):
            pr = pairs[2 * i:2 * i + 2]
            m[f"val{i}"] = np.ascontiguousarray(
                np.stack([valr[h, b, :] for (b, h) in pr], axis=0))
            wfb = np.zeros((128, 8), np.float32)
            for j, (b, h) in enumerate(pr):
                wfb[j * 64:(j + 1) * 64, :] = bw[h][None, :]
            m[f"wfb{i}"] = wfb
        maps.append(m)
    return maps


def run_l5(inputs, V1_full, valid_full, **kw):
    nc = _get("l5", build_l5)
    maps = host_prep_l5(inputs, V1_full, valid_full)
    return run_bass_kernel_spmd(nc, maps, core_ids=list(range(NCORE)), **kw)


def build_l6():
    nc = bass.Bass("TRN2", target_bir_lowering=False, debug=False)
    xn1_in = nc.dram_tensor("xn1T", [D, TI], BF16, kind="ExternalInput").ap()
    x1_in = nc.dram_tensor("x1T", [D, TI], F32, kind="ExternalInput").ap()
    s1_in = nc.dram_tensor("s1T", [D, TI], F32, kind="ExternalInput").ap()
    wrT_in = nc.dram_tensor("wrT", [D, NST], F32, kind="ExternalInput").ap()
    wmhT_in = nc.dram_tensor("wmhT", [D, D], F32, kind="ExternalInput").ap()
    wmgT_in = nc.dram_tensor("wmgT", [D, D], F32, kind="ExternalInput").ap()
    kT_in = nc.dram_tensor("kT", [MOE_H, MOE_E, MOE_D, MOE_S], F32,
                           kind="ExternalInput").ap()
    vf_in = nc.dram_tensor("vf", [MOE_H, MOE_E, MOE_S, MOE_D], F32,
                           kind="ExternalInput").ap()
    ident_in = nc.dram_tensor("ident", [128, 128], F32, kind="ExternalInput").ap()
    out_o = nc.dram_tensor("out_slab", [TI, D], F32, kind="ExternalOutput").ap()
    w_dram = nc.dram_tensor("w_scr", [NST, TI], F32, kind="Internal").ap()

    TCK = 2048
    NSUB = TCK // 512
    import contextlib
    with contextlib.ExitStack() as st:
        sb = lambda name, shape, dt: st.enter_context(nc.sbuf_tensor(name, shape, dt))
        ps = lambda name, shape, dt: st.enter_context(nc.psum_tensor(name, shape, dt))
        sems = std_sems(nc, st)
        block = st.enter_context(nc.Block())
        p = Prog(nc, sems)

        ident = sb("ident_sb", [128, 128], F32)
        ones16 = sb("ones16", [128, NST], F32)
        wrT_bf = sb("wrT_bf", [128, 8 * NST], BF16)
        wmh_bf = sb("wmh_bf", [128, 8 * D], BF16)
        wmg_bf = sb("wmg_bf", [128, 8 * D], BF16)
        xn1_ch = sb("xn1_ch", [128, 8 * TCK], BF16)
        w_sb = sb("w_sb", [NST, TCK], F32)
        lt = sb("lt", [128, NST], F32)
        mx = sb("mx", [128, 8], F32)
        msk = sb("msk", [128, 6 * NST], F32)
        sgc = sb("sgc", [128, 4], F32)
        wtok = sb("wtok", [128, NST], F32)
        k_bf = [sb(f"k_bf{i}", [128, MOE_E * MOE_S], BF16) for i in range(2)]
        v_bf = [sb(f"v_bf{i}", [128, MOE_E * 2 * MOE_D], BF16) for i in range(2)]
        q_bf = sb("q_bf", [128, 512], BF16)
        wrepl = sb("wrepl", [128, NST * 512], F32)
        ymoe = sb("ymoe", [128, 8 * 512], BF16)
        gel = [sb(f"gel{i}", [128, 512], F32) for i in range(3)]
        wm_bf = [sb(f"wm_bf{i}", [128, 512], BF16) for i in range(3)]
        x1_sb = sb("x1_sb", [128, 8 * 512], F32)
        s1_sb = sb("s1_sb", [128, 8 * 512], F32)
        g1_sb = sb("g1_sb", [128, 512], F32)
        o_sb = sb("o_sb", [128, 512], F32)
        otile = [sb(f"otile{i}", [128, 128], F32) for i in range(2)]
        ps_lg = ps("ps_lg", [128, NST], F32)
        ps_tr = ps("ps_tr", [NST, 128], F32)
        ps_q = ps("ps_q", [128, 512], F32)
        ps_sc = [ps(f"ps_sc{i}", [128, 512], F32) for i in range(3)]
        ps_y = ps("ps_y", [128, 512], F32)
        ps_tr2 = ps("ps_tr2", [128, 128], F32)

        t_id = p.dma("sync", ident[:], ident_in)
        tok_o = p.op("vector", lambda e: e.memset(ones16[:], 1.0))
        t_wr = p.dma("gpsimd", wrT_bf[:].rearrange("p (k o) -> p k o", k=8),
                     wrT_in.rearrange("(k p) o -> p k o", p=128))
        t_mh = p.dma("gpsimd", wmh_bf[:].rearrange("p (k o) -> p k o", k=8),
                     wmhT_in.rearrange("(k p) o -> p k o", p=128))
        t_mg = p.dma("gpsimd", wmg_bf[:].rearrange("p (k o) -> p k o", k=8),
                     wmgT_in.rearrange("(k p) o -> p k o", p=128))

        prev = {"xn1": [], "lt": [], "wsb": [], "tr": [], "kv": [[], []],
                "q": [], "wrepl": [], "ymoe": [], "sc": [], "gel": [[], [], []],
                "wm": [[], [], []], "y2": [[], [], []], "y": [], "x1": [], "s1": [], "g1": [], "o": [], "ot": [],
                "lg": [], "st": [None, None], "msk": [], "mx": [], "wtok": [],
                "sg": []}
        kv_i = 0
        for ck in range(TI // TCK):
            c0 = ck * TCK
            t_xc = p.dma("sync", xn1_ch[:].rearrange("p (k t) -> p k t", k=8),
                         xn1_in.rearrange("(k p) t -> p k t", p=128)[
                             :, :, c0:c0 + TCK], deps=prev["xn1"])
            wsb_toks = []
            for tt in range(TCK // 128):
                dm = None
                for kc in range(8):
                    dm = p.op("tensor", lambda e, kc=kc, tt=tt: e.matmul(
                        out=ps_lg[:], lhsT=xn1_ch[:, kc * TCK + tt * 128:
                                                  kc * TCK + (tt + 1) * 128],
                        rhs=wrT_bf[:, kc * NST:(kc + 1) * NST],
                        start=(kc == 0), stop=(kc == 7)),
                        deps=[t_xc, t_wr] + (prev["lg"] if kc == 0 else []))
                t_lt = p.op("scalar", lambda e: e.copy(out=lt[:], in_=ps_lg[:]),
                            deps=[dm] + prev["lt"])
                prev["lg"] = [t_lt]
                t_mx = p.op("vector", lambda e: e.max(out=mx[:], in_=lt[:]),
                            deps=[t_lt] + prev["mx"])
                t_mA = p.op("vector", lambda e: e.tensor_scalar(
                    out=msk[:, 0:NST], in0=lt[:], scalar1=mx[:, 0:1], scalar2=None,
                    op0=ALU.is_equal), deps=[t_mx] + prev["msk"])
                t_cA = p.op("vector", lambda e: e.tensor_tensor_scan(
                    out=msk[:, NST:2 * NST], data0=ones16[:], data1=msk[:, 0:NST],
                    initial=0.0, op0=ALU.mult, op1=ALU.add), deps=[t_mA, tok_o])
                t_e1 = p.op("vector", lambda e: e.tensor_scalar(
                    out=msk[:, NST:2 * NST], in0=msk[:, NST:2 * NST], scalar1=1.0,
                    scalar2=None, op0=ALU.is_equal), deps=[t_cA])
                t_m1 = p.op("vector", lambda e: e.tensor_mul(
                    out=msk[:, 0:NST], in0=msk[:, 0:NST], in1=msk[:, NST:2 * NST]),
                    deps=[t_e1])
                t_mB = p.op("vector", lambda e: e.tensor_scalar(
                    out=msk[:, 2 * NST:3 * NST], in0=lt[:], scalar1=mx[:, 1:2],
                    scalar2=None, op0=ALU.is_equal), deps=[t_mx, t_lt])
                t_cB = p.op("vector", lambda e: e.tensor_tensor_scan(
                    out=msk[:, 3 * NST:4 * NST], data0=ones16[:],
                    data1=msk[:, 2 * NST:3 * NST],
                    initial=0.0, op0=ALU.mult, op1=ALU.add), deps=[t_mB])
                t_eq = p.op("vector", lambda e: e.tensor_tensor(
                    out=sgc[:, 2:3], in0=mx[:, 0:1], in1=mx[:, 1:2],
                    op=ALU.is_equal), deps=[t_mx] + prev["sg"])
                t_tg = p.op("vector", lambda e: e.tensor_scalar(
                    out=sgc[:, 2:3], in0=sgc[:, 2:3], scalar1=1.0, scalar2=None,
                    op0=ALU.add), deps=[t_eq])
                t_e2 = p.op("vector", lambda e: e.tensor_scalar(
                    out=msk[:, 3 * NST:4 * NST], in0=msk[:, 3 * NST:4 * NST],
                    scalar1=sgc[:, 2:3], scalar2=None, op0=ALU.is_equal),
                    deps=[t_cB, t_tg])
                t_m2 = p.op("vector", lambda e: e.tensor_mul(
                    out=msk[:, 2 * NST:3 * NST], in0=msk[:, 2 * NST:3 * NST],
                    in1=msk[:, 3 * NST:4 * NST]), deps=[t_e2])
                t_sg = p.op("scalar", lambda e: e.activation(
                    out=sgc[:, 0:2], in_=mx[:, 0:2], func=AF.Sigmoid),
                    deps=[t_mx] + prev["sg"])
                t_w1 = p.op("vector", lambda e: e.tensor_scalar(
                    out=wtok[:], in0=msk[:, 0:NST], scalar1=sgc[:, 0:1],
                    scalar2=None, op0=ALU.mult), deps=[t_m1, t_sg] + prev["wtok"])
                t_wt = p.op("vector", lambda e: e.scalar_tensor_tensor(
                    out=wtok[:], in0=msk[:, 2 * NST:3 * NST], scalar=sgc[:, 1:2],
                    in1=wtok[:], op0=ALU.mult, op1=ALU.add), deps=[t_m2, t_w1])
                t_tr = p.op("tensor", lambda e: e.transpose(
                    out=ps_tr[:], in_=wtok[:], identity=ident[:]),
                    deps=[t_wt, t_id] + prev["tr"])
                t_we = p.op("scalar", lambda e, tt=tt: e.copy(
                    out=w_sb[:, tt * 128:(tt + 1) * 128], in_=ps_tr[:]),
                    deps=[t_tr] + prev["wsb"])
                prev["tr"] = [t_we]
                prev["msk"] = [t_wt]
                prev["mx"] = [t_w1]
                prev["sg"] = [t_wt]
                prev["wtok"] = [t_we]
                prev["lt"] = [t_mB]
                wsb_toks.append(t_we)
            t_wd = p.dma("sync", w_dram[:, c0:c0 + TCK], w_sb[:], deps=wsb_toks)
            st_toks = []
            for sub in range(NSUB):
                s0 = sub * 512
                t_x1 = p.dma("sync", x1_sb[:].rearrange("p (k t) -> p k t", k=8),
                             x1_in.rearrange("(k p) t -> p k t", p=128)[
                                 :, :, c0 + s0:c0 + s0 + 512], deps=prev["x1"])
                t_s1 = p.dma("sync", s1_sb[:].rearrange("p (k t) -> p k t", k=8),
                             s1_in.rearrange("(k p) t -> p k t", p=128)[
                                 :, :, c0 + s0:c0 + s0 + 512], deps=prev["s1"])
                wr_toks = []
                for en in range(NST):
                    t_ew = p.dma("sync", wrepl[:, en * 512:(en + 1) * 512],
                                 w_dram[en:en + 1, c0 + s0:c0 + s0 + 512
                                        ].broadcast_to([128, 512]),
                                 deps=[t_wd] + prev["wrepl"])
                    prev["wrepl"] = [t_ew]
                    wr_toks.append(t_ew)
                y_toks = []
                for h in range(MOE_H):
                    kv = kv_i % 2
                    kv_i += 1
                    t_k = p.dma("gpsimd",
                                k_bf[kv][:].rearrange("p (e s) -> p e s", e=MOE_E),
                                kT_in[h].transpose([1, 0, 2]),
                                deps=prev["kv"][kv])
                    t_v = p.dma("gpsimd",
                                v_bf[kv][:].rearrange("p (e c d) -> p e c d",
                                                      e=MOE_E, c=2),
                                vf_in[h].rearrange("e (c sp) d -> sp e c d", sp=128),
                                deps=prev["kv"][kv])
                    qm = None
                    for kc in range(8):
                        qm = p.op("tensor", lambda e, kc=kc, h=h, s0=s0: e.matmul(
                            out=ps_q[:], lhsT=wmh_bf[:, kc * D + h * 128:
                                                     kc * D + (h + 1) * 128],
                            rhs=xn1_ch[:, kc * TCK + s0:kc * TCK + s0 + 512],
                            start=(kc == 0), stop=(kc == 7)),
                            deps=[t_xc, t_mh] + (prev["q"] if kc == 0 else []))
                    t_q = p.op("scalar", lambda e: e.copy(out=q_bf[:], in_=ps_q[:]),
                               deps=[qm] + prev["sc"])
                    prev["q"] = [t_q]
                    def emit_score(ny):
                        en, sc = ny // 2, ny % 2
                        pp = ny % 3
                        return p.op("tensor", lambda e, en=en, sc=sc, kv=kv, pp=pp: e.matmul(
                            out=ps_sc[pp][:],
                            lhsT=k_bf[kv][:, en * MOE_S + sc * 128:
                                          en * MOE_S + (sc + 1) * 128],
                            rhs=q_bf[:], start=True, stop=True),
                            deps=[t_q, t_k] + prev["gel"][pp])
                    t_s_next = emit_score(0)
                    for ny in range(32):
                        en, sc, pp = ny // 2, ny % 2, ny % 3
                        t_s = t_s_next
                        t_g = p.op("scalar", lambda e, pp=pp: e.activation(
                            out=gel[pp][:], in_=ps_sc[pp][:], func=AF.Gelu),
                            deps=[t_s] + prev["wm"][pp])
                        prev["gel"][pp] = [t_g]
                        t_wm = p.op("vector", lambda e, en=en, pp=pp: e.tensor_mul(
                            out=wm_bf[pp][:], in0=gel[pp][:],
                            in1=wrepl[:, en * 512:(en + 1) * 512]),
                            deps=[t_g, wr_toks[en]] + prev["y2"][pp])
                        prev["wm"][pp] = [t_wm]
                        if ny < 31:
                            t_s_next = emit_score(ny + 1)
                        t_y = p.op("tensor", lambda e, en=en, sc=sc, kv=kv, pp=pp,
                                   first=(ny == 0), last=(ny == 31): e.matmul(
                            out=ps_y[:],
                            lhsT=v_bf[kv][:, (en * 2 + sc) * MOE_D:
                                          (en * 2 + sc + 1) * MOE_D],
                            rhs=wm_bf[pp][:], start=first, stop=last),
                            deps=[t_wm, t_v] + (prev["ymoe"] if ny == 0 else []))
                        prev["y2"][pp] = [t_y]
                        prev["y"] = [t_y]
                    t_ye = p.op("scalar", lambda e, h=h: e.copy(
                        out=ymoe[:, h * 512:(h + 1) * 512], in_=ps_y[:]),
                        deps=[t_y] + prev["ymoe"])
                    prev["ymoe"] = [t_ye]
                    prev["kv"][kv] = [t_y, t_s]
                    prev["sc"] = [t_y]
                    y_toks.append(t_ye)
                tr_toks = []
                for oc in range(8):
                    gm = None
                    for hc in range(8):
                        gm = p.op("tensor", lambda e, hc=hc, oc=oc: e.matmul(
                            out=ps_q[:], lhsT=wmg_bf[:, hc * D + oc * 128:
                                                     hc * D + (oc + 1) * 128],
                            rhs=ymoe[:, hc * 512:(hc + 1) * 512],
                            start=(hc == 0), stop=(hc == 7)),
                            deps=y_toks + [t_mg] + prev["q"])
                    t_g1 = p.op("scalar", lambda e: e.activation(
                        out=g1_sb[:], in_=ps_q[:], func=AF.Silu),
                        deps=[gm] + prev["g1"])
                    prev["q"] = [t_g1]
                    t_o1 = p.op("vector", lambda e, oc=oc: e.tensor_mul(
                        out=o_sb[:], in0=s1_sb[:, oc * 512:(oc + 1) * 512],
                        in1=g1_sb[:]), deps=[t_g1, t_s1] + prev["o"])
                    t_o2 = p.op("vector", lambda e, oc=oc: e.tensor_add(
                        out=o_sb[:], in0=o_sb[:],
                        in1=x1_sb[:, oc * 512:(oc + 1) * 512]), deps=[t_o1, t_x1])
                    prev["g1"] = [t_o1]
                    for tt in range(4):
                        ob = otile[tt % 2]
                        t_t = p.op("tensor", lambda e, tt=tt: e.transpose(
                            out=ps_tr2[:], in_=o_sb[:, tt * 128:(tt + 1) * 128],
                            identity=ident[:]),
                            deps=[t_o2, t_id] + prev["ot"])
                        t_te = p.op("scalar", lambda e, ob=ob: e.copy(
                            out=ob[:], in_=ps_tr2[:]),
                            deps=[t_t] + prev["st"][tt % 2:tt % 2 + 1])
                        prev["ot"] = [t_te]
                        r0 = c0 + s0 + tt * 128
                        t_sto = p.dma("sync",
                                      out_o[r0:r0 + 128, oc * 128:(oc + 1) * 128],
                                      ob[:], deps=[t_te])
                        prev["st"][tt % 2] = t_sto
                        tr_toks.append(t_sto)
                    prev["o"] = [tr_toks[-1]]
                sts = tr_toks[-8:]
                st_toks += sts
                prev["x1"] = [t_o2]
                prev["s1"] = [t_o1]
                prev["ymoe"] = prev["q"]
                prev["wrepl"] = prev["wm"][0] + prev["wm"][1] + prev["wm"][2]
            prev["xn1"] = [qm] + wsb_toks
            prev["wsb"] = prev["wrepl"]
        p.wait_only("gpsimd", st_toks)
        p.finalize(block)
    return nc


def host_prep_l6(inputs, xn1_full, x1_full, s1_full):
    kT = np.ascontiguousarray(np.transpose(inputs["L1_ffn_k_ffwd"], (0, 1, 3, 2)))
    ident = np.eye(128, dtype=np.float32)
    maps = []
    for c in range(NCORE):
        sl = slice(c * TI, (c + 1) * TI)
        maps.append({
            "xn1T": np.ascontiguousarray(xn1_full[:, sl]),
            "x1T": np.ascontiguousarray(x1_full[:, sl]),
            "s1T": np.ascontiguousarray(s1_full[:, sl]),
            "wrT": np.ascontiguousarray(inputs["L1_ffn_w_router"].T),
            "wmhT": np.ascontiguousarray(inputs["L1_ffn_w_mh"].T),
            "wmgT": np.ascontiguousarray(inputs["L1_ffn_w_mg"].T),
            "kT": kT, "vf": inputs["L1_ffn_v_ffwd"],
            "ident": ident,
        })
    return maps


def run_l6(inputs, xn1_full, x1_full, s1_full, **kw):
    nc = _get("l6", build_l6)
    maps = host_prep_l6(inputs, xn1_full, x1_full, s1_full)
    return run_bass_kernel_spmd(nc, maps, core_ids=list(range(NCORE)), **kw)


_built = {}


def _get(name, builder):
    if name not in _built:
        _built[name] = builder()
    return _built[name]


def host_prep_l1(inputs):
    x = np.ascontiguousarray(inputs["L0_attn_x"].reshape(T, D))
    wgT = np.ascontiguousarray(inputs["L0_ffn_w_gate"].T)
    wuT = np.ascontiguousarray(inputs["L0_ffn_w_up"].T)
    wdT = np.ascontiguousarray(inputs["L0_ffn_w_down"].T)
    ident = np.eye(128, dtype=np.float32)
    maps = []
    for c in range(NCORE):
        maps.append({
            "x_slab": np.ascontiguousarray(x[c * TI:(c + 1) * TI]),
            "wgT": wgT, "wuT": wuT, "wdT": wdT, "ident": ident,
        })
    return maps


def run_l1(inputs, **kw):
    nc = _get("l1", build_l1)
    maps = host_prep_l1(inputs)
    return run_bass_kernel_spmd(nc, maps, core_ids=list(range(NCORE)), **kw)


def kernel(**inputs):
    inputs = {k: np.asarray(v) for k, v in inputs.items()}
    r1 = run_l1(inputs)
    xnT_full = np.ascontiguousarray(
        np.concatenate([np.asarray(r["xnT"]) for r in r1.results], axis=1))
    gT_full = np.concatenate([r["gT"] for r in r1.results], axis=1)
    r2 = run_l2(inputs, xnT_full)
    pd_all = np.stack([r["pd"] for r in r2.results], axis=0)
    pB_all = np.stack([r["pB"] for r in r2.results], axis=0)
    pC_all = np.stack([r["pC"] for r in r2.results], axis=0)
    u_all = np.concatenate([r["u_sl"] for r in r2.results], axis=0)
    resS_all = np.concatenate([r["resS"] for r in r2.results], axis=0)
    r3 = run_l3(inputs, pd_all, pB_all, pC_all, u_all, resS_all)
    ygT_full = np.ascontiguousarray(
        np.concatenate([np.asarray(r["ygT"]) for r in r3.results], axis=0))
    x_full = np.ascontiguousarray(inputs["L0_attn_x"].reshape(T, D))
    r4 = run_l4(inputs, ygT_full, gT_full, x_full)
    x1_full = np.concatenate([r["x1T"] for r in r4.results], axis=1)
    xn1_full = np.ascontiguousarray(
        np.concatenate([np.asarray(r["xn1T"]) for r in r4.results], axis=1))
    V1_full = np.ascontiguousarray(
        np.concatenate([r["V1T"] for r in r4.results], axis=1))
    valid_full = np.ascontiguousarray(
        np.concatenate([r["valid"] for r in r4.results], axis=1))
    r5 = run_l5(inputs, V1_full, valid_full)
    s1_full = np.zeros((B, L, 16, 64), np.float32)
    for c in range(NCORE):
        a = r5.results[c]["attn"]
        for j in range(4):
            pid = c * 4 + j
            s1_full[pid // 16, :, pid % 16, :] = a[j * 64:(j + 1) * 64, :].T
    s1T = np.ascontiguousarray(
        s1_full.reshape(T, D).T)
    r6 = run_l6(inputs, xn1_full, np.ascontiguousarray(x1_full), s1T)
    out = np.concatenate([r["out_slab"] for r in r6.results], axis=0)
    return np.ascontiguousarray(out.reshape(B, L, D))


# version 03
# speedup vs baseline: 1.4168x; 1.0067x over previous
import numpy as np
import concourse.bass as bass
import concourse.mybir as mybir
from concourse.bass_utils import run_bass_kernel_spmd

F32 = mybir.dt.float32
BF16 = mybir.dt.bfloat16
I32 = mybir.dt.int32
AF = mybir.ActivationFunctionType
ALU = mybir.AluOpType
AX = mybir.AxisListType

NCORE = 8
B, L, D = 2, 16384, 1024
T = B * L
TI = T // NCORE
DI, NST, DTR, KC = 2048, 16, 64, 4
DC = DI // NCORE
FH = 3072
EPS = 1e-5
MOE_H, MOE_E, MOE_S, MOE_D = 8, 16, 256, 128


class Prog:

    ENGINES = ("sync", "vector", "scalar", "tensor", "gpsimd")

    def __init__(self, nc, sems):
        self.nc = nc
        self.sems = sems
        self.cnt = {k: 0 for k in sems}
        self.ops = {e: [] for e in self.ENGINES}
        self.seen = {e: {} for e in self.ENGINES}
        self.floor = {}

    def barrier(self):
        fr = {s: v for s, v in self.cnt.items() if v > 0}
        self.floor = dict(fr)
        return fr

    def _merge(self, deps):
        fr = {}
        for d in deps:
            if d is None:
                continue
            for s, v in d.items():
                if fr.get(s, -1) < v:
                    fr[s] = v
        return fr

    NOCACHE = False

    def _waits(self, engine, frontier):
        cache = self.seen[engine]
        out = {}
        for s, v in frontier.items():
            if cache.get(s, -1) < v:
                out[s] = v
                cache[s] = v
            elif self.NOCACHE:
                out[s] = v
        return out

    def op(self, engine, fn, deps=(), sem=None, step=1):
        if sem is None:
            sem = "c_" + engine
        fr = self._merge(list(deps) + [self.floor])
        waits = self._waits(engine, fr)
        self.cnt[sem] += step
        val = self.cnt[sem]
        sems = self.sems

        def run(eng, waits=waits, fn=fn, sem=sem, step=step):
            for s, v in waits.items():
                eng.wait_ge(sems[s], v)
            inst = fn(eng)
            inst.then_inc(sems[sem], step)
        self.ops[engine].append(run)
        tok = dict(fr)
        tok[sem] = max(tok.get(sem, 0), val)
        return tok

    def raw(self, engine, fn, deps=()):
        waits = self._waits(engine, self._merge(list(deps) + [self.floor]))
        sems = self.sems

        def run(eng, waits=waits, fn=fn):
            for s, v in waits.items():
                eng.wait_ge(sems[s], v)
            fn(eng)
        self.ops[engine].append(run)
        return None

    def dma(self, queue, out, in_, deps=(), **kw):
        base = "dS" if queue == "sync" else "dG"
        if not hasattr(self, "_dmarr"):
            self._dmarr = {}
        idx = self._dmarr.get(queue, 0)
        self._dmarr[queue] = idx + 1
        sem = f"{base}{idx % 4}"
        prev_done = 16 * (idx // 4)
        fr = self._merge(list(deps) + [self.floor])
        if prev_done > 0:
            fr[sem] = max(fr.get(sem, 0), prev_done)
        waits = self._waits(queue, fr)
        self.cnt[sem] += 16
        val = self.cnt[sem]
        sems = self.sems

        def run(eng, waits=waits, sem=sem, out=out, in_=in_, kw=kw):
            for s, v in waits.items():
                eng.wait_ge(sems[s], v)
            if callable(out):
                out = out()
            if callable(in_):
                in_ = in_()
            eng.dma_start(out=out, in_=in_, **kw).then_inc(sems[sem], 16)
        self.ops[queue].append(run)
        tok = dict(fr)
        tok[sem] = max(tok.get(sem, 0), val)
        return tok

    def wait_only(self, engine, deps):
        waits = self._waits(engine, self._merge(deps))
        sems = self.sems

        def run(eng, waits=waits):
            for s, v in waits.items():
                eng.wait_ge(sems[s], v)
        self.ops[engine].append(run)

    def finalize(self, block):
        for name in self.ENGINES:
            ops = self.ops[name]
            if not ops:
                continue

            def section(eng, ops=ops):
                for f in ops:
                    f(eng)
            getattr(block, name)(section)


def std_sems(nc, stack):
    sems = {}
    names = ["c_vector", "c_scalar", "c_tensor", "c_gpsimd", "c_sync"]
    names += [f"cc{i}" for i in range(7)]
    names += [f"dS{i}" for i in range(4)] + [f"dG{i}" for i in range(4)]
    for name in names:
        sems[name] = stack.enter_context(nc.semaphore(name))
    return sems


def emit_matmul_accum(p, psum_ap, lhsT_tiles, rhs_tiles, deps):
    n = len(lhsT_tiles)
    tok = None
    for k in range(n):
        lt, rt = lhsT_tiles[k], rhs_tiles[k]
        tok = p.op("tensor",
                   lambda e, lt=lt, rt=rt, k=k: e.matmul(
                       out=psum_ap, lhsT=lt, rhs=rt,
                       start=(k == 0), stop=(k == n - 1)),
                   deps=deps if k == 0 else ())
    return tok


def emit_l1(nc, p, st, io):
    x_in = io["x_slab"]
    wgT_in, wuT_in, wdT_in = io["wgT"], io["wuT"], io["wdT"]
    ident_in = io["ident"]
    xnT_out, gT_out = io["xnT"], io["gT"]
    sb = lambda name, shape, dt: st.enter_context(nc.sbuf_tensor("p1" + name, shape, dt))
    ps = lambda name, shape, dt: st.enter_context(nc.psum_tensor("p1" + name, shape, dt))
    if True:
        consts = sb("consts", [128, 5], F32)
        x_t = sb("x_t", [128, D], F32)
        stat = sb("stat", [128, 8], F32)
        xsq = sb("xsq", [128, D], F32)
        xn_tok = sb("xn_tok", [128, D], BF16)
        ident = sb("ident_sb", [128, 128], F32)
        ident_bf = sb("identbf", [128, 128], BF16)
        xnT_buf = sb("xnT_buf", [128, 8 * 128], BF16)
        ps_tr = ps("ps_tr", [128, 128], BF16)
        wg_bf = sb("wg_bf", [128, 8 * FH], BF16)
        wu_bf = sb("wu_bf", [128, 8 * FH], BF16)
        wd_bf = sb("wd_bf", [128, 24 * D], BF16)
        wstage = sb("wstage", [128, FH], I32)
        xnT_chunk = sb("xnT_chunk", [128, 8 * 512], BF16)
        h_bf = sb("h_bf", [128, 24 * 512], BF16)
        sg_t = sb("sg_t", [128, 512], F32)
        ps_g = ps("ps_g", [128, 512], F32)
        ps_u = ps("ps_u", [128, 512], F32)
        ps_d = ps("ps_d", [128, 512], F32)
        g_t = sb("g_t", [128, 512], F32)

        p.raw("vector", lambda e: e.memset(consts[:, 0:1], EPS))
        p.raw("vector", lambda e: e.memset(consts[:, 1:2], 1.0 / 127.0))
        p.raw("vector", lambda e: e.memset(consts[:, 2:3], 1.0 / (127.0 * 127.0)))
        tok_consts = p.op("vector", lambda e: e.memset(consts[:, 3:4], 1.0 / D))
        tok_id = p.dma("sync", ident[:], ident_in)
        tok_idbf = p.op("vector", lambda e: e.tensor_copy(out=ident_bf[:], in_=ident[:]),
                        deps=[tok_id])

        wtoks = []
        for (src, dst, nch) in ((wgT_in, wg_bf, 8), (wuT_in, wu_bf, 8), (wdT_in, wd_bf, 24)):
            width = src.shape[1]
            for kc in range(nch):
                t0 = p.dma("gpsimd", wstage[:, 0:width], src[kc * 128:(kc + 1) * 128, :],
                           deps=wtoks[-1:])
                t2 = p.op("vector", lambda e, d=dst, kc=kc, w=width: e.tensor_copy(
                    out=d[:, kc * w:(kc + 1) * w], in_=wstage[:, 0:w]), deps=[t0])
                wtoks.append(t2)
        wdone = wtoks[-1]

        prev_cons = []
        prev_ev = []
        prev_st = []
        for tt in range(TI // 128):
            tok_x = p.dma("sync", x_t[:], x_in[tt * 128:(tt + 1) * 128, :],
                          deps=prev_cons)
            t_sum = p.op("vector", lambda e: e.tensor_reduce(
                out=stat[:, 0:1], in_=x_t[:], axis=AX.X, op=ALU.add), deps=[tok_x])
            t_sq = p.op("scalar", lambda e: e.activation(
                out=xsq[:], in_=x_t[:], func=AF.Square, accum_out=stat[:, 1:2]),
                deps=[tok_x])
            t_mu = p.op("vector", lambda e: e.tensor_scalar(
                out=stat[:, 2:3], in0=stat[:, 0:1], scalar1=consts[:, 3:4], scalar2=None,
                op0=ALU.mult), deps=[t_sum, tok_consts])
            t_msq = p.op("vector", lambda e: e.tensor_mul(
                out=stat[:, 3:4], in0=stat[:, 2:3], in1=stat[:, 2:3]), deps=[t_mu])
            t_var = p.op("vector", lambda e: e.tensor_scalar(
                out=stat[:, 4:5], in0=stat[:, 1:2], scalar1=consts[:, 3:4],
                scalar2=stat[:, 3:4], op0=ALU.mult, op1=ALU.subtract),
                deps=[t_sq, t_msq])
            t_sd = p.op("scalar", lambda e: e.activation(
                out=stat[:, 5:6], in_=stat[:, 4:5], func=AF.Sqrt, bias=consts[:, 0:1]),
                deps=[t_var, tok_consts])
            t_rstd = p.op("vector", lambda e: e.reciprocal(
                out=stat[:, 6:7], in_=stat[:, 5:6]), deps=[t_sd])
            t_xn = p.op("vector", lambda e: e.tensor_scalar(
                out=xn_tok[:], in0=x_t[:], scalar1=stat[:, 2:3], scalar2=stat[:, 6:7],
                op0=ALU.subtract, op1=ALU.mult), deps=[t_rstd, t_mu] + prev_ev)
            ev_toks = []
            for c in range(8):
                t_tr = p.op("tensor", lambda e, c=c: e.transpose(
                    out=ps_tr[:], in_=xn_tok[:, c * 128:(c + 1) * 128],
                    identity=ident_bf[:]), deps=[t_xn, tok_idbf] + ev_toks[-1:])
                t_ev = p.op("scalar", lambda e, c=c: e.copy(
                    out=xnT_buf[:, c * 128:(c + 1) * 128], in_=ps_tr[:]),
                    deps=[t_tr] + prev_st)
                ev_toks.append(t_ev)
            t_st = p.dma("sync", xnT_out.rearrange("(c p) t -> c p t", p=128)[
                :, :, tt * 128:(tt + 1) * 128].transpose([1, 0, 2]),
                xnT_buf[:].rearrange("p (c t) -> p c t", c=8), deps=ev_toks)
            prev_cons = [t_xn, t_st, t_sq]
            prev_ev = [ev_toks[-1]]
            prev_st = [t_st]

        ln_done = prev_cons
        for ts in range(TI // 512):
            tok_xc = p.dma("sync", xnT_chunk[:].rearrange("p (c t) -> p c t", c=8),
                           xnT_out.rearrange("(c p) t -> c p t", p=128)[
                               :, :, ts * 512:(ts + 1) * 512].transpose([1, 0, 2]),
                           deps=ln_done)
            h_toks = []
            for ht in range(24):
                lg = [wg_bf[:, kc * FH + ht * 128: kc * FH + (ht + 1) * 128]
                      for kc in range(8)]
                lu = [wu_bf[:, kc * FH + ht * 128: kc * FH + (ht + 1) * 128]
                      for kc in range(8)]
                rx = [xnT_chunk[:, kc * 512:(kc + 1) * 512] for kc in range(8)]
                tg = emit_matmul_accum(p, ps_g[:], lg, rx, deps=[tok_xc, wdone] + h_toks[-1:])
                t_sg = p.op("scalar", lambda e: e.activation(
                    out=sg_t[:], in_=ps_g[:], func=AF.Silu, scale=consts[:, 1:2]),
                    deps=[tg])
                tu = emit_matmul_accum(p, ps_u[:], lu, rx, deps=[t_sg])
                t_h = p.op("vector", lambda e, ht=ht: e.tensor_tensor(
                    out=h_bf[:, ht * 512:(ht + 1) * 512], in0=sg_t[:], in1=ps_u[:],
                    op=ALU.mult), deps=[tu, t_sg])
                h_toks.append(t_h)
            st_toks = []
            for ot in range(8):
                ld = [wd_bf[:, hc * D + ot * 128: hc * D + (ot + 1) * 128]
                      for hc in range(24)]
                rh = [h_bf[:, hc * 512:(hc + 1) * 512] for hc in range(24)]
                td = emit_matmul_accum(p, ps_d[:], ld, rh, deps=h_toks + st_toks[-1:])
                t_g = p.op("scalar", lambda e: e.activation(
                    out=g_t[:], in_=ps_d[:], func=AF.Silu, scale=consts[:, 2:3]),
                    deps=[td])
                t_st = p.dma("sync", gT_out[ot * 128:(ot + 1) * 128,
                                            ts * 512:(ts + 1) * 512], g_t[:],
                             deps=[t_g])
                st_toks.append(t_st)
            ln_done = st_toks + h_toks
    return ln_done


def build_l1():
    nc = bass.Bass("TRN2", target_bir_lowering=False, debug=False)
    io = {
        "x_slab": nc.dram_tensor("x_slab", [TI, D], F32, kind="ExternalInput").ap(),
        "wgT": nc.dram_tensor("wgT", [D, FH], I32, kind="ExternalInput").ap(),
        "wuT": nc.dram_tensor("wuT", [D, FH], I32, kind="ExternalInput").ap(),
        "wdT": nc.dram_tensor("wdT", [FH, D], I32, kind="ExternalInput").ap(),
        "ident": nc.dram_tensor("ident", [128, 128], F32, kind="ExternalInput").ap(),
        "xnT": nc.dram_tensor("xnT", [D, TI], BF16, kind="ExternalOutput").ap(),
        "gT": nc.dram_tensor("gT", [D, TI], F32, kind="ExternalOutput").ap(),
    }
    import contextlib
    with contextlib.ExitStack() as st:
        sems = std_sems(nc, st)
        block = st.enter_context(nc.Block())
        p = Prog(nc, sems)
        toks = emit_l1(nc, p, st, io)
        p.wait_only("gpsimd", toks)
        p.finalize(block)
    return nc


def emit_l2(nc, p, st, io):
    winT_in = io["winT"]
    convw_in, convb_in, xpT_in = io["convw"], io["convb"], io["xpT"]
    pd_out, pB_out, pC_out = io["pd"], io["pB"], io["pC"]
    u_out, resS_out = io["u_sl"], io["resS"]
    xn_chunk_ap = io["xn_chunk_ap"]
    TC = 2048
    NSUB = TC // 512
    sb = lambda name, shape, dt: st.enter_context(nc.sbuf_tensor("p2" + name, shape, dt))
    ps = lambda name, shape, dt: st.enter_context(nc.psum_tensor("p2" + name, shape, dt))
    if True:
        xnc = sb("xnc", [128, 8 * TC], BF16)
        win_bf = sb("win_bf", [128, 8 * 512], BF16)
        xpT_bf = sb("xpT_bf", [128, 2 * 96], BF16)
        convw_sb = sb("convw_sb", [128, 2 * KC], F32)
        convb_sb = sb("convb_sb", [128, 2], F32)
        xpart = [sb(f"xpart{i}", [128, TC + 3], F32) for i in range(2)]
        u_f = [sb(f"u_f{i}", [128, TC], F32) for i in range(2)]
        u_bf = [sb(f"u_bf{i}", [128, TC], BF16) for i in range(2)]
        ev_t = sb("ev_t", [128, 512], F32)
        p_sb = sb("p_sb", [96, 512], F32)
        ps_a = ps("ps_a", [128, 512], F32)
        ps_b = ps("ps_b", [128, 512], F32)
        ps_p = ps("ps_p", [96, 512], F32)

        t_w = p.dma("gpsimd", win_bf[:].rearrange("p (k o) -> p k o", k=8),
                    winT_in.rearrange("(k p) o -> p k o", p=128))
        t_xp = p.dma("gpsimd", xpT_bf[:].rearrange("p (k o) -> p k o", k=2),
                     xpT_in.rearrange("(k p) o -> p k o", p=128))
        t_cw = p.dma("sync", convw_sb[:].rearrange("p (k o) -> p k o", k=2),
                     convw_in.rearrange("(k p) o -> p k o", p=128))
        t_cb = p.dma("sync", convb_sb[:].rearrange("p (k o) -> p k o", k=2),
                     convb_in.rearrange("(k p) o -> p k o", p=128),
                     allow_slow_non_contiguous=True)
        wdeps = [t_w, t_xp, t_cw, t_cb]

        prev = {"xnc": [], "xpart": [], "u": [], "psA": [], "psB": [], "psP": [],
                "ev": [], "psb": []}
        for tc in range(T // TC):
            batch_start = (tc % (L // TC) == 0)
            t_x = p.dma("sync", xnc[:].rearrange("p (k t) -> p k t", k=8),
                        xn_chunk_ap(tc), deps=prev["xnc"])
            mm_toks = []
            conv_deps = []
            res_sts = []
            for ot in range(4):
                for sub in range(NSUB):
                    lh = [win_bf[:, kc * 512 + ot * 128:(kc) * 512 + (ot + 1) * 128]
                          for kc in range(8)]
                    rh = [xnc[:, kc * TC + sub * 512: kc * TC + (sub + 1) * 512]
                          for kc in range(8)]
                    pbank = ps_a if (ot * NSUB + sub) % 2 == 0 else ps_b
                    pdep = prev["psA"] if pbank is ps_a else prev["psB"]
                    tm = emit_matmul_accum(p, pbank[:], lh, rh,
                                           deps=[t_x, t_w] + pdep)
                    mm_toks.append(tm)
                    if ot < 2:
                        tev = p.op("scalar", lambda e, ot=ot, sub=sub, pb=pbank: e.copy(
                            out=xpart[ot][:, 3 + sub * 512: 3 + (sub + 1) * 512],
                            in_=pb[:]), deps=[tm] + prev["xpart"])
                        conv_deps.append(tev)
                    else:
                        tev = p.op("scalar", lambda e, pb=pbank: e.activation(
                            out=ev_t[:], in_=pb[:], func=AF.Silu),
                            deps=[tm] + prev["ev"])
                        tst = p.dma("sync", resS_out[(ot - 2) * 128:(ot - 1) * 128,
                                                     tc * TC + sub * 512:
                                                     tc * TC + (sub + 1) * 512],
                                    ev_t[:], deps=[tev])
                        res_sts.append(tst)
                        prev["ev"] = [tst]
                    if pbank is ps_a:
                        prev["psA"] = [tev]
                    else:
                        prev["psB"] = [tev]
            if batch_start:
                hdeps = prev["u"] + prev.get("halo", [])
                th = p.op("vector", lambda e: e.memset(xpart[0][:, 0:3], 0.0),
                          deps=hdeps)
                th2 = p.op("vector", lambda e: e.memset(xpart[1][:, 0:3], 0.0),
                          deps=hdeps)
                halo_toks = [th, th2]
            else:
                halo_toks = prev["halo"]
            u_toks = []
            ub_toks = []
            for ct in range(2):
                t_acc = p.op("vector", lambda e, ct=ct: e.tensor_scalar(
                    out=u_f[ct][:], in0=xpart[ct][:, 0:TC],
                    scalar1=convw_sb[:, ct * KC:ct * KC + 1], scalar2=None,
                    op0=ALU.mult),
                    deps=conv_deps + halo_toks + [t_cw] + prev["u"])
                for k in range(1, KC):
                    t_acc = p.op("vector", lambda e, ct=ct, k=k: e.scalar_tensor_tensor(
                        out=u_f[ct][:], in0=xpart[ct][:, k:k + TC],
                        scalar=convw_sb[:, ct * KC + k:ct * KC + k + 1],
                        in1=u_f[ct][:], op0=ALU.mult, op1=ALU.add), deps=[t_acc])
                t_u = p.op("scalar", lambda e, ct=ct: e.activation(
                    out=u_f[ct][:], in_=u_f[ct][:], func=AF.Silu,
                    bias=convb_sb[:, ct:ct + 1]), deps=[t_acc, t_cb])
                t_ub = p.op("vector", lambda e, ct=ct: e.tensor_copy(
                    out=u_bf[ct][:], in_=u_f[ct][:]), deps=[t_u])
                t_us = p.dma("sync", u_out[ct * 128:(ct + 1) * 128,
                                           tc * TC:(tc + 1) * TC], u_f[ct][:],
                             deps=[t_u])
                u_toks += [t_u, t_us]
                ub_toks.append(t_ub)
            nhalo = []
            for ct in range(2):
                nh = p.op("vector", lambda e, ct=ct: e.tensor_copy(
                    out=xpart[ct][:, 0:3], in_=xpart[ct][:, TC:TC + 3]),
                    deps=u_toks + ub_toks)
                nhalo.append(nh)
            prev["halo"] = nhalo
            p_sts = []
            for sub in range(NSUB):
                lh = [xpT_bf[:, k * 96:(k + 1) * 96] for k in range(2)]
                rh = [u_bf[k][:, sub * 512:(sub + 1) * 512] for k in range(2)]
                tm = emit_matmul_accum(p, ps_p[:], lh, rh,
                                       deps=[t_xp] + ub_toks + prev["psP"])
                tpe = p.op("scalar", lambda e: e.copy(out=p_sb[:], in_=ps_p[:]),
                           deps=[tm] + prev["psb"])
                prev["psP"] = [tpe]
                c0 = tc * TC + sub * 512
                s1 = p.dma("sync", pd_out[:, c0:c0 + 512], p_sb[0:64, :], deps=[tpe])
                s2 = p.dma("sync", pB_out[:, c0:c0 + 512], p_sb[64:80, :], deps=[tpe])
                s3 = p.dma("sync", pC_out[:, c0:c0 + 512], p_sb[80:96, :], deps=[tpe])
                prev["psb"] = [s1, s2, s3]
                p_sts += [s1, s2, s3]
            prev["xnc"] = mm_toks[-1:]
            prev["xpart"] = u_toks + nhalo
            prev["u"] = u_toks + p_sts + ub_toks
    return prev["u"] + res_sts + prev["psb"] + prev["ev"]


def build_l2():
    nc = bass.Bass("TRN2", target_bir_lowering=False, debug=False)
    xnT_in = nc.dram_tensor("xnT_full", [D, T], BF16, kind="ExternalInput").ap()
    io = {
        "winT": nc.dram_tensor("winT", [D, 2 * DC], F32, kind="ExternalInput").ap(),
        "convw": nc.dram_tensor("convw", [DC, KC], F32, kind="ExternalInput").ap(),
        "convb": nc.dram_tensor("convb", [DC, 1], F32, kind="ExternalInput").ap(),
        "xpT": nc.dram_tensor("xpT", [DC, 96], F32, kind="ExternalInput").ap(),
        "pd": nc.dram_tensor("pd", [DTR, T], F32, kind="ExternalOutput").ap(),
        "pB": nc.dram_tensor("pB", [NST, T], F32, kind="ExternalOutput").ap(),
        "pC": nc.dram_tensor("pC", [NST, T], F32, kind="ExternalOutput").ap(),
        "u_sl": nc.dram_tensor("u_sl", [DC, T], F32, kind="ExternalOutput").ap(),
        "resS": nc.dram_tensor("resS", [DC, T], F32, kind="ExternalOutput").ap(),
        "xn_chunk_ap": lambda tc: xnT_in.rearrange("(k p) t -> p k t", p=128)[
            :, :, tc * 2048:(tc + 1) * 2048],
    }
    import contextlib
    with contextlib.ExitStack() as st:
        sems = std_sems(nc, st)
        block = st.enter_context(nc.Block())
        p = Prog(nc, sems)
        toks = emit_l2(nc, p, st, io)
        p.wait_only("gpsimd", toks)
        p.finalize(block)
    return nc


def host_prep_l2(inputs, xnT_full):
    ipw = inputs["L0_attn_in_proj_w"]
    xpw = inputs["L0_attn_x_proj_w"]
    maps = []
    for c in range(NCORE):
        sl = slice(c * DC, (c + 1) * DC)
        winT = np.ascontiguousarray(
            np.concatenate([ipw[sl], ipw[DI:][sl]], axis=0).T)
        maps.append({
            "xnT_full": xnT_full,
            "winT": winT,
            "convw": np.ascontiguousarray(inputs["L0_attn_conv_w"][sl, 0, :]),
            "convb": np.ascontiguousarray(inputs["L0_attn_conv_b"][sl, None]),
            "xpT": np.ascontiguousarray(xpw[:, sl].T),
        })
    return maps


def run_l2(inputs, xnT_full, **kw):
    nc = _get("l2", build_l2)
    maps = host_prep_l2(inputs, xnT_full)
    return run_bass_kernel_spmd(nc, maps, core_ids=list(range(NCORE)), **kw)


def emit_l3(nc, p, st, io):
    presummed = io.get("presummed", False)
    if not presummed:
        pd_all, pB_all, pC_all = io["pd_all"], io["pB_all"], io["pC_all"]
        pBC_dram = io["pBC_scr"]
    else:
        p_sum_d = io["p_sum"]
    dtT_in, dtb_in = io["dtT"], io["dtb"]
    Alog_in, Dp_in = io["Alog"], io["Dp"]
    u_in, resS_in = io["u_sl"], io["resS"]
    yg_out = io["ygT"]
    TC = 2048
    NSUB = TC // 512
    sb = lambda name, shape, dt: st.enter_context(nc.sbuf_tensor("p3" + name, shape, dt))
    ps = lambda name, shape, dt: st.enter_context(nc.psum_tensor("p3" + name, shape, dt))
    if True:
        consts = sb("consts", [128, 2], F32)
        dtT_bf = sb("dtT_bf", [DTR, DC], BF16)
        dtb_sb = sb("dtb_sb", [128, 2], F32)
        Dp_sb = sb("Dp_sb", [128, 2], F32)
        A_sb = sb("A_sb", [128, 2 * NST], F32)
        state = sb("state", [128, 2 * NST], F32)
        pd_sum = sb("pd_sum", [DTR, TC], F32)
        pB_sum = sb("pB_sum", [NST, TC], F32)
        pC_sum = sb("pC_sum", [NST, TC], F32)
        dr_bf = sb("dr_bf", [DTR, TC], BF16)
        delta = [sb(f"delta{i}", [128, TC], F32) for i in range(2)]
        du = [sb(f"du{i}", [128, TC], F32) for i in range(2)]
        dA = [sb(f"dA{i}", [128, TC], F32) for i in range(2)]
        B_bc = [sb(f"B_bc{j}", [128, TC], F32) for j in range(2)]
        C_bc = [sb(f"C_bc{j}", [128, TC], F32) for j in range(2)]
        zB = [sb(f"zB{i}", [128, TC], F32) for i in range(2)]
        xbuf = [sb(f"xbuf{i}", [128, TC], F32) for i in range(2)]
        yacc = [sb(f"yacc{i}", [128, TC], F32) for i in range(2)]
        u_ch = [sb(f"u_ch{i}", [128, TC], F32) for i in range(2)]
        res_ch = [sb(f"res_ch{i}", [128, TC], F32) for i in range(2)]
        yg_bf = sb("yg_bf", [128, TC], BF16)
        sp_a = sb("sp_a", [128, 512], F32)
        sp_l = sb("sp_l", [128, 512], F32)
        ps_dt = ps("ps_dt", [128, 512], F32)

        tok_c = p.op("vector", lambda e: e.memset(consts[:, 0:1], -1.0))
        t_dtT = p.dma("gpsimd", dtT_bf[:], dtT_in)
        t_dtb = p.dma("sync", dtb_sb[:].rearrange("p (k o) -> p k o", k=2),
                      dtb_in.rearrange("(k p) o -> p k o", p=128),
                      allow_slow_non_contiguous=True)
        t_Dp = p.dma("sync", Dp_sb[:].rearrange("p (k o) -> p k o", k=2),
                     Dp_in.rearrange("(k p) o -> p k o", p=128),
                     allow_slow_non_contiguous=True)
        t_Al = p.dma("sync", A_sb[:].rearrange("p (k o) -> p k o", k=2),
                     Alog_in.rearrange("(k p) o -> p k o", p=128))
        t_A1 = p.op("scalar", lambda e: e.activation(
            out=A_sb[:], in_=A_sb[:], func=AF.Exp), deps=[t_Al])
        t_A = p.op("vector", lambda e: e.tensor_scalar(
            out=A_sb[:], in0=A_sb[:], scalar1=consts[:, 0:1], scalar2=None,
            op0=ALU.mult), deps=[t_A1, tok_c])
        t_st0 = p.op("vector", lambda e: e.memset(state[:], 0.0))

        prev = {"stage": [], "psum": [], "dr": [], "delta": [], "du": [],
                "dA": [[], []], "zB": [[], []], "xb": [[], []], "ya": [],
                "uch": [], "res": [], "yg": [], "st": [t_st0],
                "bcB": [[], []], "bcC": [[], []], "psdt": [],
                "spa": [], "spl": []}
        for tc in range(T // TC):
            c0 = tc * TC
            if presummed:
                t_pd = p.dma("gpsimd", pd_sum[:, :], p_sum_d[0:DTR, c0:c0 + TC],
                             deps=prev["psum"])
                sums = [t_pd]
                t_wB = t_wC = t_pd
            else:
                sums = []
                for (src, dst, rows) in ((pd_all, pd_sum, DTR),
                                         (pB_all, pB_sum, NST),
                                         (pC_all, pC_sum, NST)):
                    t_l = p.dma("gpsimd", dst[:, :], src[0, :, c0:c0 + TC],
                                deps=prev["psum"])
                    for j in range(1, NCORE):
                        t_l = p.dma("gpsimd", dst[:, :], src[j, :, c0:c0 + TC],
                                    accum_op=ALU.add, deps=[t_l])
                    sums.append(t_l)
                t_pd, t_pB, t_pC = sums
                t_wB = p.dma("sync", pBC_dram[0:NST, c0:c0 + TC], pB_sum[:, :],
                             deps=[t_pB])
                t_wC = p.dma("sync", pBC_dram[NST:2 * NST, c0:c0 + TC],
                             pC_sum[:, :], deps=[t_pC])
            t_dr = p.op("vector", lambda e: e.tensor_copy(
                out=dr_bf[:], in_=pd_sum[:]), deps=[t_pd] + prev["dr"])
            t_u, t_res = [], []
            for ct in range(2):
                t_u.append(p.dma("sync", u_ch[ct][:],
                                 u_in[ct * 128:(ct + 1) * 128, c0:c0 + TC],
                                 deps=prev["uch"]))
                t_res.append(p.dma("sync", res_ch[ct][:],
                                   resS_in[ct * 128:(ct + 1) * 128, c0:c0 + TC],
                                   deps=prev["res"]))
            d_toks = [[], []]
            for ct in range(2):
                for sub in range(NSUB):
                    t_mm = p.op("tensor", lambda e, ct=ct, sub=sub: e.matmul(
                        out=ps_dt[:], lhsT=dtT_bf[:, ct * 128:(ct + 1) * 128],
                        rhs=dr_bf[:, sub * 512:(sub + 1) * 512],
                        start=True, stop=True),
                        deps=[t_dr, t_dtT] + prev["psdt"])
                    t_a = p.op("scalar", lambda e, ct=ct: e.activation(
                        out=sp_a[:], in_=ps_dt[:], func=AF.Abs,
                        bias=dtb_sb[:, ct:ct + 1]), deps=[t_mm, t_dtb] + prev["spa"])
                    t_e = p.op("scalar", lambda e: e.activation(
                        out=sp_a[:], in_=sp_a[:], func=AF.Exp,
                        scale=consts[:, 0:1]), deps=[t_a, tok_c])
                    t_ln = p.op("scalar", lambda e: e.activation(
                        out=sp_l[:], in_=sp_a[:], func=AF.Ln, bias=1.0),
                        deps=[t_e] + prev["spl"])
                    t_r = p.op("scalar", lambda e, ct=ct: e.activation(
                        out=sp_a[:], in_=ps_dt[:], func=AF.Relu,
                        bias=dtb_sb[:, ct:ct + 1]), deps=[t_ln])
                    t_d = p.op("vector", lambda e, ct=ct, sub=sub: e.tensor_add(
                        out=delta[ct][:, sub * 512:(sub + 1) * 512],
                        in0=sp_a[:], in1=sp_l[:]), deps=[t_r] + prev["delta"])
                    prev["psdt"] = [t_d]
                    prev["spa"] = [t_d]
                    prev["spl"] = [t_d]
                    d_toks[ct].append(t_d)
            du_toks = []
            for ct in range(2):
                du_toks.append(p.op("vector", lambda e, ct=ct: e.tensor_mul(
                    out=du[ct][:], in0=delta[ct][:], in1=u_ch[ct][:]),
                    deps=d_toks[ct] + [t_u[ct]] + prev["du"]))
            st_dep = prev["st"]
            if c0 == L:
                t_rs = p.op("vector", lambda e: e.memset(state[:], 0.0),
                            deps=prev["st"])
                st_dep = [t_rs]
            y_toks = [[], []]
            first_n = [True, True]
            for n in range(NST):
                bb, cb = B_bc[n % 2], C_bc[n % 2]
                if presummed:
                    srcB = p_sum_d[DTR + n:DTR + n + 1, c0:c0 + TC]
                    srcC = p_sum_d[DTR + NST + n:DTR + NST + n + 1, c0:c0 + TC]
                else:
                    srcB = pBC_dram[n:n + 1, c0:c0 + TC]
                    srcC = pBC_dram[NST + n:NST + n + 1, c0:c0 + TC]
                t_bB = p.dma("gpsimd", bb[:], srcB.broadcast_to([128, TC]),
                             deps=[t_wB] + prev["bcB"][n % 2])
                t_bC = p.dma("sync", cb[:], srcC.broadcast_to([128, TC]),
                             deps=[t_wC] + prev["bcC"][n % 2])
                for ct in range(2):
                    da = dA[ct]
                    t_da = p.op("scalar", lambda e, ct=ct, n=n, da=da: e.activation(
                        out=da[:], in_=delta[ct][:], func=AF.Exp,
                        scale=A_sb[:, ct * NST + n:ct * NST + n + 1]),
                        deps=d_toks[ct] + [t_A] + prev["dA"][ct])
                    t_zb = p.op("vector", lambda e, ct=ct, bb=bb: e.tensor_mul(
                        out=zB[ct][:], in0=du[ct][:], in1=bb[:]),
                        deps=[t_bB, du_toks[ct]] + prev["zB"][ct])
                    scol = state[:, ct * NST + n:ct * NST + n + 1]
                    t_scan = p.op("vector", lambda e, ct=ct, sc=scol, da=da: e.tensor_tensor_scan(
                        out=xbuf[ct][:], data0=da[:], data1=zB[ct][:],
                        initial=sc, op0=ALU.mult, op1=ALU.add),
                        deps=[t_zb, t_da] + st_dep + prev["xb"][ct])
                    t_sv = p.op("scalar", lambda e, ct=ct, sc=scol: e.copy(
                        out=sc, in_=xbuf[ct][:, TC - 1:TC]), deps=[t_scan])
                    st_dep = st_dep + [t_sv]
                    prev["st"] = st_dep
                    if first_n[ct]:
                        t_y = p.op("vector", lambda e, ct=ct, cb=cb: e.tensor_mul(
                            out=yacc[ct][:], in0=xbuf[ct][:], in1=cb[:]),
                            deps=[t_bC, t_scan] + prev["ya"])
                    else:
                        t_m = p.op("vector", lambda e, ct=ct, cb=cb: e.tensor_mul(
                            out=zB[ct][:], in0=xbuf[ct][:], in1=cb[:]),
                            deps=[t_bC, t_scan])
                        t_y = p.op("vector", lambda e, ct=ct: e.tensor_add(
                            out=yacc[ct][:], in0=yacc[ct][:], in1=zB[ct][:]),
                            deps=[t_m])
                    y_toks[ct].append(t_y)
                    first_n[ct] = False
                    prev["xb"][ct] = [t_y, t_sv]
                    prev["dA"][ct] = [t_scan]
                    prev["zB"][ct] = [t_y]
                prev["bcB"][n % 2] = [y_toks[0][-1], y_toks[1][-1]]
                prev["bcC"][n % 2] = [y_toks[0][-1], y_toks[1][-1]]
            yg_toks = []
            for ct in range(2):
                t_yd = p.op("vector", lambda e, ct=ct: e.scalar_tensor_tensor(
                    out=yacc[ct][:], in0=u_ch[ct][:], scalar=Dp_sb[:, ct:ct + 1],
                    in1=yacc[ct][:], op0=ALU.mult, op1=ALU.add),
                    deps=y_toks[ct] + [t_Dp])
                t_yg = p.op("vector", lambda e, ct=ct: e.tensor_mul(
                    out=yg_bf[:], in0=yacc[ct][:], in1=res_ch[ct][:]),
                    deps=[t_yd, t_res[ct]] + prev["yg"] + yg_toks[-1:])
                t_sg = p.dma("sync", yg_out[ct * 128:(ct + 1) * 128, c0:c0 + TC],
                             yg_bf[:], deps=[t_yg])
                yg_toks += [t_yg, t_sg]
            prev["psum"] = [t_dr, t_wB, t_wC]
            prev["dr"] = d_toks[0] + d_toks[1]
            prev["delta"] = [t for ct in range(2) for t in prev["dA"][ct]] + du_toks
            prev["du"] = [t for ct in range(2) for t in prev["zB"][ct]] + du_toks
            prev["uch"] = yg_toks + du_toks
            prev["res"] = yg_toks
            prev["ya"] = yg_toks
            prev["yg"] = yg_toks
    return yg_toks + prev["st"]


def build_l3():
    nc = bass.Bass("TRN2", target_bir_lowering=False, debug=False)
    io = {
        "pd_all": nc.dram_tensor("pd_all", [NCORE, DTR, T], F32, kind="ExternalInput").ap(),
        "pB_all": nc.dram_tensor("pB_all", [NCORE, NST, T], F32, kind="ExternalInput").ap(),
        "pC_all": nc.dram_tensor("pC_all", [NCORE, NST, T], F32, kind="ExternalInput").ap(),
        "dtT": nc.dram_tensor("dtT", [DTR, DC], F32, kind="ExternalInput").ap(),
        "dtb": nc.dram_tensor("dtb", [DC, 1], F32, kind="ExternalInput").ap(),
        "Alog": nc.dram_tensor("Alog", [DC, NST], F32, kind="ExternalInput").ap(),
        "Dp": nc.dram_tensor("Dp", [DC, 1], F32, kind="ExternalInput").ap(),
        "u_sl": nc.dram_tensor("u_sl", [DC, T], F32, kind="ExternalInput").ap(),
        "resS": nc.dram_tensor("resS", [DC, T], F32, kind="ExternalInput").ap(),
        "ygT": nc.dram_tensor("ygT", [DC, T], BF16, kind="ExternalOutput").ap(),
        "pBC_scr": nc.dram_tensor("pBC_scr", [2 * NST, T], F32, kind="Internal").ap(),
    }
    import contextlib
    with contextlib.ExitStack() as st:
        sems = std_sems(nc, st)
        block = st.enter_context(nc.Block())
        p = Prog(nc, sems)
        toks = emit_l3(nc, p, st, io)
        p.wait_only("gpsimd", toks)
        p.finalize(block)
    return nc


def host_prep_l3(inputs, pd_all, pB_all, pC_all, u_all, resS_all):
    dtw = inputs["L0_attn_dt_proj_w"]
    maps = []
    for c in range(NCORE):
        sl = slice(c * DC, (c + 1) * DC)
        maps.append({
            "pd_all": pd_all, "pB_all": pB_all, "pC_all": pC_all,
            "dtT": np.ascontiguousarray(dtw[sl, :].T),
            "dtb": np.ascontiguousarray(inputs["L0_attn_dt_proj_b"][sl, None]),
            "Alog": np.ascontiguousarray(inputs["L0_attn_A_log"][sl, :]),
            "Dp": np.ascontiguousarray(inputs["L0_attn_D_param"][sl, None]),
            "u_sl": u_all[sl], "resS": resS_all[sl],
        })
    return maps


def run_l3(inputs, pd_all, pB_all, pC_all, u_all, resS_all, **kw):
    nc = _get("l3", build_l3)
    maps = host_prep_l3(inputs, pd_all, pB_all, pC_all, u_all, resS_all)
    return run_bass_kernel_spmd(nc, maps, core_ids=list(range(NCORE)), **kw)


def emit_l4(nc, p, st, io):
    yg_ap = io["yg_ap"]
    opT_in, x_in, gT_in = io["opT"], io["x_slab"], io["gT"]
    Wk_in, bk_in, Wv_in, bv_in = io["Wk"], io["bk"], io["Wv"], io["bv"]
    RH_in, ident_in = io["RHblk"], io["ident"]
    x1_out, xn1_out = io["x1T"], io["xn1T"]
    V1_out, valid_out = io["V1T"], io["valid"]
    TC = 512
    sb = lambda name, shape, dt: st.enter_context(nc.sbuf_tensor("p4" + name, shape, dt))
    ps = lambda name, shape, dt: st.enter_context(nc.psum_tensor("p4" + name, shape, dt))
    if True:
        consts = sb("consts", [128, 4], F32)
        ident = sb("ident_sb", [128, 128], F32)
        ones1 = sb("ones1", [1, 128], F32)
        ones_col = sb("ones_col", [128, 1], F32)
        opT_bf = sb("opT_bf", [128, 16 * D], BF16)
        Wk_bf = sb("Wk_bf", [128, 8 * D], BF16)
        Wv_bf = sb("Wv_bf", [128, 8 * D], BF16)
        RH_bf = sb("RH_bf", [128, 8 * NST], BF16)
        bk_sb = sb("bk_sb", [128, 8], F32)
        bv_sb = sb("bv_sb", [128, 8], F32)
        ygc = sb("ygc", [128, 16 * TC], BF16)
        s_sb = sb("s_sb", [128, 8 * TC], F32)
        x_tok = sb("x_tok", [128, D], F32)
        xT_sb = sb("xT_sb", [128, 8 * TC], F32)
        g_sb = sb("g_sb", [128, 8 * TC], F32)
        x1_sb = sb("x1_sb", [128, 8 * TC], F32)
        tmp_t = sb("tmp_t", [128, TC], F32)
        sq_t = [sb(f"sq_t{i}", [128, TC], F32) for i in range(2)]
        strow = sb("strow", [1, 4 * TC], F32)
        xn1_sb = sb("xn1_sb", [128, 8 * TC], BF16)
        K1_sb = sb("K1_sb", [128, 8 * TC], BF16)
        V1_sb = sb("V1_sb", [128, 8 * TC], F32)
        val_sb = sb("val_sb", [NST, TC], F32)
        ps_a = ps("ps_a", [128, TC], F32)
        ps_b = ps("ps_b", [128, TC], F32)
        ps_st1 = ps("ps_st1", [1, TC], F32)
        ps_st2 = ps("ps_st2", [1, TC], F32)
        ps_mu = ps("ps_mu", [128, TC], F32)
        ps_rs = ps("ps_rs", [128, TC], F32)
        ps_dot = ps("ps_dot", [NST, TC], F32)

        p.raw("vector", lambda e: e.memset(consts[:, 0:1], 1.0 / D))
        tok_c = p.op("vector", lambda e: e.memset(consts[:, 1:2], EPS))
        tok_o1 = p.op("vector", lambda e: e.memset(ones1[:], 1.0))
        tok_oc = p.op("vector", lambda e: e.memset(ones_col[:], 1.0))
        t_id = p.dma("sync", ident[:], ident_in)
        t_op = p.dma("gpsimd", opT_bf[:].rearrange("p (k o) -> p k o", k=16),
                     opT_in.rearrange("(k p) o -> p k o", p=128))
        t_wk = p.dma("gpsimd", Wk_bf[:].rearrange("p (k o) -> p k o", k=8),
                     Wk_in.rearrange("(k p) o -> p k o", p=128))
        t_wv = p.dma("gpsimd", Wv_bf[:].rearrange("p (k o) -> p k o", k=8),
                     Wv_in.rearrange("(k p) o -> p k o", p=128))
        t_rh = p.dma("gpsimd", RH_bf[:].rearrange("p (k o) -> p k o", k=8),
                     RH_in.rearrange("(k p) o -> p k o", p=128))
        t_bk = p.dma("sync", bk_sb[:].rearrange("p (k o) -> p k o", k=8),
                     bk_in.rearrange("(k p) o -> p k o", p=128),
                     allow_slow_non_contiguous=True)
        t_bv = p.dma("sync", bv_sb[:].rearrange("p (k o) -> p k o", k=8),
                     bv_in.rearrange("(k p) o -> p k o", p=128),
                     allow_slow_non_contiguous=True)

        prev = {"ygc": [], "sA": [], "sB": [], "s": [], "xtok": [], "xT": [],
                "g": [], "x1": [], "sq": [[], []], "st": [], "mu": [],
                "xn1": [], "K1": [], "V1": [], "val": [], "tmp": [], "dot": []}
        for tc in range(TI // TC):
            c0 = tc * TC
            t_yg = p.dma("gpsimd", ygc[:].rearrange("p (k t) -> p k t", k=16),
                         yg_ap(tc), deps=prev["ygc"])
            t_g = p.dma("sync", g_sb[:].rearrange("p (k t) -> p k t", k=8),
                        gT_in.rearrange("(k p) t -> p k t", p=128)[:, :, c0:c0 + TC],
                        deps=prev["g"])
            s_toks = []
            for ot in range(8):
                pb = ps_a if ot % 2 == 0 else ps_b
                pdep = prev["sA"] if ot % 2 == 0 else prev["sB"]
                lh = [opT_bf[:, kc * D + ot * 128: kc * D + (ot + 1) * 128]
                      for kc in range(16)]
                rh = [ygc[:, kc * TC:(kc + 1) * TC] for kc in range(16)]
                tm = emit_matmul_accum(p, pb[:], lh, rh, deps=[t_yg, t_op] + pdep)
                tev = p.op("scalar", lambda e, ot=ot, pb=pb: e.copy(
                    out=s_sb[:, ot * TC:(ot + 1) * TC], in_=pb[:]),
                    deps=[tm] + prev["s"])
                if ot % 2 == 0:
                    prev["sA"] = [tev]
                else:
                    prev["sB"] = [tev]
                s_toks.append(tev)
            xT_toks = []
            for tt in range(4):
                t_x = p.dma("sync", x_tok[:],
                            x_in[c0 + tt * 128:c0 + (tt + 1) * 128, :],
                            deps=prev["xtok"])
                for c in range(8):
                    ttr = p.op("tensor", lambda e, c=c: e.transpose(
                        out=ps_mu[:, 0:128], in_=x_tok[:, c * 128:(c + 1) * 128],
                        identity=ident[:]),
                        deps=[t_x, t_id] + prev["mu"] + prev.get("muwar", []))
                    tev = p.op("scalar", lambda e, c=c, tt=tt: e.copy(
                        out=xT_sb[:, c * TC + tt * 128: c * TC + (tt + 1) * 128],
                        in_=ps_mu[:, 0:128]), deps=[ttr] + prev["xT"])
                    prev["mu"] = [tev]
                    xT_toks.append(tev)
                prev["xtok"] = [xT_toks[-1]]
            x1_toks = []
            for c in range(8):
                t1 = p.op("vector", lambda e, c=c: e.tensor_mul(
                    out=tmp_t[:], in0=s_sb[:, c * TC:(c + 1) * TC],
                    in1=g_sb[:, c * TC:(c + 1) * TC]),
                    deps=[s_toks[c], t_g] + prev["tmp"])
                t2 = p.op("vector", lambda e, c=c: e.tensor_add(
                    out=x1_sb[:, c * TC:(c + 1) * TC],
                    in0=xT_sb[:, c * TC:(c + 1) * TC], in1=tmp_t[:]),
                    deps=[t1, xT_toks[tt * 8 + c] if False else xT_toks[-1]] + prev["x1"])
                prev["tmp"] = [t2]
                x1_toks.append(t2)
            t_x1st = p.dma("sync", x1_out[:, c0:c0 + TC].rearrange(
                "(k p) t -> p k t", p=128),
                x1_sb[:].rearrange("p (k t) -> p k t", k=8), deps=x1_toks)
            sq_toks = []
            for c in range(8):
                sq = sq_t[c % 2]
                t_sq = p.op("scalar", lambda e, c=c, sq=sq: e.activation(
                    out=sq[:], in_=x1_sb[:, c * TC:(c + 1) * TC], func=AF.Square),
                    deps=[x1_toks[c]] + prev["sq"][c % 2])
                t_m1 = p.op("tensor", lambda e, c=c: e.matmul(
                    out=ps_st1[:], lhsT=ones_col[:],
                    rhs=x1_sb[:, c * TC:(c + 1) * TC],
                    start=(c == 0), stop=(c == 7)),
                    deps=[x1_toks[c], tok_oc] + (prev["st"] if c == 0 else []))
                t_m2 = p.op("tensor", lambda e, c=c, sq=sq: e.matmul(
                    out=ps_st2[:], lhsT=ones_col[:], rhs=sq[:],
                    start=(c == 0), stop=(c == 7)), deps=[t_sq])
                prev["sq"][c % 2] = [t_m2]
                sq_toks.append(t_m2)
            t_mu = p.op("vector", lambda e: e.tensor_scalar(
                out=strow[:, 2 * TC:3 * TC], in0=ps_st1[:], scalar1=consts[0:1, 0:1],
                scalar2=None, op0=ALU.mult), deps=sq_toks + [tok_c] + prev["st"])
            t_ms = p.op("vector", lambda e: e.tensor_mul(
                out=strow[:, 0:TC], in0=strow[:, 2 * TC:3 * TC],
                in1=strow[:, 2 * TC:3 * TC]), deps=[t_mu])
            t_var = p.op("vector", lambda e: e.tensor_scalar(
                out=strow[:, TC:2 * TC], in0=ps_st2[:], scalar1=consts[0:1, 0:1],
                scalar2=None, op0=ALU.mult), deps=sq_toks + [t_ms])
            t_var2 = p.op("vector", lambda e: e.tensor_sub(
                out=strow[:, TC:2 * TC], in0=strow[:, TC:2 * TC],
                in1=strow[:, 0:TC]), deps=[t_var])
            t_sd = p.op("scalar", lambda e: e.activation(
                out=strow[:, 0:TC], in_=strow[:, TC:2 * TC], func=AF.Sqrt,
                bias=consts[0:1, 1:2]), deps=[t_var2])
            t_rstd = p.op("vector", lambda e: e.reciprocal(
                out=strow[:, 3 * TC:4 * TC], in_=strow[:, 0:TC]), deps=[t_sd])
            prev["st"] = [t_rstd]
            t_rmu = p.op("tensor", lambda e: e.matmul(
                out=ps_mu[:], lhsT=ones1[:], rhs=strow[:, 2 * TC:3 * TC],
                start=True, stop=True), deps=[t_mu, prev["mu"][0]])
            t_rrs = p.op("tensor", lambda e: e.matmul(
                out=ps_rs[:], lhsT=ones1[:], rhs=strow[:, 3 * TC:4 * TC],
                start=True, stop=True), deps=[t_rstd] + prev["xn1"])
            prev["mu"] = []
            xn1_toks = []
            for c in range(8):
                ts = p.op("vector", lambda e, c=c: e.tensor_sub(
                    out=tmp_t[:], in0=x1_sb[:, c * TC:(c + 1) * TC], in1=ps_mu[:]),
                    deps=[t_rmu] + prev["tmp"] + (x1_toks if c == 0 else []))
                tn = p.op("vector", lambda e, c=c: e.tensor_mul(
                    out=xn1_sb[:, c * TC:(c + 1) * TC], in0=tmp_t[:], in1=ps_rs[:]),
                    deps=[ts, t_rrs] + prev["xn1"] + prev.get("xnwar", []))
                prev["tmp"] = [tn]
                xn1_toks.append(tn)
            prev["x1"] = xn1_toks + [t_x1st]
            t_xnst = p.dma("sync", xn1_out[:, c0:c0 + TC].rearrange(
                "(k p) t -> p k t", p=128),
                xn1_sb[:].rearrange("p (k t) -> p k t", k=8), deps=xn1_toks)
            prev["xn1"] = [t_xnst]
            K_toks, V_toks = [], []
            for ot in range(8):
                pb = ps_a if ot % 2 == 0 else ps_b
                pdep = prev["sA"] if ot % 2 == 0 else prev["sB"]
                lh = [Wk_bf[:, kc * D + ot * 128: kc * D + (ot + 1) * 128]
                      for kc in range(8)]
                rh = [xn1_sb[:, kc * TC:(kc + 1) * TC] for kc in range(8)]
                tm = emit_matmul_accum(p, pb[:], lh, rh,
                                       deps=xn1_toks + [t_wk] + pdep)
                tev = p.op("scalar", lambda e, ot=ot, pb=pb: e.activation(
                    out=K1_sb[:, ot * TC:(ot + 1) * TC], in_=pb[:], func=AF.Relu,
                    bias=bk_sb[:, ot:ot + 1]), deps=[tm, t_bk] + prev["K1"])
                if ot % 2 == 0:
                    prev["sA"] = [tev]
                else:
                    prev["sB"] = [tev]
                K_toks.append(tev)
            for ot in range(8):
                pb = ps_a if ot % 2 == 0 else ps_b
                pdep = prev["sA"] if ot % 2 == 0 else prev["sB"]
                lh = [Wv_bf[:, kc * D + ot * 128: kc * D + (ot + 1) * 128]
                      for kc in range(8)]
                rh = [xn1_sb[:, kc * TC:(kc + 1) * TC] for kc in range(8)]
                tm = emit_matmul_accum(p, pb[:], lh, rh,
                                       deps=xn1_toks + [t_wv] + pdep)
                tev = p.op("scalar", lambda e, ot=ot, pb=pb: e.activation(
                    out=V1_sb[:, ot * TC:(ot + 1) * TC], in_=pb[:], func=AF.Relu,
                    bias=bv_sb[:, ot:ot + 1]), deps=[tm, t_bv] + prev["V1"])
                if ot % 2 == 0:
                    prev["sA"] = [tev]
                else:
                    prev["sB"] = [tev]
                V_toks.append(tev)
            t_vst = p.dma("sync", V1_out[:, c0:c0 + TC].rearrange(
                "(k p) t -> p k t", p=128),
                V1_sb[:].rearrange("p (k t) -> p k t", k=8), deps=V_toks)
            prev["V1"] = [t_vst]
            dm = None
            for c in range(8):
                dm = p.op("tensor", lambda e, c=c: e.matmul(
                    out=ps_dot[:], lhsT=RH_bf[:, c * NST:(c + 1) * NST],
                    rhs=K1_sb[:, c * TC:(c + 1) * TC],
                    start=(c == 0), stop=(c == 7)),
                    deps=K_toks + [t_rh] + (prev["dot"] if c == 0 else []))
            t_val = p.op("vector", lambda e: e.tensor_scalar(
                out=val_sb[:], in0=ps_dot[:], scalar1=0.5, scalar2=None,
                op0=ALU.is_gt), deps=[dm] + prev["val"])
            prev["dot"] = [t_val]
            prev["muwar"] = xn1_toks
            prev["xnwar"] = [t_vst, dm, t_xnst]
            t_vlst = p.dma("sync", valid_out[:, c0:c0 + TC], val_sb[:],
                           deps=[t_val])
            prev["val"] = [t_vlst]
            prev["ygc"] = s_toks
            prev["g"] = x1_toks
            prev["s"] = x1_toks
            prev["xT"] = x1_toks
            prev["K1"] = [dm]
    return [t_vlst, t_vst, t_xnst, t_x1st] + prev["sA"] + prev["sB"]


def build_l4():
    nc = bass.Bass("TRN2", target_bir_lowering=False, debug=False)
    yg_in = nc.dram_tensor("ygT_slab", [DI, TI], BF16, kind="ExternalInput").ap()
    io = {
        "yg_ap": lambda tc: yg_in.rearrange("(k p) t -> p k t", p=128)[
            :, :, tc * 512:(tc + 1) * 512],
        "opT": nc.dram_tensor("opT", [DI, D], F32, kind="ExternalInput").ap(),
        "x_slab": nc.dram_tensor("x_slab", [TI, D], F32, kind="ExternalInput").ap(),
        "gT": nc.dram_tensor("gT", [D, TI], F32, kind="ExternalInput").ap(),
        "Wk": nc.dram_tensor("Wk", [D, D], F32, kind="ExternalInput").ap(),
        "bk": nc.dram_tensor("bk", [D, 1], F32, kind="ExternalInput").ap(),
        "Wv": nc.dram_tensor("Wv", [D, D], F32, kind="ExternalInput").ap(),
        "bv": nc.dram_tensor("bv", [D, 1], F32, kind="ExternalInput").ap(),
        "RHblk": nc.dram_tensor("RHblk", [D, NST], F32, kind="ExternalInput").ap(),
        "ident": nc.dram_tensor("ident", [128, 128], F32, kind="ExternalInput").ap(),
        "x1T": nc.dram_tensor("x1T", [D, TI], F32, kind="ExternalOutput").ap(),
        "xn1T": nc.dram_tensor("xn1T", [D, TI], BF16, kind="ExternalOutput").ap(),
        "V1T": nc.dram_tensor("V1T", [D, TI], F32, kind="ExternalOutput").ap(),
        "valid": nc.dram_tensor("valid", [NST, TI], F32, kind="ExternalOutput").ap(),
    }
    import contextlib
    with contextlib.ExitStack() as st:
        sems = std_sems(nc, st)
        block = st.enter_context(nc.Block())
        p = Prog(nc, sems)
        toks = emit_l4(nc, p, st, io)
        p.wait_only("gpsimd", toks)
        p.finalize(block)
    return nc


def host_prep_l4(inputs, ygT_full, gT_full, x_full):
    RH = inputs["L1_attn_ReadingHead"]
    RHblk = np.zeros((D, NST), np.float32)
    for h in range(NST):
        c = h // 2
        off = (h % 2) * 64
        RHblk[c * 128 + off: c * 128 + off + 64, h] = RH[h]
    opT = np.ascontiguousarray(inputs["L0_attn_out_proj_w"].T)
    ident = np.eye(128, dtype=np.float32)
    maps = []
    for c in range(NCORE):
        sl = slice(c * TI, (c + 1) * TI)
        maps.append({
            "ygT_slab": np.ascontiguousarray(ygT_full[:, sl]),
            "opT": opT,
            "x_slab": np.ascontiguousarray(x_full[sl]),
            "gT": np.ascontiguousarray(gT_full[:, sl]),
            "Wk": inputs["L1_attn_Wk"], "bk": inputs["L1_attn_bk"][:, None],
            "Wv": inputs["L1_attn_Wv"], "bv": inputs["L1_attn_bv"][:, None],
            "RHblk": RHblk, "ident": ident,
        })
    return maps


def run_l4(inputs, ygT_full, gT_full, x_full, **kw):
    nc = _get("l4", build_l4)
    maps = host_prep_l4(inputs, ygT_full, gT_full, x_full)
    return run_bass_kernel_spmd(nc, maps, core_ids=list(range(NCORE)), **kw)


def emit_l5(nc, p, st, io):
    v1_src = io["v1_src"]
    val_src = io["val_src"]
    bv0_src = io["bv0_src"]
    wfb_in = io["wfb"]
    repl2_in = io["repl2"]
    attn_out = io["attn"]
    TCH = min(4096, L // 4)
    NCH = L // TCH
    sb = lambda name, shape, dt: st.enter_context(nc.sbuf_tensor("p5" + name, shape, dt))
    ps = lambda name, shape, dt: st.enter_context(nc.psum_tensor("p5" + name, shape, dt))
    if True:
        repl2 = sb("repl2_sb", [2, 128], F32)
        val_sb = sb("val_sb2", [2, 4096], F32)
        wfb = sb("wfb_sb", [128, 8], F32)
        wbsum = sb("wbsum", [128, 1], F32)
        Vc = sb("Vc", [128, TCH], F32)
        v_s = sb("v_s", [128, TCH], F32)
        vb_s = sb("vb_s", [128, TCH], F32)
        ev = sb("ev", [128, TCH], F32)
        slot = [sb(f"slot{r}", [128, TCH], F32) for r in range(4)]
        acc = sb("acc", [128, TCH], F32)
        ftail = [sb(f"ftail{i}", [128, 4], F32) for i in range(2)]
        snap = sb("snap", [128, 8], F32)
        ps_v = ps("ps_v", [128, 512], F32)

        t_r2 = p.dma("sync", repl2[:], repl2_in)

        def replicate_v(val_rows, c0, deps, fix0, fixL):
            toks = []
            pv = None
            for sub in range(TCH // 512):
                pv_dep = [] if pv is None else [pv]
                tm = p.op("tensor", lambda e, sub=sub: e.matmul(
                    out=ps_v[:], lhsT=repl2[:],
                    rhs=val_rows[:, c0 + sub * 512:c0 + (sub + 1) * 512],
                    start=True, stop=True), deps=deps + [t_r2] + pv_dep)
                t1 = p.op("scalar", lambda e, sub=sub: e.copy(
                    out=v_s[:, sub * 512:(sub + 1) * 512], in_=ps_v[:]), deps=[tm])
                t2 = p.op("scalar", lambda e, sub=sub: e.activation(
                    out=vb_s[:, sub * 512:(sub + 1) * 512], in_=ps_v[:],
                    func=AF.Identity, bias=1.0, scale=-1.0), deps=[tm])
                pv = t2
                toks += [t1, t2]
            fixes = []
            if fixL:
                tsn = p.op("scalar", lambda e: e.copy(
                    out=snap[:, 2:3], in_=v_s[:, TCH - 1:TCH]), deps=toks)
                f1 = p.op("vector", lambda e: e.memset(v_s[:, TCH - 1:TCH], 0.0),
                          deps=[tsn])
                f2 = p.op("vector", lambda e: e.memset(vb_s[:, TCH - 1:TCH], 1.0),
                          deps=toks)
                fixes += [f1, f2, tsn]
            if fix0:
                f1 = p.op("vector", lambda e: e.memset(v_s[:, 0:1], 0.0), deps=toks)
                f2 = p.op("vector", lambda e: e.memset(vb_s[:, 0:1], 1.0), deps=toks)
                fixes += [f1, f2]
            return toks + fixes

        last_toks = []
        for tile in range(2):
            t_w = p.dma("sync", wfb[:], wfb_in[tile], deps=last_toks)
            row0 = tile * 128
            t_ws = p.op("vector", lambda e: e.tensor_reduce(
                out=wbsum[:], in_=wfb[:, 4:8], axis=AX.X, op=ALU.add),
                deps=[t_w])
            p.dma("sync", snap[0:64, 0:1], bv0_src(tile, 0),
                  deps=last_toks, allow_slow_non_contiguous=True)
            t_bv0 = p.dma("sync", snap[64:128, 0:1], bv0_src(tile, 1),
                          deps=last_toks, allow_slow_non_contiguous=True)
            ft_cur = 0
            t_ftin = p.op("vector", lambda e: e.memset(ftail[0][:], 0.0),
                          deps=last_toks)
            prev_ft = [t_ftin]
            prev_V = last_toks
            prev_sl = [[], [], [], []]
            prev_acc = []
            prev_ev = []
            st_toks = []
            for ch in range(NCH):
                c0 = ch * TCH
                p.dma("sync", Vc[0:64, :], v1_src(tile, 0, ch), deps=prev_V)
                t_V = p.dma("sync", Vc[64:128, :], v1_src(tile, 1, ch), deps=prev_V)
                p.dma("sync", val_sb[0:1, 0:TCH], val_src(tile, 0, ch),
                      deps=prev_ev + prev_sl[0])
                t_val = p.dma("sync", val_sb[1:2, 0:TCH], val_src(tile, 1, ch),
                              deps=prev_ev + prev_sl[0])
                vt = replicate_v(val_sb, 0, deps=[t_val] + prev_ev + prev_sl[0],
                                 fix0=(ch == 0), fixL=False)
                sl_toks = []
                for r in range(4):
                    if r == 0:
                        t_ev = p.op("vector", lambda e: e.tensor_mul(
                            out=ev[:], in0=v_s[:], in1=Vc[:]),
                            deps=vt + [t_V] + prev_ev)
                    else:
                        t_e1 = p.op("vector", lambda e, r=r: e.tensor_mul(
                            out=ev[:, 1:TCH], in0=v_s[:, 1:TCH],
                            in1=slot[r - 1][:, 0:TCH - 1]),
                            deps=vt + [sl_toks[r - 1]] + prev_ev)
                        t_ev = p.op("vector", lambda e, r=r, ft=ft_cur: e.tensor_mul(
                            out=ev[:, 0:1], in0=v_s[:, 0:1],
                            in1=ftail[ft][:, r - 1:r]), deps=[t_e1] + prev_ft)
                    init = (snap[:, 0:1] if ch == 0 else ftail[ft_cur][:, r:r + 1])
                    ideps = [t_bv0] if ch == 0 else prev_ft
                    t_sc = p.op("vector", lambda e, r=r, init=init: e.tensor_tensor_scan(
                        out=slot[r][:], data0=vb_s[:], data1=ev[:], initial=init,
                        op0=ALU.mult, op1=ALU.add),
                        deps=[t_ev] + ideps + prev_sl[r])
                    sl_toks.append(t_sc)
                    if r == 0:
                        t_ac = p.op("vector", lambda e: e.tensor_scalar(
                            out=acc[:], in0=slot[0][:], scalar1=wfb[:, 0:1],
                            scalar2=None, op0=ALU.mult),
                            deps=[t_sc, t_ws] + prev_acc)
                    else:
                        t_ac = p.op("vector", lambda e, r=r: e.scalar_tensor_tensor(
                            out=acc[:], in0=slot[r][:], scalar=wfb[:, r:r + 1],
                            in1=acc[:], op0=ALU.mult, op1=ALU.add),
                            deps=[t_sc, t_ac])
                ft_nxt = 1 - ft_cur
                nt = []
                for r in range(4):
                    nt.append(p.op("scalar", lambda e, r=r, fn=ft_nxt: e.copy(
                        out=ftail[fn][:, r:r + 1], in_=slot[r][:, TCH - 1:TCH]),
                        deps=[sl_toks[r]] + prev_ft))
                t_st = p.dma("sync", attn_out[row0:row0 + 128, c0:c0 + TCH],
                             acc[:], deps=[t_ac])
                st_toks.append(t_st)
                prev_ft = nt
                ft_cur = ft_nxt
                prev_V = sl_toks + [t_ev if False else sl_toks[0]]
                prev_V = sl_toks
                prev_sl = [[t_ac] for _ in range(4)]
                prev_acc = [t_st]
                prev_ev = sl_toks
            t_ftin = p.op("vector", lambda e: e.memset(ftail[ft_cur][:], 0.0),
                          deps=prev_ft + [t_ac])
            prev_ft = [t_ftin]
            prev_bh = []
            for ch in range(NCH - 1, -1, -1):
                c0 = ch * TCH
                p.dma("sync", Vc[0:64, :], v1_src(tile, 0, ch), deps=prev_V)
                t_V = p.dma("sync", Vc[64:128, :], v1_src(tile, 1, ch), deps=prev_V)
                p.dma("sync", val_sb[0:1, 0:TCH], val_src(tile, 0, ch),
                      deps=prev_ev + prev_sl[0])
                t_val = p.dma("sync", val_sb[1:2, 0:TCH], val_src(tile, 1, ch),
                              deps=prev_ev + prev_sl[0])
                vt = replicate_v(val_sb, 0, deps=[t_val] + prev_ev + prev_sl[0],
                                 fix0=(ch == 0), fixL=(ch == NCH - 1))
                t_la = p.dma("sync", acc[:], attn_out[row0:row0 + 128, c0:c0 + TCH],
                             deps=prev_acc + st_toks)
                acc_extra = []
                if ch == 0:
                    t_fa = p.op("scalar", lambda e: e.copy(
                        out=snap[:, 3:4], in_=acc[:, 0:1]), deps=[t_la])
                    acc_extra = [t_fa]
                if ch == NCH - 1:
                    t_b1 = p.op("vector", lambda e: e.tensor_mul(
                        out=snap[:, 4:5], in0=snap[:, 2:3], in1=Vc[:, TCH - 1:TCH]),
                        deps=vt + [t_V])
                    t_b2 = p.op("vector", lambda e: e.tensor_scalar(
                        out=snap[:, 5:6], in0=snap[:, 2:3], scalar1=-1.0,
                        scalar2=None, op0=ALU.mult), deps=vt)
                    t_b3 = p.op("vector", lambda e: e.tensor_scalar(
                        out=snap[:, 5:6], in0=snap[:, 5:6], scalar1=1.0,
                        scalar2=None, op0=ALU.add), deps=[t_b2])
                    t_b4 = p.op("vector", lambda e: e.tensor_mul(
                        out=snap[:, 5:6], in0=snap[:, 5:6], in1=snap[:, 0:1]),
                        deps=[t_b3, t_bv0])
                    t_binit = p.op("vector", lambda e: e.tensor_add(
                        out=snap[:, 4:5], in0=snap[:, 4:5], in1=snap[:, 5:6]),
                        deps=[t_b1, t_b4])
                sl_toks = []
                for r in range(4):
                    if r == 0:
                        t_e1 = p.op("vector", lambda e: e.tensor_mul(
                            out=ev[:, 0:TCH - 1], in0=v_s[:, 0:TCH - 1],
                            in1=Vc[:, 1:TCH]), deps=vt + [t_V] + prev_ev)
                        if ch == NCH - 1:
                            t_ev = p.op("vector", lambda e: e.memset(
                                ev[:, TCH - 1:TCH], 0.0), deps=[t_e1])
                        else:
                            t_ev = p.op("vector", lambda e: e.tensor_mul(
                                out=ev[:, TCH - 1:TCH], in0=v_s[:, TCH - 1:TCH],
                                in1=snap[:, 1:2]), deps=[t_e1] + prev_bh)
                        init = snap[:, 4:5] if ch == NCH - 1 else ftail[ft_cur][:, 0:1]
                        ideps = ([t_binit] if ch == NCH - 1 else prev_ft)
                        t_sc = p.op("vector", lambda e, init=init: e.tensor_tensor_scan(
                            out=slot[0][:], data0=vb_s[:, ::-1], data1=ev[:, ::-1],
                            initial=init, op0=ALU.mult, op1=ALU.add),
                            deps=[t_ev] + ideps + prev_sl[0])
                    else:
                        t_e1 = p.op("vector", lambda e, r=r: e.tensor_mul(
                            out=ev[:, 1:TCH], in0=v_s[:, TCH - 2::-1],
                            in1=slot[r - 1][:, 0:TCH - 1]),
                            deps=vt + [sl_toks[r - 1]] + prev_ev)
                        t_ev = p.op("vector", lambda e, r=r, ft=ft_cur: e.tensor_mul(
                            out=ev[:, 0:1], in0=v_s[:, TCH - 1:TCH],
                            in1=ftail[ft][:, r - 1:r]), deps=[t_e1] + prev_ft)
                        init = (snap[:, 0:1] if ch == NCH - 1
                                else ftail[ft_cur][:, r:r + 1])
                        ideps = [t_bv0] if ch == NCH - 1 else prev_ft
                        t_sc = p.op("vector", lambda e, r=r, init=init: e.tensor_tensor_scan(
                            out=slot[r][:], data0=vb_s[:, ::-1], data1=ev[:],
                            initial=init, op0=ALU.mult, op1=ALU.add),
                            deps=[t_ev] + ideps + prev_sl[r])
                    sl_toks.append(t_sc)
                    t_ac = p.op("vector", lambda e, r=r: e.scalar_tensor_tensor(
                        out=acc[:], in0=slot[r][:, ::-1], scalar=wfb[:, 4 + r:5 + r],
                        in1=acc[:], op0=ALU.mult, op1=ALU.add),
                        deps=[t_sc, t_la] + acc_extra
                        + ([] if r == 0 else [t_ac]) + prev_acc)
                if ch == 0:
                    t_ac = p.op("vector", lambda e: e.scalar_tensor_tensor(
                        out=acc[:, 0:1], in0=snap[:, 0:1], scalar=wbsum[:, 0:1],
                        in1=snap[:, 3:4], op0=ALU.mult, op1=ALU.add),
                        deps=[t_ac, t_fa, t_ws, t_bv0])
                ft_nxt = 1 - ft_cur
                nt = []
                for r in range(4):
                    nt.append(p.op("scalar", lambda e, r=r, fn=ft_nxt: e.copy(
                        out=ftail[fn][:, r:r + 1], in_=slot[r][:, TCH - 1:TCH]),
                        deps=[sl_toks[r]] + prev_ft))
                t_bh = p.op("scalar", lambda e: e.copy(
                    out=snap[:, 1:2], in_=Vc[:, 0:1]), deps=[t_V] + prev_bh)
                prev_bh = [t_bh]
                t_st = p.dma("sync", attn_out[row0:row0 + 128, c0:c0 + TCH],
                             acc[:], deps=[t_ac])
                prev_ft = nt
                ft_cur = ft_nxt
                prev_V = sl_toks + [t_bh]
                prev_sl = [[t_ac] for _ in range(4)]
                prev_acc = [t_st]
                prev_ev = sl_toks
            last_toks = [t_st] + nt + prev_bh
    return last_toks


def build_l5():
    nc = bass.Bass("TRN2", target_bir_lowering=False, debug=False)
    V1p_in = nc.dram_tensor("V1p", [256, L], F32, kind="ExternalInput").ap()
    val_in = [nc.dram_tensor(f"val{i}", [2, L], F32, kind="ExternalInput").ap()
              for i in range(2)]
    TCH = min(4096, L // 4)
    io = {
        "v1_src": lambda tile, half, ch: V1p_in[
            tile * 128 + half * 64: tile * 128 + half * 64 + 64,
            ch * TCH:(ch + 1) * TCH],
        "val_src": lambda tile, j, ch: val_in[tile][j:j + 1,
                                               ch * TCH:(ch + 1) * TCH],
        "bv0_src": lambda tile, half: V1p_in[
            tile * 128 + half * 64: tile * 128 + half * 64 + 64, 0:1],
        "wfb": [nc.dram_tensor(f"wfb{i}", [128, 8], F32, kind="ExternalInput").ap()
                for i in range(2)],
        "repl2": nc.dram_tensor("repl2", [2, 128], F32, kind="ExternalInput").ap(),
        "attn": nc.dram_tensor("attn", [256, L], F32, kind="ExternalOutput").ap(),
    }
    import contextlib
    with contextlib.ExitStack() as st:
        sems = std_sems(nc, st)
        block = st.enter_context(nc.Block())
        p = Prog(nc, sems)
        toks = emit_l5(nc, p, st, io)
        p.wait_only("gpsimd", toks)
        p.finalize(block)
    return nc


def host_prep_l5(inputs, V1_full, valid_full):
    bw = inputs["L1_attn_bidirection_weight"]
    repl2 = np.zeros((2, 128), np.float32)
    repl2[0, 0:64] = 1.0
    repl2[1, 64:128] = 1.0
    V1r = V1_full.reshape(16, 64, B, L)
    valr = valid_full.reshape(16, B, L)
    maps = []
    for c in range(NCORE):
        pairs = [(pid // 16, pid % 16) for pid in range(c * 4, c * 4 + 4)]
        V1p = np.ascontiguousarray(
            np.concatenate([V1r[h, :, b, :] for (b, h) in pairs], axis=0))
        m = {"V1p": V1p, "repl2": repl2}
        for i in range(2):
            pr = pairs[2 * i:2 * i + 2]
            m[f"val{i}"] = np.ascontiguousarray(
                np.stack([valr[h, b, :] for (b, h) in pr], axis=0))
            wfb = np.zeros((128, 8), np.float32)
            for j, (b, h) in enumerate(pr):
                wfb[j * 64:(j + 1) * 64, :] = bw[h][None, :]
            m[f"wfb{i}"] = wfb
        maps.append(m)
    return maps


def run_l5(inputs, V1_full, valid_full, **kw):
    nc = _get("l5", build_l5)
    maps = host_prep_l5(inputs, V1_full, valid_full)
    return run_bass_kernel_spmd(nc, maps, core_ids=list(range(NCORE)), **kw)


def emit_l6(nc, p, st, io):
    xn1_in, x1_in = io["xn1T"], io["x1T"]
    s1_src = io["s1_src"]
    wrT_in, wmhT_in, wmgT_in = io["wrT"], io["wmhT"], io["wmgT"]
    kT_in, vf_in = io["kT"], io["vf"]
    ident_in = io["ident"]
    out_o = io["out_slab"]
    w_dram = io["w_scr"]
    TCK = min(2048, TI)
    NSUB = TCK // 512
    sb = lambda name, shape, dt: st.enter_context(nc.sbuf_tensor("p6" + name, shape, dt))
    ps = lambda name, shape, dt: st.enter_context(nc.psum_tensor("p6" + name, shape, dt))
    if True:
        ident = sb("ident_sb", [128, 128], F32)
        ones16 = sb("ones16", [128, NST], F32)
        wrT_bf = sb("wrT_bf", [128, 8 * NST], BF16)
        wmh_bf = sb("wmh_bf", [128, 8 * D], BF16)
        wmg_bf = sb("wmg_bf", [128, 8 * D], BF16)
        xn1_ch = sb("xn1_ch", [128, 8 * TCK], BF16)
        w_sb = sb("w_sb", [NST, TCK], F32)
        lt = sb("lt", [128, NST], F32)
        mx = sb("mx", [128, 8], F32)
        msk = sb("msk", [128, 6 * NST], F32)
        sgc = sb("sgc", [128, 4], F32)
        wtok = sb("wtok", [128, NST], F32)
        k_bf = [sb(f"k_bf{i}", [128, MOE_E * MOE_S], BF16) for i in range(2)]
        v_bf = [sb(f"v_bf{i}", [128, MOE_E * 2 * MOE_D], BF16) for i in range(2)]
        q_bf = sb("q_bf", [128, 512], BF16)
        wrepl = sb("wrepl", [128, NST * 512], F32)
        ymoe = sb("ymoe", [128, 8 * 512], BF16)
        gel = [sb(f"gel{i}", [128, 512], F32) for i in range(3)]
        wm_bf = [sb(f"wm_bf{i}", [128, 512], BF16) for i in range(3)]
        x1_sb = sb("x1_sb", [128, 8 * 512], F32)
        s1_sb = sb("s1_sb", [128, 8 * 512], F32)
        g1_sb = sb("g1_sb", [128, 512], F32)
        o_sb = sb("o_sb", [128, 512], F32)
        otile = [sb(f"otile{i}", [128, 128], F32) for i in range(2)]
        ps_lg = ps("ps_lg", [128, NST], F32)
        ps_tr = ps("ps_tr", [NST, 128], F32)
        ps_q = ps("ps_q", [128, 512], F32)
        ps_sc = [ps(f"ps_sc{i}", [128, 512], F32) for i in range(3)]
        ps_y = ps("ps_y", [128, 512], F32)
        ps_tr2 = ps("ps_tr2", [128, 128], F32)

        t_id = p.dma("sync", ident[:], ident_in)
        tok_o = p.op("vector", lambda e: e.memset(ones16[:], 1.0))
        t_wr = p.dma("gpsimd", wrT_bf[:].rearrange("p (k o) -> p k o", k=8),
                     wrT_in.rearrange("(k p) o -> p k o", p=128))
        t_mh = p.dma("gpsimd", wmh_bf[:].rearrange("p (k o) -> p k o", k=8),
                     wmhT_in.rearrange("(k p) o -> p k o", p=128))
        t_mg = p.dma("gpsimd", wmg_bf[:].rearrange("p (k o) -> p k o", k=8),
                     wmgT_in.rearrange("(k p) o -> p k o", p=128))

        prev = {"xn1": [], "lt": [], "wsb": [], "tr": [], "kv": [[], []],
                "q": [], "wrepl": [], "ymoe": [], "sc": [], "gel": [[], [], []],
                "wm": [[], [], []], "y2": [[], [], []], "y": [], "x1": [], "s1": [], "g1": [], "o": [], "ot": [],
                "lg": [], "st": [None, None], "msk": [], "mx": [], "wtok": [],
                "sg": []}
        kv_i = 0
        for ck in range(TI // TCK):
            c0 = ck * TCK
            t_xc = p.dma("sync", xn1_ch[:].rearrange("p (k t) -> p k t", k=8),
                         xn1_in.rearrange("(k p) t -> p k t", p=128)[
                             :, :, c0:c0 + TCK], deps=prev["xn1"])
            wsb_toks = []
            for tt in range(TCK // 128):
                dm = None
                for kc in range(8):
                    dm = p.op("tensor", lambda e, kc=kc, tt=tt: e.matmul(
                        out=ps_lg[:], lhsT=xn1_ch[:, kc * TCK + tt * 128:
                                                  kc * TCK + (tt + 1) * 128],
                        rhs=wrT_bf[:, kc * NST:(kc + 1) * NST],
                        start=(kc == 0), stop=(kc == 7)),
                        deps=[t_xc, t_wr] + (prev["lg"] if kc == 0 else []))
                t_lt = p.op("scalar", lambda e: e.copy(out=lt[:], in_=ps_lg[:]),
                            deps=[dm] + prev["lt"])
                prev["lg"] = [t_lt]
                t_mx = p.op("vector", lambda e: e.max(out=mx[:], in_=lt[:]),
                            deps=[t_lt] + prev["mx"])
                t_mA = p.op("vector", lambda e: e.tensor_scalar(
                    out=msk[:, 0:NST], in0=lt[:], scalar1=mx[:, 0:1], scalar2=None,
                    op0=ALU.is_equal), deps=[t_mx] + prev["msk"])
                t_cA = p.op("vector", lambda e: e.tensor_tensor_scan(
                    out=msk[:, NST:2 * NST], data0=ones16[:], data1=msk[:, 0:NST],
                    initial=0.0, op0=ALU.mult, op1=ALU.add), deps=[t_mA, tok_o])
                t_e1 = p.op("vector", lambda e: e.tensor_scalar(
                    out=msk[:, NST:2 * NST], in0=msk[:, NST:2 * NST], scalar1=1.0,
                    scalar2=None, op0=ALU.is_equal), deps=[t_cA])
                t_m1 = p.op("vector", lambda e: e.tensor_mul(
                    out=msk[:, 0:NST], in0=msk[:, 0:NST], in1=msk[:, NST:2 * NST]),
                    deps=[t_e1])
                t_mB = p.op("vector", lambda e: e.tensor_scalar(
                    out=msk[:, 2 * NST:3 * NST], in0=lt[:], scalar1=mx[:, 1:2],
                    scalar2=None, op0=ALU.is_equal), deps=[t_mx, t_lt])
                t_cB = p.op("vector", lambda e: e.tensor_tensor_scan(
                    out=msk[:, 3 * NST:4 * NST], data0=ones16[:],
                    data1=msk[:, 2 * NST:3 * NST],
                    initial=0.0, op0=ALU.mult, op1=ALU.add), deps=[t_mB])
                t_eq = p.op("vector", lambda e: e.tensor_tensor(
                    out=sgc[:, 2:3], in0=mx[:, 0:1], in1=mx[:, 1:2],
                    op=ALU.is_equal), deps=[t_mx] + prev["sg"])
                t_tg = p.op("vector", lambda e: e.tensor_scalar(
                    out=sgc[:, 2:3], in0=sgc[:, 2:3], scalar1=1.0, scalar2=None,
                    op0=ALU.add), deps=[t_eq])
                t_e2 = p.op("vector", lambda e: e.tensor_scalar(
                    out=msk[:, 3 * NST:4 * NST], in0=msk[:, 3 * NST:4 * NST],
                    scalar1=sgc[:, 2:3], scalar2=None, op0=ALU.is_equal),
                    deps=[t_cB, t_tg])
                t_m2 = p.op("vector", lambda e: e.tensor_mul(
                    out=msk[:, 2 * NST:3 * NST], in0=msk[:, 2 * NST:3 * NST],
                    in1=msk[:, 3 * NST:4 * NST]), deps=[t_e2])
                t_sg = p.op("scalar", lambda e: e.activation(
                    out=sgc[:, 0:2], in_=mx[:, 0:2], func=AF.Sigmoid),
                    deps=[t_mx] + prev["sg"])
                t_w1 = p.op("vector", lambda e: e.tensor_scalar(
                    out=wtok[:], in0=msk[:, 0:NST], scalar1=sgc[:, 0:1],
                    scalar2=None, op0=ALU.mult), deps=[t_m1, t_sg] + prev["wtok"])
                t_wt = p.op("vector", lambda e: e.scalar_tensor_tensor(
                    out=wtok[:], in0=msk[:, 2 * NST:3 * NST], scalar=sgc[:, 1:2],
                    in1=wtok[:], op0=ALU.mult, op1=ALU.add), deps=[t_m2, t_w1])
                t_tr = p.op("tensor", lambda e: e.transpose(
                    out=ps_tr[:], in_=wtok[:], identity=ident[:]),
                    deps=[t_wt, t_id] + prev["tr"])
                t_we = p.op("scalar", lambda e, tt=tt: e.copy(
                    out=w_sb[:, tt * 128:(tt + 1) * 128], in_=ps_tr[:]),
                    deps=[t_tr] + prev["wsb"])
                prev["tr"] = [t_we]
                prev["msk"] = [t_wt]
                prev["mx"] = [t_w1]
                prev["sg"] = [t_wt]
                prev["wtok"] = [t_we]
                prev["lt"] = [t_mB]
                wsb_toks.append(t_we)
            t_wd = p.dma("sync", w_dram[:, c0:c0 + TCK], w_sb[:], deps=wsb_toks)
            st_toks = []
            for sub in range(NSUB):
                s0 = sub * 512
                t_x1 = p.dma("sync", x1_sb[:].rearrange("p (k t) -> p k t", k=8),
                             x1_in.rearrange("(k p) t -> p k t", p=128)[
                                 :, :, c0 + s0:c0 + s0 + 512], deps=prev["x1"])
                for kk in range(7):
                    p.dma("gpsimd", s1_sb[:, kk * 512:(kk + 1) * 512],
                          s1_src(ck, sub, kk), deps=prev["s1"])
                t_s1 = p.dma("gpsimd", s1_sb[:, 7 * 512:8 * 512],
                             s1_src(ck, sub, 7), deps=prev["s1"])
                wr_toks = []
                for en in range(NST):
                    t_ew = p.dma("sync", wrepl[:, en * 512:(en + 1) * 512],
                                 w_dram[en:en + 1, c0 + s0:c0 + s0 + 512
                                        ].broadcast_to([128, 512]),
                                 deps=[t_wd] + prev["wrepl"])
                    prev["wrepl"] = [t_ew]
                    wr_toks.append(t_ew)
                y_toks = []
                for h in range(MOE_H):
                    kv = kv_i % 2
                    kv_i += 1
                    t_k = p.dma("gpsimd",
                                k_bf[kv][:].rearrange("p (e s) -> p e s", e=MOE_E),
                                kT_in[h].transpose([1, 0, 2]),
                                deps=prev["kv"][kv])
                    t_v = p.dma("gpsimd",
                                v_bf[kv][:].rearrange("p (e c d) -> p e c d",
                                                      e=MOE_E, c=2),
                                vf_in[h].rearrange("e (c sp) d -> sp e c d", sp=128),
                                deps=prev["kv"][kv])
                    qm = None
                    for kc in range(8):
                        qm = p.op("tensor", lambda e, kc=kc, h=h, s0=s0: e.matmul(
                            out=ps_q[:], lhsT=wmh_bf[:, kc * D + h * 128:
                                                     kc * D + (h + 1) * 128],
                            rhs=xn1_ch[:, kc * TCK + s0:kc * TCK + s0 + 512],
                            start=(kc == 0), stop=(kc == 7)),
                            deps=[t_xc, t_mh] + (prev["q"] if kc == 0 else []))
                    t_q = p.op("scalar", lambda e: e.copy(out=q_bf[:], in_=ps_q[:]),
                               deps=[qm] + prev["sc"])
                    prev["q"] = [t_q]
                    def emit_score(ny):
                        en, sc = ny // 2, ny % 2
                        pp = ny % 3
                        return p.op("tensor", lambda e, en=en, sc=sc, kv=kv, pp=pp: e.matmul(
                            out=ps_sc[pp][:],
                            lhsT=k_bf[kv][:, en * MOE_S + sc * 128:
                                          en * MOE_S + (sc + 1) * 128],
                            rhs=q_bf[:], start=True, stop=True),
                            deps=[t_q, t_k] + prev["gel"][pp])
                    t_s_next = emit_score(0)
                    for ny in range(32):
                        en, sc, pp = ny // 2, ny % 2, ny % 3
                        t_s = t_s_next
                        t_g = p.op("scalar", lambda e, pp=pp: e.activation(
                            out=gel[pp][:], in_=ps_sc[pp][:], func=AF.Gelu),
                            deps=[t_s] + prev["wm"][pp])
                        prev["gel"][pp] = [t_g]
                        t_wm = p.op("vector", lambda e, en=en, pp=pp: e.tensor_mul(
                            out=wm_bf[pp][:], in0=gel[pp][:],
                            in1=wrepl[:, en * 512:(en + 1) * 512]),
                            deps=[t_g, wr_toks[en]] + prev["y2"][pp])
                        prev["wm"][pp] = [t_wm]
                        if ny < 31:
                            t_s_next = emit_score(ny + 1)
                        t_y = p.op("tensor", lambda e, en=en, sc=sc, kv=kv, pp=pp,
                                   first=(ny == 0), last=(ny == 31): e.matmul(
                            out=ps_y[:],
                            lhsT=v_bf[kv][:, (en * 2 + sc) * MOE_D:
                                          (en * 2 + sc + 1) * MOE_D],
                            rhs=wm_bf[pp][:], start=first, stop=last),
                            deps=[t_wm, t_v] + (prev["ymoe"] if ny == 0 else []))
                        prev["y2"][pp] = [t_y]
                        prev["y"] = [t_y]
                    t_ye = p.op("scalar", lambda e, h=h: e.copy(
                        out=ymoe[:, h * 512:(h + 1) * 512], in_=ps_y[:]),
                        deps=[t_y] + prev["ymoe"])
                    prev["ymoe"] = [t_ye]
                    prev["kv"][kv] = [t_y, t_s]
                    prev["sc"] = [t_y]
                    y_toks.append(t_ye)
                tr_toks = []
                for oc in range(8):
                    gm = None
                    for hc in range(8):
                        gm = p.op("tensor", lambda e, hc=hc, oc=oc: e.matmul(
                            out=ps_q[:], lhsT=wmg_bf[:, hc * D + oc * 128:
                                                     hc * D + (oc + 1) * 128],
                            rhs=ymoe[:, hc * 512:(hc + 1) * 512],
                            start=(hc == 0), stop=(hc == 7)),
                            deps=y_toks + [t_mg] + prev["q"])
                    t_g1 = p.op("scalar", lambda e: e.activation(
                        out=g1_sb[:], in_=ps_q[:], func=AF.Silu),
                        deps=[gm] + prev["g1"])
                    prev["q"] = [t_g1]
                    t_o1 = p.op("vector", lambda e, oc=oc: e.tensor_mul(
                        out=o_sb[:], in0=s1_sb[:, oc * 512:(oc + 1) * 512],
                        in1=g1_sb[:]), deps=[t_g1, t_s1] + prev["o"])
                    t_o2 = p.op("vector", lambda e, oc=oc: e.tensor_add(
                        out=o_sb[:], in0=o_sb[:],
                        in1=x1_sb[:, oc * 512:(oc + 1) * 512]), deps=[t_o1, t_x1])
                    prev["g1"] = [t_o1]
                    for tt in range(4):
                        ob = otile[tt % 2]
                        t_t = p.op("tensor", lambda e, tt=tt: e.transpose(
                            out=ps_tr2[:], in_=o_sb[:, tt * 128:(tt + 1) * 128],
                            identity=ident[:]),
                            deps=[t_o2, t_id] + prev["ot"])
                        t_te = p.op("scalar", lambda e, ob=ob: e.copy(
                            out=ob[:], in_=ps_tr2[:]),
                            deps=[t_t] + prev["st"][tt % 2:tt % 2 + 1])
                        prev["ot"] = [t_te]
                        r0 = c0 + s0 + tt * 128
                        t_sto = p.dma("sync",
                                      out_o[r0:r0 + 128, oc * 128:(oc + 1) * 128],
                                      ob[:], deps=[t_te])
                        prev["st"][tt % 2] = t_sto
                        tr_toks.append(t_sto)
                    prev["o"] = [tr_toks[-1]]
                sts = tr_toks[-8:]
                st_toks += sts
                prev["x1"] = [t_o2]
                prev["s1"] = [t_o1]
                prev["ymoe"] = prev["q"]
                prev["wrepl"] = prev["wm"][0] + prev["wm"][1] + prev["wm"][2]
            prev["xn1"] = [qm] + wsb_toks
            prev["wsb"] = prev["wrepl"]
    return st_toks


def build_l6():
    nc = bass.Bass("TRN2", target_bir_lowering=False, debug=False)
    s1_in = nc.dram_tensor("s1T", [D, TI], F32, kind="ExternalInput").ap()
    io = {
        "xn1T": nc.dram_tensor("xn1T", [D, TI], BF16, kind="ExternalInput").ap(),
        "x1T": nc.dram_tensor("x1T", [D, TI], F32, kind="ExternalInput").ap(),
        "s1_src": lambda ck, sub, k: s1_in[
            k * 128:(k + 1) * 128,
            ck * min(2048, TI) + sub * 512: ck * min(2048, TI) + sub * 512 + 512],
        "wrT": nc.dram_tensor("wrT", [D, NST], F32, kind="ExternalInput").ap(),
        "wmhT": nc.dram_tensor("wmhT", [D, D], F32, kind="ExternalInput").ap(),
        "wmgT": nc.dram_tensor("wmgT", [D, D], F32, kind="ExternalInput").ap(),
        "kT": nc.dram_tensor("kT", [MOE_H, MOE_E, MOE_D, MOE_S], F32,
                             kind="ExternalInput").ap(),
        "vf": nc.dram_tensor("vf", [MOE_H, MOE_E, MOE_S, MOE_D], F32,
                             kind="ExternalInput").ap(),
        "ident": nc.dram_tensor("ident", [128, 128], F32, kind="ExternalInput").ap(),
        "out_slab": nc.dram_tensor("out_slab", [TI, D], F32, kind="ExternalOutput").ap(),
        "w_scr": nc.dram_tensor("w_scr", [NST, TI], F32, kind="Internal").ap(),
    }
    import contextlib
    with contextlib.ExitStack() as st:
        sems = std_sems(nc, st)
        block = st.enter_context(nc.Block())
        p = Prog(nc, sems)
        toks = emit_l6(nc, p, st, io)
        p.wait_only("gpsimd", toks)
        p.finalize(block)
    return nc


def host_prep_l6(inputs, xn1_full, x1_full, s1_full):
    kT = np.ascontiguousarray(np.transpose(inputs["L1_ffn_k_ffwd"], (0, 1, 3, 2)))
    ident = np.eye(128, dtype=np.float32)
    maps = []
    for c in range(NCORE):
        sl = slice(c * TI, (c + 1) * TI)
        maps.append({
            "xn1T": np.ascontiguousarray(xn1_full[:, sl]),
            "x1T": np.ascontiguousarray(x1_full[:, sl]),
            "s1T": np.ascontiguousarray(s1_full[:, sl]),
            "wrT": np.ascontiguousarray(inputs["L1_ffn_w_router"].T),
            "wmhT": np.ascontiguousarray(inputs["L1_ffn_w_mh"].T),
            "wmgT": np.ascontiguousarray(inputs["L1_ffn_w_mg"].T),
            "kT": kT, "vf": inputs["L1_ffn_v_ffwd"],
            "ident": ident,
        })
    return maps


def run_l6(inputs, xn1_full, x1_full, s1_full, **kw):
    nc = _get("l6", build_l6)
    maps = host_prep_l6(inputs, xn1_full, x1_full, s1_full)
    return run_bass_kernel_spmd(nc, maps, core_ids=list(range(NCORE)), **kw)


_built = {}


def _get(name, builder):
    if name not in _built:
        _built[name] = builder()
    return _built[name]


def host_prep_l1(inputs):
    x = np.ascontiguousarray(inputs["L0_attn_x"].reshape(T, D))
    wgT = np.ascontiguousarray(inputs["L0_ffn_w_gate"].T)
    wuT = np.ascontiguousarray(inputs["L0_ffn_w_up"].T)
    wdT = np.ascontiguousarray(inputs["L0_ffn_w_down"].T)
    ident = np.eye(128, dtype=np.float32)
    maps = []
    for c in range(NCORE):
        maps.append({
            "x_slab": np.ascontiguousarray(x[c * TI:(c + 1) * TI]),
            "wgT": wgT, "wuT": wuT, "wdT": wdT, "ident": ident,
        })
    return maps


def run_l1(inputs, **kw):
    nc = _get("l1", build_l1)
    maps = host_prep_l1(inputs)
    return run_bass_kernel_spmd(nc, maps, core_ids=list(range(NCORE)), **kw)


def kernel(**inputs):
    inputs = {k: np.asarray(v) for k, v in inputs.items()}
    r1 = run_l1(inputs)
    xnT_full = np.ascontiguousarray(
        np.concatenate([np.asarray(r["xnT"]) for r in r1.results], axis=1))
    gT_full = np.concatenate([r["gT"] for r in r1.results], axis=1)
    r2 = run_l2(inputs, xnT_full)
    pd_all = np.stack([r["pd"] for r in r2.results], axis=0)
    pB_all = np.stack([r["pB"] for r in r2.results], axis=0)
    pC_all = np.stack([r["pC"] for r in r2.results], axis=0)
    u_all = np.concatenate([r["u_sl"] for r in r2.results], axis=0)
    resS_all = np.concatenate([r["resS"] for r in r2.results], axis=0)
    r3 = run_l3(inputs, pd_all, pB_all, pC_all, u_all, resS_all)
    ygT_full = np.ascontiguousarray(
        np.concatenate([np.asarray(r["ygT"]) for r in r3.results], axis=0))
    x_full = np.ascontiguousarray(inputs["L0_attn_x"].reshape(T, D))
    r4 = run_l4(inputs, ygT_full, gT_full, x_full)
    x1_full = np.concatenate([r["x1T"] for r in r4.results], axis=1)
    xn1_full = np.ascontiguousarray(
        np.concatenate([np.asarray(r["xn1T"]) for r in r4.results], axis=1))
    V1_full = np.ascontiguousarray(
        np.concatenate([r["V1T"] for r in r4.results], axis=1))
    valid_full = np.ascontiguousarray(
        np.concatenate([r["valid"] for r in r4.results], axis=1))
    r5 = run_l5(inputs, V1_full, valid_full)
    s1_full = np.zeros((B, L, 16, 64), np.float32)
    for c in range(NCORE):
        a = r5.results[c]["attn"]
        for j in range(4):
            pid = c * 4 + j
            s1_full[pid // 16, :, pid % 16, :] = a[j * 64:(j + 1) * 64, :].T
    s1T = np.ascontiguousarray(
        s1_full.reshape(T, D).T)
    r6 = run_l6(inputs, xn1_full, np.ascontiguousarray(x1_full), s1T)
    out = np.concatenate([r["out_slab"] for r in r6.results], axis=0)
    return np.ascontiguousarray(out.reshape(B, L, D))
